# version 14
# baseline (speedup 1.0000x reference)
import numpy as np
import concourse.bass as bass
import concourse.mybir as mybir
from concourse.bass import AP
from concourse.bass_utils import run_bass_kernel_spmd

F32 = mybir.dt.float32
BF16 = mybir.dt.bfloat16
I8 = mybir.dt.int8
I32 = mybir.dt.int32
AF = mybir.ActivationFunctionType
ALU = mybir.AluOpType
AXX = mybir.AxisListType.X

DM = 1024
DH = 1024
NC_DM = DM // 128
NC_DH = DH // 128
NST = 16
S = 8192
TT = 256
NT = S // TT
EPS = 1e-5

DBG = False


class _DummyInstr:
    def then_inc(self, *a, **k):
        return self


class _DummyEngine:
    def __getattr__(self, name):
        def f(*a, **k):
            return _DummyInstr()
        return f


def build_program(ev_table=None):
    pass2 = ev_table is not None
    nc = bass.Bass("TRN2", target_bir_lowering=False)

    xin = nc.dram_tensor("xin", [DM, S], F32, kind="ExternalInput").ap()
    w_in_t = nc.dram_tensor("w_in_t", [DM, 2 * DH], F32, kind="ExternalInput").ap()
    wxp_t = nc.dram_tensor("wxp_t", [DH, 160], F32, kind="ExternalInput").ap()
    wdt_t = nc.dram_tensor("wdt_t", [128, DH], F32, kind="ExternalInput").ap()
    wout_t = nc.dram_tensor("wout_t", [DH, DM], F32, kind="ExternalInput").ap()
    wmod = nc.dram_tensor("wmod", [DM, 6 * DM], F32, kind="ExternalInput").ap()
    wffn_t = nc.dram_tensor("wffn_t", [DM, DM], I8, kind="ExternalInput").ap()
    convw = nc.dram_tensor("convw", [128, NC_DH * 4], F32, kind="ExternalInput").ap()
    convb = nc.dram_tensor("convb", [128, NC_DH], F32, kind="ExternalInput").ap()
    bdt = nc.dram_tensor("bdt", [128, NC_DH], F32, kind="ExternalInput").ap()
    alog = nc.dram_tensor("alog", [128, NC_DH * NST], F32, kind="ExternalInput").ap()
    dpar = nc.dram_tensor("dpar", [128, NC_DH], F32, kind="ExternalInput").ap()
    lngam = nc.dram_tensor("lngam", [128, NC_DM], F32, kind="ExternalInput").ap()
    lnbet = nc.dram_tensor("lnbet", [128, NC_DM], F32, kind="ExternalInput").ap()
    wsc = nc.dram_tensor("wsc", [128, NC_DM], F32, kind="ExternalInput").ap()
    fbias = nc.dram_tensor("fbias", [128, NC_DM], F32, kind="ExternalInput").ap()
    cvec = nc.dram_tensor("cvec", [128, NC_DM], F32, kind="ExternalInput").ap()

    xout = nc.dram_tensor("xout", [DM, S], F32, kind="ExternalOutput").ap()
    dbg = nc.dram_tensor("dbg", [128, 8 * TT], F32, kind="ExternalOutput").ap() if DBG else None

    g_buf = nc.dram_tensor("g_buf", [NT, DH * TT], F32).ap()
    r0_buf = nc.dram_tensor("r0_buf", [NT, DM * TT], F32).ap()
    x2_buf = nc.dram_tensor("x2_buf", [NT, DM * TT], F32).ap()
    arp_i = nc.dram_tensor("arp_i", [NT, 160 * TT], F32).ap()
    arp_o = nc.dram_tensor("arp_o", [NT, 160 * TT], F32).ap()
    aro_i = nc.dram_tensor("aro_i", [NT, DM * TT], F32).ap()
    aro_o = nc.dram_tensor("aro_o", [NT, DM * TT], F32).ap()
    mx_i = nc.dram_tensor("mx_i", [1, 1], F32).ap()
    mx_o = nc.dram_tensor("mx_o", [1, 1], F32).ap()
    st_buf = nc.dram_tensor("st_buf", [NT, 2 * TT], F32).ap()

    from contextlib import ExitStack

    led = {}
    ev = {}

    def bump(sem_name, amt):
        if pass2:
            return 0
        led[sem_name] = led.get(sem_name, 0) + amt
        return led[sem_name]

    def mark(name, sem_name):
        if pass2:
            return
        ev[name] = (sem_name, led.get(sem_name, 0))

    def alias(dst, src):
        if pass2:
            return
        ev[dst] = ev[src]

    def mark_led(name):
        if pass2:
            return
        ev[name] = dict(led)

    def wait(eng, name):
        tbl = ev_table or ev
        if not pass2 and name not in tbl:
            return
        s, v = tbl[name]
        if v > 0:
            eng.wait_ge(sems[s], v)

    def wait_led(eng, name, only=None):
        for snm, val in (ev_table or ev)[name].items():
            if val > 0 and (only is None or snm in only):
                eng.wait_ge(sems[snm], val)

    sems = {}


    with ExitStack() as ctx:
        for sname in ["dS", "dG", "dM0", "dM1", "dW", "dX", "dR", "dBC", "dBC1",
                      "dOut", "dMsc", "dSt", "sP", "sV", "sA", "sG", "sC"]:
            sems[sname] = ctx.enter_context(nc.semaphore(sname))

        def inc(instr, sname):
            instr.then_inc(sems[sname], 1)
            return bump(sname, 1)

        def dma(eng, out, in_, q="dS"):
            eng.dma_start(out=out, in_=in_).then_inc(sems[q], 16)
            return bump(q, 16)

        sb = {}
        actx = ctx.enter_context(ExitStack())

        def alloc(name, shape, dtype=F32, scope=None):
            tgt = scope if scope is not None else ctx
            sb[name] = tgt.enter_context(nc.sbuf_tensor(name, shape, dtype))
            return sb[name]

        vecs = alloc("vecs", [128, 16 * NC_DM])
        aneg = alloc("aneg", [128, NC_DH * NST])
        ones1 = alloc("ones1", [128, 1])
        onesr = alloc("onesr", [1, 128])
        lastst = alloc("lastst", [128, NC_DH * NST])

        VOFF = {}
        for i, nm in enumerate(["convb", "bdt", "dpar", "lngam", "lnbet", "wsc",
                                "fbias", "sa", "ca1", "gap", "sbv", "cb1", "gbp",
                                "gbw", "gbb", "cvec"]):
            VOFF[nm] = i * NC_DM

        def vcol(nm, c):
            return sb["vecs"][:, VOFF[nm] + c:VOFF[nm] + c + 1]

        mu_a = alloc("mu_a", [1, TT])
        r_a = alloc("r_a", [1, TT])
        stt1 = alloc("stt1", [1, TT])
        stt2 = alloc("stt2", [1, TT])
        murep = alloc("murep", [128, TT])
        rrep = alloc("rrep", [128, TT])
        mxacc = alloc("mxacc", [128, 1])
        mxrow = alloc("mxrow", [1, 130])
        xsq = alloc("xsq", [128, 2 * TT])

        alloc("w_in_sb", [128, NC_DM * 2 * DH], scope=actx)
        alloc("wxp_sb", [128, NC_DH * 160], scope=actx)
        alloc("wdt_sb", [128, DH], scope=actx)
        alloc("convw_sb", [128, NC_DH * 4], scope=actx)
        alloc("x_t", [128, NC_DM * TT], scope=actx)
        alloc("h1_t", [128, NC_DM * TT], scope=actx)
        alloc("r0_t", [128, NC_DM * TT], scope=actx)
        sb["xn_t"] = sb["r0_t"]
        alloc("xs_t", [128, NC_DH * (TT + 3)], scope=actx)
        alloc("scrA", [128, NC_DH * TT], scope=actx)
        alloc("scrB", [128, 8 * TT], scope=actx)
        alloc("scrC", [128, 4 * TT], scope=actx)
        alloc("delta_t", [128, NC_DH * TT], scope=actx)
        alloc("du_t", [128, NC_DH * TT], scope=actx)
        alloc("HC0", [128, 4 * TT], scope=actx)
        alloc("HC1", [128, 4 * TT], scope=actx)
        alloc("sz_t", [128, NC_DH * TT], scope=actx)
        alloc("H_t", [128, 4 * TT], scope=actx)
        alloc("H1", [128, 4 * TT], scope=actx)
        alloc("scrC1", [128, 4 * TT], scope=actx)
        alloc("y_t", [128, NC_DH * TT], scope=actx)
        alloc("pdt_t", [128, TT], scope=actx)
        alloc("pbc_t", [32, TT], scope=actx)
        alloc("dt_sb", [128, TT], scope=actx)
        alloc("bc_sb", [32, TT], scope=actx)
        s0ctx = actx.enter_context(ExitStack())
        alloc("cs_t", [128, NC_DM], scope=s0ctx)
        alloc("modrow", [1, 1024], scope=s0ctx)
        alloc("wm0", [128, 512], scope=s0ctx)
        alloc("wm1", [128, 512], scope=s0ctx)

        ps = {}
        for nm, shape in [("smu", [1, TT]), ("sm2", [1, TT]), ("bc", [128, 2 * TT]),
                          ("xz0", [128, TT]), ("xz1", [128, TT]),
                          ("pdt", [128, TT]), ("pbc", [32, TT])]:
            ps[nm] = ctx.enter_context(nc.psum_tensor("ps_" + nm, shape, F32))
        ps["mu"] = ps["smu"][0:1, :]
        ps["m2"] = ps["sm2"][0:1, :]
        ps["mod"] = ps["bc"][0:1, 0:512]
        ps["dl0"] = ps["xz0"]
        ps["dl1"] = ps["xz1"]

        block = ctx.enter_context(nc.Block())
        thunks = {k: [] for k in ["sync", "tensor", "vector", "scalar", "gpsimd"]}
        _dummy = _DummyEngine()

        def sec(engname):
            if not pass2:
                def dec(fn):
                    fn(_dummy)
                return dec

            def dec(fn):
                thunks[engname].append(fn)
            return dec

        @sec("sync")
        def _(sy):
            dma(sy, sb["w_in_sb"][:].rearrange("p (c o) -> p c o", c=NC_DM),
                w_in_t[:, :].rearrange("(c p) o -> p c o", p=128), q="dW")
            dma(sy, sb["wxp_sb"][:].rearrange("p (c o) -> p c o", c=NC_DH),
                wxp_t[:, :].rearrange("(c p) o -> p c o", p=128), q="dW")
            dma(sy, sb["wdt_sb"][:], wdt_t, q="dW")
            dma(sy, sb["convw_sb"][:], convw, q="dW")
            dma(sy, sb["aneg"][:], alog, q="dW")
            for nm, src in [("convb", convb), ("bdt", bdt), ("dpar", dpar),
                            ("lngam", lngam), ("lnbet", lnbet), ("wsc", wsc),
                            ("fbias", fbias), ("cvec", cvec)]:
                dma(sy, sb["vecs"][:, VOFF[nm]:VOFF[nm] + NC_DM], src, q="dW")
            mark("weights", "dW")

        @sec("vector")
        def _(v):
            v.memset(sb["ones1"][:], 1.0)
            v.memset(sb["onesr"][:], 1.0)
            v.memset(sb["lastst"][:], 0.0)
            wait(v, "weights")
            inc(v.memset(sb["mxacc"][:], 0.0), "sV")
            mark("v_setup", "sV")

        @sec("scalar")
        def _(sc):
            wait(sc, "weights")
            sc.activation(out=sb["cs_t"][:], in_=sb["vecs"][:, VOFF["cvec"]:VOFF["cvec"] + NC_DM],
                          func=AF.Silu)
            inc(sc.activation(out=sb["aneg"][:], in_=sb["aneg"][:], func=AF.Exp), "sA")
            mark("a_setup", "sA")

        @sec("vector")
        def _(v):
            wait(v, "a_setup")
            inc(v.tensor_scalar(out=sb["aneg"][:], in0=sb["aneg"][:], scalar1=-1.0,
                                scalar2=None, op0=ALU.mult), "sV")
            mark("aneg_done", "sV")

        NJ = 6 * DM // 512
        QNAMES = ["sa", "ca1", "gap", "sbv", "cb1", "gbp"]
        for j in range(NJ):
            for k in range(NC_DM):
                idx = j * NC_DM + k

                @sec("sync")
                def _(sy, j=j, k=k, idx=idx):
                    if idx == 0:
                        wait(sy, "a_setup")
                    if idx >= 2:
                        wait(sy, f"modmm_{idx - 2}")
                    wm = [sb["wm0"], sb["wm1"]][idx % 2]
                    dma(sy, wm[:], wmod[k * 128:(k + 1) * 128, j * 512:(j + 1) * 512],
                        q="dM%d" % (idx % 2))
                    mark(f"modld_{idx}", "dM%d" % (idx % 2))

                @sec("tensor")
                def _(te, j=j, k=k, idx=idx):
                    wm = [sb["wm0"], sb["wm1"]][idx % 2]
                    wait(te, f"modld_{idx}")
                    if k == 0 and j > 0:
                        wait(te, f"modev_{j - 1}")
                    te.matmul(ps["mod"], sb["cs_t"][:, k:k + 1], wm[:],
                              start=(k == 0), stop=(k == NC_DM - 1)).then_inc(sems["sP"], 1)
                    bump("sP", 1)
                    mark(f"modmm_{idx}", "sP")
                    if k == NC_DM - 1:
                        mark(f"modcol_{j}", "sP")

            @sec("vector")
            def _(v, j=j):
                wait(v, f"modcol_{j}")
                if j >= 2:
                    wait(v, f"modsc_{(j - 2) // 2}")
                inc(v.tensor_copy(sb["modrow"][0:1, (j % 2) * 512:(j % 2) * 512 + 512],
                                  ps["mod"]), "sV")
                mark(f"modev_{j}", "sV")

            if j % 2 == 1:
                q = j // 2

                @sec("sync")
                def _(sy, q=q, j=j):
                    wait(sy, f"modev_{j}")
                    for jj in range(NC_DM):
                        sy.dma_start(out=sb["vecs"][:, VOFF[QNAMES[q]] + jj:VOFF[QNAMES[q]] + jj + 1],
                                     in_=sb["modrow"][0:1, jj * 128:(jj + 1) * 128],
                                     allow_slow_non_contiguous=True).then_inc(sems["dMsc"], 16)
                        bump("dMsc", 16)
                    mark(f"modsc_{q}", "dMsc")
        mark("mod_scattered", "dMsc")

        @sec("vector")
        def _(v):
            wait(v, "mod_scattered")
            V = sb["vecs"]
            sl = lambda nm: V[:, VOFF[nm]:VOFF[nm] + NC_DM]
            v.tensor_scalar(out=sl("ca1"), in0=sl("ca1"), scalar1=1.0, scalar2=None, op0=ALU.add)
            v.tensor_scalar(out=sl("cb1"), in0=sl("cb1"), scalar1=1.0, scalar2=None, op0=ALU.add)
            for nm in ["gap", "gbp"]:
                g = sl(nm)
                v.drain()
                v.tensor_scalar(out=sb["murep"][:, 0:NC_DM], in0=g, scalar1=0.0,
                                scalar2=None, op0=ALU.is_ge)
                v.drain()
                v.tensor_scalar(out=sb["murep"][:, 0:NC_DM], in0=sb["murep"][:, 0:NC_DM],
                                scalar1=2.0, scalar2=-1.0, op0=ALU.mult, op1=ALU.add)
                v.tensor_scalar(out=sb["rrep"][:, 0:NC_DM], in0=g, scalar1=-1.0,
                                scalar2=None, op0=ALU.mult)
                v.drain()
                v.tensor_tensor(out=sb["rrep"][:, 0:NC_DM], in0=g, in1=sb["rrep"][:, 0:NC_DM],
                                op=ALU.max)
                v.drain()
                v.tensor_scalar(out=sb["rrep"][:, 0:NC_DM], in0=sb["rrep"][:, 0:NC_DM],
                                scalar1=0.25, scalar2=None, op0=ALU.max)
                v.drain()
                v.tensor_tensor(out=g, in0=sb["murep"][:, 0:NC_DM],
                                in1=sb["rrep"][:, 0:NC_DM], op=ALU.mult)
            v.drain()
            v.tensor_tensor(out=sl("gbw"), in0=sl("gbp"), in1=sl("wsc"), op=ALU.mult)
            inc(v.tensor_tensor(out=sl("gbb"), in0=sl("gbp"), in1=sl("fbias"),
                                op=ALU.mult), "sV")
            mark("mod_done", "sV")

        s0ctx.close()
        alloc("bcrep0", [128, 2 * 4 * TT], scope=actx)
        alloc("bcrep1", [128, 2 * 4 * TT], scope=actx)
        mark_led("bar0")

        def wait_bar0(eng):
            wait_led(eng, "bar0")

        def tsl(tau):
            return slice(tau * TT, (tau + 1) * TT)

        def csl(tile, c, width=TT):
            return tile[:, c * width:(c + 1) * width]

        def stats_round(tag, tau, src_name, sc_first=None, te_first=None):
            for c in range(NC_DM):
                @sec("scalar")
                def _(sc, c=c):
                    if c == 0 and sc_first is not None:
                        sc_first(sc)
                    if c >= 2:
                        wait(sc, f"{tag}m2_{tau}_{c - 2}")
                    sc.activation(out=csl(sb["xsq"], c % 2), in_=csl(sb[src_name], c),
                                  func=AF.Square).then_inc(sems["sA"], 1)
                    bump("sA", 1)
                    mark(f"{tag}sq_{tau}_{c}", "sA")

                @sec("tensor")
                def _(te, c=c):
                    if c == 0 and te_first is not None:
                        te_first(te)
                    te.matmul(ps["mu"], sb["ones1"][:], csl(sb[src_name], c),
                              start=(c == 0), stop=(c == NC_DM - 1)).then_inc(sems["sP"], 1)
                    bump("sP", 1)
                    wait(te, f"{tag}sq_{tau}_{c}")
                    te.matmul(ps["m2"], sb["ones1"][:], csl(sb["xsq"], c % 2),
                              start=(c == 0), stop=(c == NC_DM - 1)).then_inc(sems["sP"], 1)
                    bump("sP", 1)
                    mark(f"{tag}m2_{tau}_{c}", "sP")
            mark(f"stats_{tag}_{tau}", "sP")

        def stats_post(v, tau, tag):
            v.drain()
            v.tensor_scalar(out=sb["mu_a"][:], in0=ps["mu"], scalar1=1.0 / DM,
                            scalar2=None, op0=ALU.mult)
            v.tensor_scalar(out=sb["stt1"][:], in0=ps["m2"], scalar1=1.0 / DM,
                            scalar2=None, op0=ALU.mult)
            v.drain()
            v.tensor_tensor(out=sb["stt2"][:], in0=sb["mu_a"][:], in1=sb["mu_a"][:],
                            op=ALU.mult)
            v.drain()
            v.tensor_tensor(out=sb["stt1"][:], in0=sb["stt1"][:], in1=sb["stt2"][:],
                            op=ALU.subtract)
            v.drain()
            inc(v.tensor_scalar(out=sb["stt1"][:], in0=sb["stt1"][:], scalar1=EPS,
                                scalar2=None, op0=ALU.add), "sV")
            mark(f"vpre_{tag}_{tau}", "sV")


        def newton(v, tau, tag):
            alias(f"post_{tag}_{tau}", f"rstd_{tag}_{tau}")


        def bcast(te, tau, tag):
            te.matmul(ps["bc"][:, 0:TT], sb["onesr"][:], sb["mu_a"][:],
                      start=True, stop=True).then_inc(sems["sP"], 1)
            bump("sP", 1)
            te.matmul(ps["bc"][:, TT:2 * TT], sb["onesr"][:], sb["r_a"][:],
                      start=True, stop=True).then_inc(sems["sP"], 1)
            bump("sP", 1)
            mark(f"bc_{tag}_{tau}", "sP")


        def rep_evac(v, tau, tag, dmu=None, dre=None):
            v.drain()
            v.tensor_copy(dmu if dmu is not None else sb["murep"][:],
                          ps["bc"][:, 0:TT])
            inc(v.tensor_copy(dre if dre is not None else sb["rrep"][:],
                              ps["bc"][:, TT:2 * TT]), "sV")
            mark(f"rep_{tag}_{tau}", "sV")


        def front_pieces(tau):
            fmu = sb["HC1"][:, 0:TT]
            fre = sb["HC1"][:, TT:2 * TT]

            def p0():
                @sec("sync")
                def _(sy, tau=tau):
                    if tau == 0:
                        wait_bar0(sy)
                    else:
                        wait(sy, f"r0c_{tau - 1}")
                    dma(sy, sb["x_t"][:].rearrange("p (c t) -> p c t", c=NC_DM),
                        xin[:, tsl(tau)].rearrange("(c p) t -> p c t", p=128), q="dX")
                    mark(f"xld_{tau}", "dX")

            def p1():
                def _scf1(sc, tau=tau):
                    wait(sc, f"xld_{tau}")
                    if tau == 0:
                        wait_bar0(sc)

                def _tef1(te, tau=tau):
                    if tau == 0:
                        wait_bar0(te)
                    else:
                        wait(te, f"post2_{tau - 1}")
                    wait(te, f"xld_{tau}")

                stats_round("s1", tau, "x_t", sc_first=_scf1, te_first=_tef1)
                alias(f"stats1_{tau}", f"stats_s1_{tau}")

            def p2a():
                @sec("vector")
                def _(v, tau=tau):
                    wait(v, f"stats1_{tau}")
                    stats_post(v, tau, "ln1")

                @sec("scalar")
                def _(sc, tau=tau):
                    wait(sc, f"vpre_ln1_{tau}")
                    sc.activation(out=sb["stt2"][:], in_=sb["stt1"][:], func=AF.Ln)
                    sc.drain()
                    sc.activation(out=sb["r_a"][:], in_=sb["stt2"][:], func=AF.Exp,
                                  scale=-0.5).then_inc(sems["sA"], 1)
                    bump("sA", 1)
                    mark(f"rstd_ln1_{tau}", "sA")

                @sec("vector")
                def _(v, tau=tau):
                    wait(v, f"rstd_ln1_{tau}")
                    newton(v, tau, "ln1")
                alias(f"post1_{tau}", f"post_ln1_{tau}")

                @sec("tensor")
                def _(te, tau=tau):
                    wait(te, f"post1_{tau}")
                    if tau > 0:
                        wait(te, f"rep2_{tau - 1}")
                    bcast(te, tau, "ln1")

            def p2b():
                @sec("vector")
                def _(v, tau=tau):
                    wait(v, f"bc_ln1_{tau}")
                    rep_evac(v, tau, "ln1", dmu=fmu, dre=fre)
                alias(f"rep1_{tau}", f"rep_ln1_{tau}")

            def p3():
                @sec("vector")
                def _(v, tau=tau):
                    if tau > 0:
                        wait(v, f"r0sp_{tau - 1}")
                    wait(v, f"rep1_{tau}")
                    v.drain()
                    for c in range(NC_DM):
                        v.tensor_tensor(out=csl(sb["xn_t"], c), in0=csl(sb["x_t"], c),
                                        in1=fmu, op=ALU.subtract)
                    v.drain()
                    for c in range(NC_DM):
                        v.tensor_tensor(out=csl(sb["xn_t"], c), in0=csl(sb["xn_t"], c),
                                        in1=fre, op=ALU.mult)
                    if tau > 0:
                        wait(v, f"xzmm_{tau - 1}_15")
                    v.drain()
                    for c in range(NC_DM):
                        v.tensor_scalar(out=csl(sb["h1_t"], c), in0=csl(sb["xn_t"], c),
                                        scalar1=vcol("ca1", c), scalar2=vcol("sa", c),
                                        op0=ALU.mult, op1=ALU.add)
                    v.drain()
                    for c in range(NC_DM):
                        li = v.scalar_tensor_tensor(out=csl(sb["r0_t"], c),
                                                    in0=csl(sb["h1_t"], c),
                                                    scalar=vcol("gap", c),
                                                    in1=csl(sb["x_t"], c),
                                                    op0=ALU.mult, op1=ALU.add)
                    inc(li, "sV")
                    mark(f"r0c_{tau}", "sV")

            def p4a():
                def _scf2(sc, tau=tau):
                    wait(sc, f"r0c_{tau}")

                def _tef2(te, tau=tau):
                    wait(te, f"post1_{tau}")
                    wait(te, f"r0c_{tau}")

                stats_round("s2", tau, "h1_t", sc_first=_scf2, te_first=_tef2)
                alias(f"stats2_{tau}", f"stats_s2_{tau}")

            def p4b():
                @sec("vector")
                def _(v, tau=tau):
                    wait(v, f"stats2_{tau}")
                    stats_post(v, tau, "ln2i")

                @sec("scalar")
                def _(sc, tau=tau):
                    wait(sc, f"vpre_ln2i_{tau}")
                    sc.activation(out=sb["stt2"][:], in_=sb["stt1"][:], func=AF.Ln)
                    sc.drain()
                    sc.activation(out=sb["r_a"][:], in_=sb["stt2"][:], func=AF.Exp,
                                  scale=-0.5).then_inc(sems["sA"], 1)
                    bump("sA", 1)
                    mark(f"rstd_ln2i_{tau}", "sA")

                @sec("vector")
                def _(v, tau=tau):
                    wait(v, f"rstd_ln2i_{tau}")
                    newton(v, tau, "ln2i")
                alias(f"post2_{tau}", f"post_ln2i_{tau}")

                @sec("tensor")
                def _(te, tau=tau):
                    wait(te, f"post2_{tau}")
                    wait(te, f"rep1_{tau}")
                    bcast(te, tau, "ln2i")

            def p4c():
                @sec("vector")
                def _(v, tau=tau):
                    wait(v, f"bc_ln2i_{tau}")
                    rep_evac(v, tau, "ln2i", dmu=fmu, dre=fre)
                alias(f"rep2_{tau}", f"rep_ln2i_{tau}")

            def p5():
                @sec("vector")
                def _(v, tau=tau):
                    wait(v, f"rep2_{tau}")
                    v.drain()
                    for c in range(NC_DM):
                        v.tensor_tensor(out=csl(sb["h1_t"], c), in0=csl(sb["h1_t"], c),
                                        in1=fmu, op=ALU.subtract)
                    v.drain()
                    for c in range(NC_DM):
                        v.tensor_tensor(out=csl(sb["h1_t"], c), in0=csl(sb["h1_t"], c),
                                        in1=fre, op=ALU.mult)
                    v.drain()
                    for c in range(NC_DM):
                        li = v.tensor_scalar(out=csl(sb["h1_t"], c), in0=csl(sb["h1_t"], c),
                                             scalar1=vcol("lngam", c),
                                             scalar2=vcol("lnbet", c),
                                             op0=ALU.mult, op1=ALU.add)
                    inc(li, "sV")
                    mark(f"xn2_{tau}", "sV")

                @sec("vector")
                def _(v, tau=tau):
                    if tau > 0:
                        wait(v, f"conv_{tau - 1}")
                        src = AP(sb["xs_t"][:].tensor, TT,
                                 [[NC_DH * (TT + 3), 128], [TT + 3, NC_DH], [1, 3]])
                        dst = AP(sb["xs_t"][:].tensor, 0,
                                 [[NC_DH * (TT + 3), 128], [TT + 3, NC_DH], [1, 3]])
                        inc(v.tensor_copy(dst, src), "sV")
                    else:
                        inc(v.memset(AP(sb["xs_t"][:].tensor, 0,
                                        [[NC_DH * (TT + 3), 128], [TT + 3, NC_DH], [1, 3]]),
                                     0.0), "sV")
                    mark(f"ctx_{tau}", "sV")

                @sec("sync")
                def _(sy, tau=tau):
                    wait(sy, f"r0c_{tau}")
                    dma(sy, r0_buf[tau:tau + 1, :].rearrange("o (p c t) -> o p c t",
                                                             p=128, c=NC_DM),
                        sb["r0_t"][:].rearrange("p (c t) -> p c t", c=NC_DM), q="dR")
                    mark(f"r0sp_{tau}", "dR")

            return [p0, p1, p2a, p2b, p3, p4a, p4b, p4c, p5]

        def stage_front(tau):
            for p in front_pieces(tau):
                p()


        def emit_winevac(tau, o):
            @sec("scalar")
            def _(sc, tau=tau, o=o):
                wait(sc, f"xzmm_{tau}_{o}")
                pst = ps["xz0"] if o % 2 == 0 else ps["xz1"]
                if o == 0:
                    wait(sc, f"ctx_{tau}")
                li = sc.copy(sb["xs_t"][:, o * (TT + 3) + 3:(o + 1) * (TT + 3)],
                             pst[:])
                inc(li, "sA")
                mark(f"xzev_{tau}_{o}", "sA")
                if o == 7:
                    mark(f"xsv_{tau}", "sA")

        def stage_win(tau):
            for o in range(16):
                @sec("tensor")
                def _(te, tau=tau, o=o):
                    if o == 0:
                        wait(te, f"xn2_{tau}")
                    if o < 2 and tau > 0:
                        wait(te, f"sp_{tau - 1}_{o + 6}")
                    if o >= 2 or tau > 0:
                        oo, tt2 = o - 2, tau
                        if oo < 0:
                            oo += 16
                            tt2 -= 1
                        wait(te, f"xzev_{tt2}_{oo}")
                    pst = ps["xz0"] if o % 2 == 0 else ps["xz1"]
                    for k in range(NC_DM):
                        li = te.matmul(pst[:], sb["w_in_sb"][:, k * 2048 + o * 128:
                                                             k * 2048 + o * 128 + 128],
                                       csl(sb["h1_t"], k), start=(k == 0),
                                       stop=(k == NC_DM - 1))
                    inc(li, "sP")
                    mark(f"xzmm_{tau}_{o}", "sP")

                if o >= 8:
                    @sec("scalar")
                    def _(sc, tau=tau, o=o):
                        wait(sc, f"xzmm_{tau}_{o}")
                        pst = ps["xz0"] if o % 2 == 0 else ps["xz1"]
                        if tau > 0 and o == 8:
                            wait(sc, f"g_{tau - 1}")
                        li = sc.activation(out=csl(sb["sz_t"], o - 8), in_=pst[:],
                                           func=AF.Silu)
                        inc(li, "sA")
                        mark(f"xzev_{tau}_{o}", "sA")
                        if o == 15:
                            mark(f"xsev_{tau}", "sA")
                elif tau == 0:
                    emit_winevac(0, o)


        def stage_tail(tau):
            for p in range(4):
                @sec("vector")
                def _(v, tau=tau, p=p):
                    if p == 0:
                        wait(v, f"xsv_{tau}")
                        if tau > 0:
                            wait(v, f"exp_{tau - 1}_7_3")
                    v.drain()
                    for j in range(4):
                        if j > 0:
                            v.drain()
                        for c in (2 * p, 2 * p + 1):
                            base = c * (TT + 3)
                            if j == 0:
                                li = v.tensor_scalar(
                                    out=csl(sb["scrB"], c),
                                    in0=sb["xs_t"][:, base:base + TT],
                                    scalar1=sb["convw_sb"][:, 4 * c:4 * c + 1],
                                    scalar2=vcol("convb", c),
                                    op0=ALU.mult, op1=ALU.add)
                            else:
                                li = v.scalar_tensor_tensor(
                                    out=csl(sb["scrB"], c),
                                    in0=sb["xs_t"][:, base + j:base + j + TT],
                                    scalar=sb["convw_sb"][:, 4 * c + j:4 * c + j + 1],
                                    in1=csl(sb["scrB"], c),
                                    op0=ALU.mult, op1=ALU.add)
                    inc(li, "sV")
                    mark(f"conv_{tau}_{p}", "sV")
                    if p == 3:
                        mark(f"conv_{tau}", "sV")

                @sec("scalar")
                def _(sc, tau=tau, p=p):
                    wait(sc, f"conv_{tau}_{p}")
                    if p == 0 and tau > 0:
                        wait(sc, f"gsp_{tau - 1}")
                    inc(sc.activation(out=sb["scrA"][:, 2 * p * TT:(2 * p + 2) * TT],
                                      in_=sb["scrB"][:, 2 * p * TT:(2 * p + 2) * TT],
                                      func=AF.Silu), "sA")
                    mark(f"u_{tau}_{p}", "sA")
                    if p == 3:
                        mark(f"u_{tau}", "sA")

                @sec("tensor")
                def _(te, tau=tau, p=p):
                    wait(te, f"u_{tau}_{p}")
                    if p == 0 and tau > 0:
                        wait(te, f"projev_{tau - 1}")
                    for k in (2 * p, 2 * p + 1):
                        te.matmul(ps["pdt"][:], sb["wxp_sb"][:, k * 160:k * 160 + 128],
                                  csl(sb["scrA"], k), start=(k == 0),
                                  stop=(k == NC_DH - 1))
                        li = te.matmul(ps["pbc"][:],
                                       sb["wxp_sb"][:, k * 160 + 128:k * 160 + 160],
                                       csl(sb["scrA"], k), start=(k == 0),
                                       stop=(k == NC_DH - 1))
                    if p == 3:
                        inc(li, "sP")
                        mark(f"projmm_{tau}", "sP")

            @sec("vector")
            def _(v, tau=tau):
                wait(v, f"projmm_{tau}")
                if tau > 0:
                    wait(v, f"prst_{tau - 1}")
                v.tensor_copy(sb["pdt_t"][:], ps["pdt"][:])
                inc(v.tensor_copy(sb["pbc_t"][:], ps["pbc"][:]), "sV")
                mark(f"projev_{tau}", "sV")

            @sec("gpsimd")
            def _(g, tau=tau):
                wait(g, f"projev_{tau}")
                dma(g, arp_i[tau:tau + 1, :].rearrange("o (p t) -> o p t", p=160)[:, 0:128, :],
                    sb["pdt_t"][:], q="dG")
                dma(g, arp_i[tau:tau + 1, :].rearrange("o (p t) -> o p t", p=160)[:, 128:160, :],
                    sb["pbc_t"][:], q="dG")
                mark(f"prst_{tau}", "dG")
                wait(g, f"prst_{tau}")
                g.collective_compute(
                    "AllReduce", ALU.add,
                    replica_groups=[[0, 1], [2, 3], [4, 5], [6, 7]],
                    ins=[arp_i[tau:tau + 1, :].opt()],
                    outs=[arp_o[tau:tau + 1, :].opt()],
                ).then_inc(sems["sC"], 1)
                bump("sC", 1)
                mark(f"ccp_{tau}", "sC")
                if tau > 0:
                    wait(g, f"wdtmm_{tau - 1}")
                    wait(g, f"bcfl_{tau - 1}_3")
                wait(g, f"ccp_{tau}")
                dma(g, sb["dt_sb"][:],
                    arp_o[tau:tau + 1, :].rearrange("o (p t) -> o p t", p=160)[:, 0:128, :],
                    q="dG")
                dma(g, sb["bc_sb"][:],
                    arp_o[tau:tau + 1, :].rearrange("o (p t) -> o p t", p=160)[:, 128:160, :],
                    q="dG")
                mark(f"dtbc_{tau}", "dG")


        def punit(tau, h, k, n):
            idx = h * 8 + k - n
            t2 = tau
            if idx < 0:
                idx += 32
                t2 -= 1
            return t2, idx // 8, idx % 8

        def hctree(v, tau, h, k):
            hcp = sb["HC0"]
            Hp = sb["H_t"] if k % 2 == 0 else sb["H1"]
            bcr = sb["bcrep0"] if h % 2 == 0 else sb["bcrep1"]
            v.drain()
            v.tensor_tensor(out=hcp[:], in0=Hp[:],
                            in1=bcr[:, 4 * TT:8 * TT], op=ALU.mult)
            v.drain()
            hv = AP(hcp[:].tensor, 0, [[4 * TT, 128], [1, TT], [TT, 4]])
            if h == 0:
                li = v.tensor_reduce(out=csl(sb["y_t"], k), in_=hv, axis=AXX,
                                     op=ALU.add)
            else:
                v.tensor_reduce(out=sb["murep"][:], in_=hv, axis=AXX,
                                op=ALU.add)
                v.drain()
                li = v.tensor_tensor(out=csl(sb["y_t"], k), in0=csl(sb["y_t"], k),
                                     in1=sb["murep"][:], op=ALU.add)
            inc(li, "sV")
            mark(f"tree_{tau}_{k}_{h}", "sV")

        def stage_scanblk(tau):
            fp = None
            fp_at = {(0, 0): 0, (0, 2): 1, (0, 4): 2, (0, 5): 3, (0, 6): 4,
                     (0, 7): 5, (1, 1): 6, (1, 2): 7, (1, 3): 8}
            for o in range(NC_DH):
                @sec("tensor")
                def _(te, tau=tau, o=o):
                    if o == 0:
                        wait(te, f"dtbc_{tau}")
                        if tau > 0:
                            wait(te, f"exp_{tau - 1}_7_3")
                    if o >= 2:
                        wait(te, f"sp_{tau}_{o - 2}")
                    pst = ps["dl0"] if o % 2 == 0 else ps["dl1"]
                    te.matmul(pst[:], sb["wdt_sb"][:, o * 128:(o + 1) * 128],
                              sb["dt_sb"][:], start=True, stop=True).then_inc(sems["sP"], 1)
                    bump("sP", 1)
                    mark(f"wdt_{tau}_{o}", "sP")
                    if o == NC_DH - 1:
                        mark(f"wdtmm_{tau}", "sP")

                @sec("scalar")
                def _(sc, tau=tau, o=o):
                    wait(sc, f"wdt_{tau}_{o}")
                    pst = ps["dl0"] if o % 2 == 0 else ps["dl1"]
                    sc.activation(out=csl(sb["xsq"], o % 2), in_=pst[:], func=AF.Exp,
                                  bias=vcol("bdt", o))
                    sc.drain()
                    sc.activation(out=csl(sb["delta_t"], o), in_=csl(sb["xsq"], o % 2),
                                  func=AF.Ln, bias=1.0).then_inc(sems["sA"], 1)
                    bump("sA", 1)
                    mark(f"sp_{tau}_{o}", "sA")
                    if o == NC_DH - 1:
                        mark(f"delta_{tau}", "sA")

            @sec("vector")
            def _(v, tau=tau):
                wait(v, f"delta_{tau}")
                if tau > 0:
                    wait(v, f"bw_{tau - 1}_7_3")
                v.drain()
                inc(v.tensor_tensor(out=sb["du_t"][:], in0=sb["delta_t"][:],
                                    in1=sb["scrA"][:], op=ALU.mult), "sV")
                mark(f"du_{tau}", "sV")

            for h in range(4):
                @sec("sync")
                def _(sy, tau=tau, h=h):
                    wait(sy, f"dtbc_{tau}")
                    bcr = sb["bcrep0"] if h % 2 == 0 else sb["bcrep1"]
                    qn = "dBC" if h % 2 == 0 else "dBC1"
                    if tau * 4 + h >= 2:
                        pt, ph = (tau, h - 2) if h >= 2 else (tau - 1, h + 2)
                        wait(sy, f"bw_{pt}_7_{ph}")
                        wait(sy, f"tree_{pt}_7_{ph}")
                    dma(sy, bcr[0:1, 0:4 * TT].rearrange("o (n t) -> o n t", n=4),
                        sb["bc_sb"][4 * h:4 * h + 4, :], q=qn)
                    dma(sy, bcr[0:1, 4 * TT:8 * TT].rearrange("o (n t) -> o n t", n=4),
                        sb["bc_sb"][16 + 4 * h:20 + 4 * h, :], q=qn)
                    nh = 1
                    step = 0
                    while nh < 128:
                        mark(f"bcq_{tau}_{h}_{step}", qn)
                        wait(sy, f"bcq_{tau}_{h}_{step}")
                        dma(sy, bcr[nh:2 * nh, :], bcr[0:nh, :], q=qn)
                        nh *= 2
                        step += 1
                    mark(f"bcfl_{tau}_{h}", qn)

                for k in range(NC_DH):
                    @sec("scalar")
                    def _(sc, tau=tau, h=h, k=k):
                        wait(sc, f"delta_{tau}")
                        p2t, p2h, p2k = punit(tau, h, k, 2)
                        if p2t >= 0:
                            wait(sc, f"scans_{p2t}_{p2k}_{p2h}")
                        jb = 4 * (k % 2)
                        for j in range(4):
                            n = 4 * h + j
                            li = sc.activation(
                                out=csl(sb["scrB"], jb + j), in_=csl(sb["delta_t"], k),
                                func=AF.Exp,
                                scale=sb["aneg"][:, k * NST + n:k * NST + n + 1])
                        inc(li, "sA")
                        mark(f"exp_{tau}_{k}_{h}", "sA")
                        p1t, p1h, p1k = punit(tau, h, k, 1)
                        if p1t >= 0:
                            wait(sc, f"scans_{p1t}_{p1k}_{p1h}")
                            Hp = sb["H_t"] if p1k % 2 == 0 else sb["H1"]
                            inc(sc.copy(
                                sb["lastst"][:, p1k * NST + 4 * p1h:
                                             p1k * NST + 4 * p1h + 4],
                                AP(Hp[:].tensor, TT - 1, [[4 * TT, 128], [TT, 4]])),
                                "sA")

                    if h == 2 and tau + 1 < NT:
                        emit_winevac(tau + 1, k)

                    @sec("gpsimd")
                    def _(g, tau=tau, h=h, k=k):
                        wait(g, f"du_{tau}")
                        wait(g, f"bcfl_{tau}_{h}")
                        p2t, p2h, p2k = punit(tau, h, k, 2)
                        if p2t == tau:
                            wait(g, f"scans_{p2t}_{p2k}_{p2h}")
                        elif (h, k) == (0, 1):
                            wait(g, f"scans_{tau}_0_0")
                        rep = AP(sb["du_t"][:].tensor, k * TT,
                                 [[NC_DH * TT, 128], [0, 4], [1, TT]])
                        bcr = sb["bcrep0"] if h % 2 == 0 else sb["bcrep1"]
                        scrCp = sb["scrC"] if k % 2 == 0 else sb["scrC1"]
                        g.tensor_tensor(out=scrCp[:].rearrange(
                                            "p (n t) -> p n t", n=4),
                                        in0=rep,
                                        in1=bcr[:, 0:4 * TT].rearrange(
                                            "p (n t) -> p n t", n=4),
                                        op=ALU.mult).then_inc(sems["sG"], 1)
                        bump("sG", 1)
                        mark(f"bw_{tau}_{k}_{h}", "sG")

                    @sec("vector")
                    def _(v, tau=tau, h=h, k=k):
                        wait(v, f"exp_{tau}_{k}_{h}")
                        wait(v, f"bw_{tau}_{k}_{h}")
                        jb = 4 * (k % 2)
                        Hp = sb["H_t"] if k % 2 == 0 else sb["H1"]
                        scrCp = sb["scrC"] if k % 2 == 0 else sb["scrC1"]
                        for j in range(4):
                            n = 4 * h + j
                            li = v.tensor_tensor_scan(
                                out=csl(Hp, j), data0=csl(sb["scrB"], jb + j),
                                data1=csl(scrCp, j),
                                initial=sb["lastst"][:, k * NST + n:k * NST + n + 1],
                                op0=ALU.mult, op1=ALU.add)
                        inc(li, "sV")
                        mark(f"scans_{tau}_{k}_{h}", "sV")
                        hctree(v, tau, h, k)

                    if fp is not None and (h, k) in fp_at:
                        fp[fp_at[(h, k)]]()

            @sec("vector")
            def _(v, tau=tau):
                v.drain()
                for k in range(NC_DH):
                    v.scalar_tensor_tensor(out=csl(sb["y_t"], k), in0=csl(sb["scrA"], k),
                                           scalar=vcol("dpar", k), in1=csl(sb["y_t"], k),
                                           op0=ALU.mult, op1=ALU.add)
                v.drain()
                for k in range(NC_DH):
                    li = v.tensor_tensor(out=csl(sb["scrA"], k), in0=csl(sb["y_t"], k),
                                         in1=csl(sb["sz_t"], k), op=ALU.mult)
                inc(li, "sV")
                mark(f"g_{tau}", "sV")

            @sec("sync")
            def _(sy, tau=tau):
                wait(sy, f"g_{tau}")
                dma(sy, g_buf[tau:tau + 1, :].rearrange("o (p c t) -> o p c t",
                                                        p=128, c=NC_DH),
                    sb["scrA"][:].rearrange("p (c t) -> p c t", c=NC_DH), q="dOut")
                mark(f"gsp_{tau}", "dOut")

            if DBG and tau == 0:
                @sec("sync")
                def _(sy, tau=tau):
                    wait(sy, f"g_{tau}")
                    dma(sy, dbg, sb["delta_t"][:])
                    mark("dbgw", "dS")


        for tau in range(NT):
            stage_front(tau)
            if tau >= 1:
                stage_scanblk(tau - 1)
            stage_win(tau)
            stage_tail(tau)
        stage_scanblk(NT - 1)
        actx.close()
        alloc("wout_sb", [128, NC_DH * DM])
        alloc("g_ld", [128, NC_DH * TT])
        alloc("pout_t", [128, NC_DM * TT])
        alloc("qi_t", [128, NC_DM * TT], I32)
        alloc("qf_t", [128, NC_DM * TT])
        alloc("qb_t", [128, NC_DM * TT], BF16)
        alloc("wf8", [128, NC_DM * DM], I8)
        alloc("wfb", [128, NC_DM * DM], BF16)
        alloc("c2scr", [128, NC_DM * TT])
        alloc("ps_ld", [128, NC_DM * TT])
        alloc("r0_ld", [128, NC_DM * TT])

        mark_led("barrierB")

        def wait_barrier(eng):
            wait_led(eng, "barrierB")

        @sec("sync")
        def _(sy):
            wait_barrier(sy)
            dma(sy, sb["wout_sb"][:].rearrange("p (c o) -> p c o", c=NC_DH),
                wout_t[:, :].rearrange("(c p) o -> p c o", p=128), q="dW")
            dma(sy, sb["wf8"][:].rearrange("p (c o) -> p c o", c=NC_DM),
                wffn_t[:, :].rearrange("(c p) o -> p c o", p=128), q="dW")
            mark("bweights", "dW")

        @sec("vector")
        def _(v):
            wait_barrier(v)
            wait(v, "bweights")
            inc(v.tensor_copy(sb["wfb"][:], sb["wf8"][:]), "sV")
            mark("wfb_done", "sV")

        @sec("tensor")
        def _(te):
            wait_barrier(te)
            te.sem_inc(sems["sP"], 1)
            bump("sP", 1)
            mark("te_bar", "sP")

        @sec("scalar")
        def _(sc):
            wait_barrier(sc)
            sc.sem_inc(sems["sA"], 1)
            bump("sA", 1)
            mark("sc_bar", "sA")

        @sec("gpsimd")
        def _(g):
            wait_barrier(g)
            g.sem_inc(sems["sG"], 1)
            bump("sG", 1)
            mark("g_bar", "sG")

        def stage_B(tau):
            @sec("sync")
            def _(sy, tau=tau):
                if tau == 0:
                    wait(sy, f"gsp_{NT - 1}")
                else:
                    wait(sy, f"poutmm_{tau - 1}")
                dma(sy, sb["g_ld"][:].rearrange("p (c t) -> p c t", c=NC_DH),
                    g_buf[tau:tau + 1, :].rearrange("o (p c t) -> o p c t",
                                                    p=128, c=NC_DH), q="dX")
                mark(f"gld_{tau}", "dX")

            for o in range(NC_DM):
                @sec("tensor")
                def _(te, tau=tau, o=o):
                    if o == 0:
                        wait(te, f"gld_{tau}")
                        wait(te, "bweights")
                    if o >= 2 or tau > 0:
                        oo, tt2 = o - 2, tau
                        if oo < 0:
                            oo += NC_DM
                            tt2 -= 1
                        wait(te, f"poutev_{tt2}_{oo}")
                    pst = ps["xz0"] if o % 2 == 0 else ps["xz1"]
                    for k in range(NC_DH):
                        li = te.matmul(pst[:], sb["wout_sb"][:, k * DM + o * 128:
                                                             k * DM + o * 128 + 128],
                                       csl(sb["g_ld"], k), start=(k == 0),
                                       stop=(k == NC_DH - 1))
                    inc(li, "sP")
                    mark(f"poutm_{tau}_{o}", "sP")
                    if o == NC_DM - 1:
                        mark(f"poutmm_{tau}", "sP")

                @sec("vector")
                def _(v, tau=tau, o=o):
                    wait(v, f"poutm_{tau}_{o}")
                    if tau > 0 and o == 0:
                        wait(v, f"post_{tau - 1}")
                    pst = ps["xz0"] if o % 2 == 0 else ps["xz1"]
                    inc(v.tensor_copy(csl(sb["pout_t"], o), pst[:]), "sV")
                    mark(f"poutev_{tau}_{o}", "sV")
                    if o == NC_DM - 1:
                        mark(f"poutevall_{tau}", "sV")

            @sec("gpsimd")
            def _(g, tau=tau):
                wait(g, f"poutevall_{tau}")
                dma(g, aro_i[tau:tau + 1, :].rearrange("o (p c t) -> o p c t",
                                                       p=128, c=NC_DM),
                    sb["pout_t"][:].rearrange("p (c t) -> p c t", c=NC_DM), q="dG")
                mark(f"post_{tau}", "dG")
                wait(g, f"post_{tau}")
                g.collective_compute(
                    "AllReduce", ALU.add,
                    replica_groups=[[0, 1], [2, 3], [4, 5], [6, 7]],
                    ins=[aro_i[tau:tau + 1, :].opt()],
                    outs=[aro_o[tau:tau + 1, :].opt()],
                ).then_inc(sems["sC"], 1)
                bump("sC", 1)
                mark(f"cco_{tau}", "sC")


        def stage_C1(tau):
            @sec("gpsimd")
            def _(g, tau=tau):
                wait(g, f"cco_{tau}")
                if tau > 0:
                    wait(g, f"x2c_{tau - 1}")
                    wait(g, f"mx_{tau - 1}")
                    wait(g, f"stats3_{tau - 1}")
                    wait(g, f"s3sq_{tau - 1}_{NC_DM - 1}")
                    wait(g, f"x2sp_{tau - 1}")
                dma(g, sb["ps_ld"][:].rearrange("p (c t) -> p c t", c=NC_DM),
                    aro_o[tau:tau + 1, :].rearrange("o (p c t) -> o p c t",
                                                    p=128, c=NC_DM), q="dG")
                mark(f"psld_{tau}", "dG")

            @sec("sync")
            def _(sy, tau=tau):
                if tau > 0:
                    wait(sy, f"mx_{tau - 1}")
                dma(sy, sb["r0_ld"][:].rearrange("p (c t) -> p c t", c=NC_DM),
                    r0_buf[tau:tau + 1, :].rearrange("o (p c t) -> o p c t",
                                                     p=128, c=NC_DM), q="dR")
                mark(f"r0ld_{tau}", "dR")

            @sec("vector")
            def _(v, tau=tau):
                wait(v, f"psld_{tau}")
                wait(v, f"r0ld_{tau}")
                v.drain()
                for c in range(NC_DM):
                    li = v.scalar_tensor_tensor(out=csl(sb["ps_ld"], c),
                                                in0=csl(sb["ps_ld"], c),
                                                scalar=vcol("gap", c), in1=csl(sb["r0_ld"], c),
                                                op0=ALU.mult, op1=ALU.add)
                inc(li, "sV")
                mark(f"x2c_{tau}", "sV")

            @sec("sync")
            def _(sy, tau=tau):
                wait(sy, f"x2c_{tau}")
                dma(sy, x2_buf[tau:tau + 1, :].rearrange("o (p c t) -> o p c t",
                                                         p=128, c=NC_DM),
                    sb["ps_ld"][:].rearrange("p (c t) -> p c t", c=NC_DM), q="dOut")
                mark(f"x2sp_{tau}", "dOut")

            def _scf3(sc, tau=tau):
                if tau == 0:
                    wait_barrier(sc)
                wait(sc, f"x2c_{tau}")

            def _tef3(te, tau=tau):
                wait(te, f"x2c_{tau}")
                if tau > 0:
                    wait(te, f"p3_{tau - 1}")

            stats_round("s3", tau, "ps_ld", sc_first=_scf3, te_first=_tef3)
            alias(f"stats3_{tau}", f"stats_s3_{tau}")

            @sec("vector")
            def _(v, tau=tau):
                wait(v, f"stats3_{tau}")
                if tau > 0:
                    wait(v, f"stsp_{tau - 1}")
                stats_post(v, tau, "ln3")

            @sec("scalar")
            def _(sc, tau=tau):
                wait(sc, f"vpre_ln3_{tau}")
                sc.activation(out=sb["stt2"][:], in_=sb["stt1"][:], func=AF.Ln)
                sc.drain()
                sc.activation(out=sb["r_a"][:], in_=sb["stt2"][:], func=AF.Exp,
                              scale=-0.5).then_inc(sems["sA"], 1)
                bump("sA", 1)
                mark(f"rstd_ln3_{tau}", "sA")

            @sec("vector")
            def _(v, tau=tau):
                wait(v, f"rstd_ln3_{tau}")
                newton(v, tau, "ln3")
            alias(f"p3_{tau}", f"post_ln3_{tau}")

            @sec("sync")
            def _(sy, tau=tau):
                wait(sy, f"p3_{tau}")
                dma(sy, st_buf[tau:tau + 1, 0:TT], sb["mu_a"][:], q="dSt")
                dma(sy, st_buf[tau:tau + 1, TT:2 * TT], sb["r_a"][:], q="dSt")
                mark(f"stsp_{tau}", "dSt")

            @sec("tensor")
            def _(te, tau=tau):
                wait(te, f"p3_{tau}")
                if tau > 0:
                    wait(te, f"mx_{tau - 1}")
                bcast(te, tau, "ln3")

            @sec("vector")
            def _(v, tau=tau):
                wait(v, f"bc_ln3_{tau}")
                rep_evac(v, tau, "ln3")
                wait(v, f"x2sp_{tau}")
                v.drain()
                for c in range(NC_DM):
                    v.tensor_tensor(out=csl(sb["r0_ld"], c), in0=csl(sb["ps_ld"], c),
                                    in1=sb["murep"][:], op=ALU.subtract)
                v.drain()
                for c in range(NC_DM):
                    v.tensor_tensor(out=csl(sb["r0_ld"], c), in0=csl(sb["r0_ld"], c),
                                    in1=sb["rrep"][:], op=ALU.mult)
                v.drain()
                for c in range(NC_DM):
                    v.tensor_scalar(out=csl(sb["r0_ld"], c), in0=csl(sb["r0_ld"], c),
                                    scalar1=vcol("cb1", c), scalar2=vcol("sbv", c),
                                    op0=ALU.mult, op1=ALU.add)
                v.drain()
                v.reduce_max(out=sb["rrep"][:, 0:1], in_=sb["r0_ld"][:], axis=AXX,
                             apply_absolute_value=True)
                v.drain()
                inc(v.tensor_tensor(out=sb["mxacc"][:], in0=sb["mxacc"][:],
                                    in1=sb["rrep"][:, 0:1], op=ALU.max), "sV")
                mark(f"mx_{tau}", "sV")


        for tau in range(NT + 1):
            if tau < NT:
                stage_B(tau)
            if tau >= 1:
                stage_C1(tau - 1)

        @sec("sync")
        def _(sy):
            wait(sy, f"mx_{NT - 1}")
            dma(sy, sb["mxrow"][0:1, 0:128].rearrange("o (a b) -> o a b", a=128),
                sb["mxacc"][:], q="dSt")
            mark("mxfl", "dSt")

        @sec("vector")
        def _(v):
            wait(v, "mxfl")
            v.drain()
            inc(v.reduce_max(out=sb["mxrow"][0:1, 128:129], in_=sb["mxrow"][0:1, 0:128],
                             axis=AXX, apply_absolute_value=False), "sV")
            mark("mxloc", "sV")

        @sec("gpsimd")
        def _(g):
            wait(g, "mxloc")
            dma(g, mx_i, sb["mxrow"][0:1, 128:129], q="dG")
            mark("mxst", "dG")
            wait(g, "mxst")
            g.collective_compute(
                "AllReduce", ALU.max,
                replica_groups=[list(range(8))],
                ins=[mx_i.opt()], outs=[mx_o.opt()],
            ).then_inc(sems["sC"], 1)
            bump("sC", 1)
            mark("ccmx", "sC")
            wait(g, "ccmx")
            dma(g, sb["mxrow"][0:1, 129:130], mx_o, q="dG")
            mark("mxg", "dG")

        @sec("vector")
        def _(v):
            wait(v, "mxg")
            v.drain()
            v.tensor_scalar(out=sb["mxrow"][0:1, 129:130], in0=sb["mxrow"][0:1, 129:130],
                            scalar1=1.0 / 127.0, scalar2=EPS, op0=ALU.mult, op1=ALU.max)
            v.drain()
            inc(v.reciprocal(sb["mxrow"][0:1, 128:129], sb["mxrow"][0:1, 129:130]), "sV")
            mark("xscv", "sV")

        @sec("tensor")
        def _(te):
            wait(te, "xscv")
            te.matmul(ps["bc"][:, 0:1], sb["onesr"][:], sb["mxrow"][0:1, 128:129],
                      start=True, stop=True)
            te.matmul(ps["bc"][:, 1:2], sb["onesr"][:], sb["mxrow"][0:1, 129:130],
                      start=True, stop=True).then_inc(sems["sP"], 1)
            bump("sP", 1)
            mark("xscb", "sP")

        @sec("vector")
        def _(v):
            wait(v, "xscb")
            V = sb["vecs"]
            sl = lambda nm: V[:, VOFF[nm]:VOFF[nm] + NC_DM]
            v.drain()
            v.tensor_copy(sb["mxacc"][:], ps["bc"][:, 0:1])
            v.tensor_copy(sb["ones1"][:], ps["bc"][:, 1:2])
            v.drain()
            v.tensor_scalar(out=sl("cb1"), in0=sl("cb1"), scalar1=sb["mxacc"][:],
                            scalar2=None, op0=ALU.mult)
            v.tensor_scalar(out=sl("sbv"), in0=sl("sbv"), scalar1=sb["mxacc"][:],
                            scalar2=None, op0=ALU.mult)
            inc(v.tensor_scalar(out=sl("gbw"), in0=sl("gbw"), scalar1=sb["ones1"][:],
                                scalar2=None, op0=ALU.mult), "sV")
            mark("c2const", "sV")

        for tau in range(NT):
            @sec("sync")
            def _(sy, tau=tau):
                if tau == 0:
                    wait(sy, "c2const")
                else:
                    wait(sy, f"x3_{tau - 1}")
                if tau == 0:
                    wait(sy, "mxfl")
                dma(sy, sb["pout_t"][:].rearrange("p (c t) -> p c t", c=NC_DM),
                    x2_buf[tau:tau + 1, :].rearrange("o (p c t) -> o p c t",
                                                     p=128, c=NC_DM), q="dX")
                dma(sy, sb["mu_a"][:], st_buf[tau:tau + 1, 0:TT], q="dX")
                dma(sy, sb["r_a"][:], st_buf[tau:tau + 1, TT:2 * TT], q="dX")
                mark(f"x2ld_{tau}", "dX")

            @sec("tensor")
            def _(te, tau=tau):
                wait(te, f"x2ld_{tau}")
                if tau == 0:
                    wait(te, "c2const")
                else:
                    wait(te, f"vq_{tau - 1}")
                bcast(te, tau, f"c2_{tau}")

            @sec("vector")
            def _(v, tau=tau):
                wait(v, f"bc_c2_{tau}_{tau}")
                rep_evac(v, tau, f"c2r_{tau}")
                wait(v, "c2const")
                if tau > 0:
                    wait(v, f"fmm_{tau - 1}_{NC_DM - 1}")
                    wait(v, f"x3sp_{tau - 1}")
                v.drain()
                for c in range(NC_DM):
                    v.tensor_tensor(out=csl(sb["g_ld"], c), in0=csl(sb["pout_t"], c),
                                    in1=sb["murep"][:], op=ALU.subtract)
                v.drain()
                for c in range(NC_DM):
                    v.tensor_tensor(out=csl(sb["g_ld"], c), in0=csl(sb["g_ld"], c),
                                    in1=sb["rrep"][:], op=ALU.mult)
                v.drain()
                for c in range(NC_DM):
                    v.tensor_scalar(out=csl(sb["g_ld"], c), in0=csl(sb["g_ld"], c),
                                    scalar1=vcol("cb1", c), scalar2=vcol("sbv", c),
                                    op0=ALU.mult, op1=ALU.add)
                v.drain()
                v.tensor_copy(sb["qi_t"][:], sb["g_ld"][:])
                v.drain()
                v.tensor_copy(sb["qf_t"][:], sb["qi_t"][:])
                v.drain()
                v.tensor_tensor(out=sb["c2scr"][:], in0=sb["g_ld"][:], in1=sb["qf_t"][:],
                                op=ALU.subtract)
                v.drain()
                v.scalar_tensor_tensor(out=sb["qf_t"][:], in0=sb["c2scr"][:], scalar=0.5,
                                       in1=sb["qf_t"][:], op0=ALU.is_ge, op1=ALU.add)
                v.drain()
                v.scalar_tensor_tensor(out=sb["qf_t"][:], in0=sb["c2scr"][:], scalar=-0.5,
                                       in1=sb["qf_t"][:], op0=ALU.is_ge, op1=ALU.add)
                v.drain()
                inc(v.tensor_scalar(out=sb["qb_t"][:], in0=sb["qf_t"][:],
                                    scalar1=-1.0, scalar2=127.0, op0=ALU.add,
                                    op1=ALU.min), "sV")
                mark(f"vq_{tau}", "sV")

            for o in range(NC_DM):
                @sec("tensor")
                def _(te, tau=tau, o=o):
                    if o == 0:
                        wait(te, f"vq_{tau}")
                        if tau == 0:
                            wait(te, "wfb_done")
                    if o >= 2 or tau > 0:
                        oo, tt2 = o - 2, tau
                        if oo < 0:
                            oo += NC_DM
                            tt2 -= 1
                        wait(te, f"fev_{tt2}_{oo}")
                    pst = ps["xz0"] if o % 2 == 0 else ps["xz1"]
                    for k in range(NC_DM):
                        li = te.matmul(pst[:], sb["wfb"][:, k * DM + o * 128:
                                                         k * DM + o * 128 + 128],
                                       csl(sb["qb_t"], k), start=(k == 0),
                                       stop=(k == NC_DM - 1))
                    inc(li, "sP")
                    mark(f"fmm_{tau}_{o}", "sP")

                @sec("vector")
                def _(v, tau=tau, o=o):
                    wait(v, f"fmm_{tau}_{o}")
                    if tau > 0 and o == 0:
                        wait(v, f"x3sp_{tau - 1}")
                    pst = ps["xz0"] if o % 2 == 0 else ps["xz1"]
                    v.drain()
                    v.tensor_scalar(out=csl(sb["qf_t"], o), in0=pst[:],
                                    scalar1=vcol("gbw", o), scalar2=vcol("gbb", o),
                                    op0=ALU.mult, op1=ALU.add)
                    v.drain()
                    inc(v.tensor_tensor(out=csl(sb["qf_t"], o), in0=csl(sb["qf_t"], o),
                                        in1=csl(sb["pout_t"], o), op=ALU.add), "sV")
                    mark(f"fev_{tau}_{o}", "sV")
                    if o == NC_DM - 1:
                        mark(f"x3_{tau}", "sV")

            @sec("sync")
            def _(sy, tau=tau):
                wait(sy, f"x3_{tau}")
                dma(sy, xout[:, tsl(tau)].rearrange("(c p) t -> p c t", p=128),
                    sb["qf_t"][:].rearrange("p (c t) -> p c t", c=NC_DM), q="dOut")
                mark(f"x3sp_{tau}", "dOut")

        mark_led("final")

        @sec("sync")
        def _(sy):
            wait_led(sy, "final", only={"dW", "dX", "dR", "dBC", "dBC1", "dOut",
                                        "dMsc", "dSt", "dM0", "dM1", "dS"})

        @sec("gpsimd")
        def _(g):
            wait_led(g, "final", only={"dG"})

        if pass2:
            for engname in ["sync", "tensor", "vector", "scalar", "gpsimd"]:
                fns = thunks[engname]

                @getattr(block, engname)
                def _(eng, fns=fns):
                    for fn in fns:
                        fn(eng)

    return nc, ev


def kernel(L0_attn_x, L0_attn_ln_gamma, L0_attn_ln_beta, L0_attn_W_in,
           L0_attn_conv_w, L0_attn_conv_b, L0_attn_W_xproj, L0_attn_W_dt,
           L0_attn_b_dt, L0_attn_A_log, L0_attn_D_param, L0_attn_W_out,
           L0_ffn_weight_int8, L0_ffn_weight_scale, L0_ffn_bias, L0_c, L0_wmod):
    _, ev1 = build_program()
    nc, _ = build_program(ev1)

    def vec_pack(v):
        return np.ascontiguousarray(np.asarray(v, np.float32).reshape(NC_DM, 128).T)

    x = np.asarray(L0_attn_x, np.float32)
    W_in = np.asarray(L0_attn_W_in, np.float32)
    conv_w = np.asarray(L0_attn_conv_w, np.float32)
    conv_b = np.asarray(L0_attn_conv_b, np.float32)
    W_xp = np.asarray(L0_attn_W_xproj, np.float32)
    W_dt = np.asarray(L0_attn_W_dt, np.float32)
    b_dt = np.asarray(L0_attn_b_dt, np.float32)
    A_log = np.asarray(L0_attn_A_log, np.float32)
    D_par = np.asarray(L0_attn_D_param, np.float32)
    W_out = np.asarray(L0_attn_W_out, np.float32)
    w8 = np.asarray(L0_ffn_weight_int8, np.int8)
    wscale = np.asarray(L0_ffn_weight_scale, np.float32)
    fb = np.asarray(L0_ffn_bias, np.float32)
    cc = np.asarray(L0_c, np.float32)
    wmod = np.ascontiguousarray(np.asarray(L0_wmod, np.float32))

    in_maps = []
    for core in range(8):
        b = core // 2
        hh = core % 2
        dsl = slice(hh * DH, (hh + 1) * DH)
        w_in_sl = np.concatenate([W_in[hh * DH:(hh + 1) * DH],
                                  W_in[2048 + hh * DH:2048 + (hh + 1) * DH]], 0)
        cw = conv_w[dsl, 0, :]
        m = dict(
            xin=np.ascontiguousarray(x[b].T),
            w_in_t=np.ascontiguousarray(w_in_sl.T),
            wxp_t=np.ascontiguousarray(W_xp[:, dsl].T),
            wdt_t=np.ascontiguousarray(W_dt[dsl].T),
            wout_t=np.ascontiguousarray(W_out[:, dsl].T),
            wmod=wmod,
            wffn_t=np.ascontiguousarray(w8.T),
            convw=np.ascontiguousarray(
                cw.reshape(NC_DH, 128, 4).transpose(1, 0, 2).reshape(128, NC_DH * 4)),
            convb=vec_pack(conv_b[dsl]),
            bdt=vec_pack(b_dt[dsl]),
            alog=np.ascontiguousarray(
                A_log[dsl].reshape(NC_DH, 128, NST).transpose(1, 0, 2)
                .reshape(128, NC_DH * NST)),
            dpar=vec_pack(D_par[dsl]),
            lngam=vec_pack(np.asarray(L0_attn_ln_gamma, np.float32)),
            lnbet=vec_pack(np.asarray(L0_attn_ln_beta, np.float32)),
            wsc=vec_pack(wscale[:, 0]),
            fbias=vec_pack(fb),
            cvec=vec_pack(cc[b]),
        )
        in_maps.append(m)

    res = run_bass_kernel_spmd(nc, in_maps, core_ids=list(range(8)))
    out = np.empty((4, S, DM), np.float32)
    for b in range(4):
        out[b] = res.results[2 * b]["xout"].T
    return out


# version 15
# speedup vs baseline: 1.0730x; 1.0730x over previous
import numpy as np
import concourse.bass as bass
import concourse.mybir as mybir
from concourse.bass import AP
from concourse.bass_utils import run_bass_kernel_spmd

F32 = mybir.dt.float32
BF16 = mybir.dt.bfloat16
I8 = mybir.dt.int8
I32 = mybir.dt.int32
AF = mybir.ActivationFunctionType
ALU = mybir.AluOpType
AXX = mybir.AxisListType.X

DM = 1024
DH = 1024
NC_DM = DM // 128
NC_DH = DH // 128
NST = 16
S = 8192
TT = 256
NT = S // TT
EPS = 1e-5

DBG = False


class _DummyInstr:
    def then_inc(self, *a, **k):
        return self


class _DummyEngine:
    def __getattr__(self, name):
        def f(*a, **k):
            return _DummyInstr()
        return f


def build_program(ev_table=None):
    pass2 = ev_table is not None
    nc = bass.Bass("TRN2", target_bir_lowering=False)

    xin = nc.dram_tensor("xin", [DM, S], F32, kind="ExternalInput").ap()
    w_in_t = nc.dram_tensor("w_in_t", [DM, 2 * DH], F32, kind="ExternalInput").ap()
    wxp_t = nc.dram_tensor("wxp_t", [DH, 160], F32, kind="ExternalInput").ap()
    wdt_t = nc.dram_tensor("wdt_t", [128, DH], F32, kind="ExternalInput").ap()
    wout_t = nc.dram_tensor("wout_t", [DH, DM], F32, kind="ExternalInput").ap()
    wmod = nc.dram_tensor("wmod", [DM, 6 * DM], F32, kind="ExternalInput").ap()
    wffn_t = nc.dram_tensor("wffn_t", [DM, DM], I8, kind="ExternalInput").ap()
    convw = nc.dram_tensor("convw", [128, NC_DH * 4], F32, kind="ExternalInput").ap()
    convb = nc.dram_tensor("convb", [128, NC_DH], F32, kind="ExternalInput").ap()
    bdt = nc.dram_tensor("bdt", [128, NC_DH], F32, kind="ExternalInput").ap()
    alog = nc.dram_tensor("alog", [128, NC_DH * NST], F32, kind="ExternalInput").ap()
    dpar = nc.dram_tensor("dpar", [128, NC_DH], F32, kind="ExternalInput").ap()
    lngam = nc.dram_tensor("lngam", [128, NC_DM], F32, kind="ExternalInput").ap()
    lnbet = nc.dram_tensor("lnbet", [128, NC_DM], F32, kind="ExternalInput").ap()
    wsc = nc.dram_tensor("wsc", [128, NC_DM], F32, kind="ExternalInput").ap()
    fbias = nc.dram_tensor("fbias", [128, NC_DM], F32, kind="ExternalInput").ap()
    cvec = nc.dram_tensor("cvec", [128, NC_DM], F32, kind="ExternalInput").ap()

    xout = nc.dram_tensor("xout", [DM, S], F32, kind="ExternalOutput").ap()
    dbg = nc.dram_tensor("dbg", [128, 8 * TT], F32, kind="ExternalOutput").ap() if DBG else None

    g_buf = nc.dram_tensor("g_buf", [NT, DH * TT], F32).ap()
    r0_buf = nc.dram_tensor("r0_buf", [NT, DM * TT], F32).ap()
    x2_buf = nc.dram_tensor("x2_buf", [NT, DM * TT], F32).ap()
    arp_i = nc.dram_tensor("arp_i", [NT, 160 * TT], F32).ap()
    arp_o = nc.dram_tensor("arp_o", [NT, 160 * TT], F32).ap()
    aro_i = nc.dram_tensor("aro_i", [NT, DM * TT], F32).ap()
    aro_o = nc.dram_tensor("aro_o", [NT, DM * TT], F32).ap()
    mx_i = nc.dram_tensor("mx_i", [1, 1], F32).ap()
    mx_o = nc.dram_tensor("mx_o", [1, 1], F32).ap()
    st_buf = nc.dram_tensor("st_buf", [NT, 2 * TT], F32).ap()

    from contextlib import ExitStack

    led = {}
    ev = {}

    def bump(sem_name, amt):
        if pass2:
            return 0
        led[sem_name] = led.get(sem_name, 0) + amt
        return led[sem_name]

    def mark(name, sem_name):
        if pass2:
            return
        ev[name] = (sem_name, led.get(sem_name, 0))

    def alias(dst, src):
        if pass2:
            return
        ev[dst] = ev[src]

    def mark_led(name):
        if pass2:
            return
        ev[name] = dict(led)

    def wait(eng, name):
        tbl = ev_table or ev
        if not pass2 and name not in tbl:
            return
        s, v = tbl[name]
        if v > 0:
            eng.wait_ge(sems[s], v)

    def wait_led(eng, name, only=None):
        for snm, val in (ev_table or ev)[name].items():
            if val > 0 and (only is None or snm in only):
                eng.wait_ge(sems[snm], val)

    sems = {}


    with ExitStack() as ctx:
        for sname in ["dS", "dG", "dM0", "dM1", "dW", "dX", "dR", "dBC", "dBC1",
                      "dOut", "dMsc", "dSt", "sP", "sV", "sA", "sG", "sC"]:
            sems[sname] = ctx.enter_context(nc.semaphore(sname))

        def inc(instr, sname):
            instr.then_inc(sems[sname], 1)
            return bump(sname, 1)

        def dma(eng, out, in_, q="dS"):
            eng.dma_start(out=out, in_=in_).then_inc(sems[q], 16)
            return bump(q, 16)

        sb = {}
        actx = ctx.enter_context(ExitStack())

        def alloc(name, shape, dtype=F32, scope=None):
            tgt = scope if scope is not None else ctx
            sb[name] = tgt.enter_context(nc.sbuf_tensor(name, shape, dtype))
            return sb[name]

        vecs = alloc("vecs", [128, 16 * NC_DM])
        aneg = alloc("aneg", [128, NC_DH * NST])
        ones1 = alloc("ones1", [128, 1])
        onesr = alloc("onesr", [1, 128])
        lastst = alloc("lastst", [128, NC_DH * NST])

        VOFF = {}
        for i, nm in enumerate(["convb", "bdt", "dpar", "lngam", "lnbet", "wsc",
                                "fbias", "sa", "ca1", "gap", "sbv", "cb1", "gbp",
                                "gbw", "gbb", "cvec"]):
            VOFF[nm] = i * NC_DM

        def vcol(nm, c):
            return sb["vecs"][:, VOFF[nm] + c:VOFF[nm] + c + 1]

        mu_a = alloc("mu_a", [1, TT])
        r_a = alloc("r_a", [1, TT])
        stt1 = alloc("stt1", [1, TT])
        stt2 = alloc("stt2", [1, TT])
        murep = alloc("murep", [128, TT])
        rrep = alloc("rrep", [128, TT])
        mxacc = alloc("mxacc", [128, 1])
        mxrow = alloc("mxrow", [1, 130])
        xsq = alloc("xsq", [128, 2 * TT])

        alloc("w_in_sb", [128, NC_DM * 2 * DH], scope=actx)
        alloc("wxp_sb", [128, NC_DH * 160], scope=actx)
        alloc("wdt_sb", [128, DH], scope=actx)
        alloc("convw_sb", [128, NC_DH * 4], scope=actx)
        alloc("x_t", [128, NC_DM * TT], scope=actx)
        alloc("h1_t", [128, NC_DM * TT], scope=actx)
        alloc("r0_t", [128, NC_DM * TT], scope=actx)
        sb["xn_t"] = sb["r0_t"]
        alloc("xs_t", [128, NC_DH * (TT + 3)], scope=actx)
        alloc("scrA", [128, NC_DH * TT], scope=actx)
        alloc("scrB", [128, 8 * TT], scope=actx)
        alloc("scrC", [128, 4 * TT], scope=actx)
        alloc("delta_t", [128, NC_DH * TT], scope=actx)
        alloc("du_t", [128, NC_DH * TT], scope=actx)
        alloc("HC0", [128, 4 * TT], scope=actx)
        alloc("HC1", [128, 4 * TT], scope=actx)
        alloc("sz_t", [128, NC_DH * TT], scope=actx)
        alloc("H_t", [128, 4 * TT], scope=actx)
        alloc("H1", [128, 4 * TT], scope=actx)
        alloc("scrC1", [128, 4 * TT], scope=actx)
        alloc("y_t", [128, NC_DH * TT], scope=actx)
        alloc("pdt_t", [128, TT], scope=actx)
        alloc("pbc_t", [32, TT], scope=actx)
        alloc("dt_sb", [128, TT], scope=actx)
        alloc("bc_sb", [32, TT], scope=actx)
        s0ctx = actx.enter_context(ExitStack())
        alloc("cs_t", [128, NC_DM], scope=s0ctx)
        alloc("modrow", [1, 1024], scope=s0ctx)
        alloc("wm0", [128, 512], scope=s0ctx)
        alloc("wm1", [128, 512], scope=s0ctx)

        ps = {}
        for nm, shape in [("smu", [1, TT]), ("sm2", [1, TT]), ("bc", [128, 2 * TT]),
                          ("xz0", [128, TT]), ("xz1", [128, TT]),
                          ("pdt", [128, TT]), ("pbc", [32, TT])]:
            ps[nm] = ctx.enter_context(nc.psum_tensor("ps_" + nm, shape, F32))
        ps["mu"] = ps["smu"][0:1, :]
        ps["m2"] = ps["sm2"][0:1, :]
        ps["mod"] = ps["bc"][0:1, 0:512]
        ps["dl0"] = ps["xz0"]
        ps["dl1"] = ps["xz1"]

        block = ctx.enter_context(nc.Block())
        thunks = {k: [] for k in ["sync", "tensor", "vector", "scalar", "gpsimd"]}
        _dummy = _DummyEngine()

        def sec(engname):
            if not pass2:
                def dec(fn):
                    fn(_dummy)
                return dec

            def dec(fn):
                thunks[engname].append(fn)
            return dec

        @sec("sync")
        def _(sy):
            dma(sy, sb["w_in_sb"][:].rearrange("p (c o) -> p c o", c=NC_DM),
                w_in_t[:, :].rearrange("(c p) o -> p c o", p=128), q="dW")
            dma(sy, sb["wxp_sb"][:].rearrange("p (c o) -> p c o", c=NC_DH),
                wxp_t[:, :].rearrange("(c p) o -> p c o", p=128), q="dW")
            dma(sy, sb["wdt_sb"][:], wdt_t, q="dW")
            dma(sy, sb["convw_sb"][:], convw, q="dW")
            dma(sy, sb["aneg"][:], alog, q="dW")
            for nm, src in [("convb", convb), ("bdt", bdt), ("dpar", dpar),
                            ("lngam", lngam), ("lnbet", lnbet), ("wsc", wsc),
                            ("fbias", fbias), ("cvec", cvec)]:
                dma(sy, sb["vecs"][:, VOFF[nm]:VOFF[nm] + NC_DM], src, q="dW")
            mark("weights", "dW")

        @sec("vector")
        def _(v):
            v.memset(sb["ones1"][:], 1.0)
            v.memset(sb["onesr"][:], 1.0)
            v.memset(sb["lastst"][:], 0.0)
            wait(v, "weights")
            inc(v.memset(sb["mxacc"][:], 0.0), "sV")
            mark("v_setup", "sV")

        @sec("scalar")
        def _(sc):
            wait(sc, "weights")
            sc.activation(out=sb["cs_t"][:], in_=sb["vecs"][:, VOFF["cvec"]:VOFF["cvec"] + NC_DM],
                          func=AF.Silu)
            inc(sc.activation(out=sb["aneg"][:], in_=sb["aneg"][:], func=AF.Exp), "sA")
            mark("a_setup", "sA")

        @sec("vector")
        def _(v):
            wait(v, "a_setup")
            inc(v.tensor_scalar(out=sb["aneg"][:], in0=sb["aneg"][:], scalar1=-1.0,
                                scalar2=None, op0=ALU.mult), "sV")
            mark("aneg_done", "sV")

        NJ = 6 * DM // 512
        QNAMES = ["sa", "ca1", "gap", "sbv", "cb1", "gbp"]
        for j in range(NJ):
            for k in range(NC_DM):
                idx = j * NC_DM + k

                @sec("sync")
                def _(sy, j=j, k=k, idx=idx):
                    if idx == 0:
                        wait(sy, "a_setup")
                    if idx >= 2:
                        wait(sy, f"modmm_{idx - 2}")
                    wm = [sb["wm0"], sb["wm1"]][idx % 2]
                    dma(sy, wm[:], wmod[k * 128:(k + 1) * 128, j * 512:(j + 1) * 512],
                        q="dM%d" % (idx % 2))
                    mark(f"modld_{idx}", "dM%d" % (idx % 2))

                @sec("tensor")
                def _(te, j=j, k=k, idx=idx):
                    wm = [sb["wm0"], sb["wm1"]][idx % 2]
                    wait(te, f"modld_{idx}")
                    if k == 0 and j > 0:
                        wait(te, f"modev_{j - 1}")
                    te.matmul(ps["mod"], sb["cs_t"][:, k:k + 1], wm[:],
                              start=(k == 0), stop=(k == NC_DM - 1)).then_inc(sems["sP"], 1)
                    bump("sP", 1)
                    mark(f"modmm_{idx}", "sP")
                    if k == NC_DM - 1:
                        mark(f"modcol_{j}", "sP")

            @sec("vector")
            def _(v, j=j):
                wait(v, f"modcol_{j}")
                if j >= 2:
                    wait(v, f"modsc_{(j - 2) // 2}")
                inc(v.tensor_copy(sb["modrow"][0:1, (j % 2) * 512:(j % 2) * 512 + 512],
                                  ps["mod"]), "sV")
                mark(f"modev_{j}", "sV")

            if j % 2 == 1:
                q = j // 2

                @sec("sync")
                def _(sy, q=q, j=j):
                    wait(sy, f"modev_{j}")
                    for jj in range(NC_DM):
                        sy.dma_start(out=sb["vecs"][:, VOFF[QNAMES[q]] + jj:VOFF[QNAMES[q]] + jj + 1],
                                     in_=sb["modrow"][0:1, jj * 128:(jj + 1) * 128],
                                     allow_slow_non_contiguous=True).then_inc(sems["dMsc"], 16)
                        bump("dMsc", 16)
                    mark(f"modsc_{q}", "dMsc")
        mark("mod_scattered", "dMsc")

        @sec("vector")
        def _(v):
            wait(v, "mod_scattered")
            V = sb["vecs"]
            sl = lambda nm: V[:, VOFF[nm]:VOFF[nm] + NC_DM]
            v.tensor_scalar(out=sl("ca1"), in0=sl("ca1"), scalar1=1.0, scalar2=None, op0=ALU.add)
            v.tensor_scalar(out=sl("cb1"), in0=sl("cb1"), scalar1=1.0, scalar2=None, op0=ALU.add)
            for nm in ["gap", "gbp"]:
                g = sl(nm)
                v.drain()
                v.tensor_scalar(out=sb["murep"][:, 0:NC_DM], in0=g, scalar1=0.0,
                                scalar2=None, op0=ALU.is_ge)
                v.drain()
                v.tensor_scalar(out=sb["murep"][:, 0:NC_DM], in0=sb["murep"][:, 0:NC_DM],
                                scalar1=2.0, scalar2=-1.0, op0=ALU.mult, op1=ALU.add)
                v.tensor_scalar(out=sb["rrep"][:, 0:NC_DM], in0=g, scalar1=-1.0,
                                scalar2=None, op0=ALU.mult)
                v.drain()
                v.tensor_tensor(out=sb["rrep"][:, 0:NC_DM], in0=g, in1=sb["rrep"][:, 0:NC_DM],
                                op=ALU.max)
                v.drain()
                v.tensor_scalar(out=sb["rrep"][:, 0:NC_DM], in0=sb["rrep"][:, 0:NC_DM],
                                scalar1=0.25, scalar2=None, op0=ALU.max)
                v.drain()
                v.tensor_tensor(out=g, in0=sb["murep"][:, 0:NC_DM],
                                in1=sb["rrep"][:, 0:NC_DM], op=ALU.mult)
            v.drain()
            v.tensor_tensor(out=sl("gbw"), in0=sl("gbp"), in1=sl("wsc"), op=ALU.mult)
            inc(v.tensor_tensor(out=sl("gbb"), in0=sl("gbp"), in1=sl("fbias"),
                                op=ALU.mult), "sV")
            mark("mod_done", "sV")

        s0ctx.close()
        alloc("bcrep0", [128, 2 * 4 * TT], scope=actx)
        alloc("bcrep1", [128, 2 * 4 * TT], scope=actx)
        mark_led("bar0")

        def wait_bar0(eng):
            wait_led(eng, "bar0")

        def tsl(tau):
            return slice(tau * TT, (tau + 1) * TT)

        def csl(tile, c, width=TT):
            return tile[:, c * width:(c + 1) * width]

        def stats_round(tag, tau, src_name, sc_first=None, te_first=None):
            for c in range(NC_DM):
                @sec("scalar")
                def _(sc, c=c):
                    if c == 0 and sc_first is not None:
                        sc_first(sc)
                    if c >= 2:
                        wait(sc, f"{tag}m2_{tau}_{c - 2}")
                    sc.activation(out=csl(sb["xsq"], c % 2), in_=csl(sb[src_name], c),
                                  func=AF.Square).then_inc(sems["sA"], 1)
                    bump("sA", 1)
                    mark(f"{tag}sq_{tau}_{c}", "sA")

                @sec("tensor")
                def _(te, c=c):
                    if c == 0 and te_first is not None:
                        te_first(te)
                    te.matmul(ps["mu"], sb["ones1"][:], csl(sb[src_name], c),
                              start=(c == 0), stop=(c == NC_DM - 1)).then_inc(sems["sP"], 1)
                    bump("sP", 1)
                    wait(te, f"{tag}sq_{tau}_{c}")
                    te.matmul(ps["m2"], sb["ones1"][:], csl(sb["xsq"], c % 2),
                              start=(c == 0), stop=(c == NC_DM - 1)).then_inc(sems["sP"], 1)
                    bump("sP", 1)
                    mark(f"{tag}m2_{tau}_{c}", "sP")
            mark(f"stats_{tag}_{tau}", "sP")

        def stats_post(v, tau, tag):
            v.drain()
            v.tensor_scalar(out=sb["mu_a"][:], in0=ps["mu"], scalar1=1.0 / DM,
                            scalar2=None, op0=ALU.mult)
            v.tensor_scalar(out=sb["stt1"][:], in0=ps["m2"], scalar1=1.0 / DM,
                            scalar2=None, op0=ALU.mult)
            v.drain()
            v.tensor_tensor(out=sb["stt2"][:], in0=sb["mu_a"][:], in1=sb["mu_a"][:],
                            op=ALU.mult)
            v.drain()
            v.tensor_tensor(out=sb["stt1"][:], in0=sb["stt1"][:], in1=sb["stt2"][:],
                            op=ALU.subtract)
            v.drain()
            inc(v.tensor_scalar(out=sb["stt1"][:], in0=sb["stt1"][:], scalar1=EPS,
                                scalar2=None, op0=ALU.add), "sV")
            mark(f"vpre_{tag}_{tau}", "sV")


        def newton(v, tau, tag):
            alias(f"post_{tag}_{tau}", f"rstd_{tag}_{tau}")


        def bcast(te, tau, tag):
            te.matmul(ps["bc"][:, 0:TT], sb["onesr"][:], sb["mu_a"][:],
                      start=True, stop=True).then_inc(sems["sP"], 1)
            bump("sP", 1)
            te.matmul(ps["bc"][:, TT:2 * TT], sb["onesr"][:], sb["r_a"][:],
                      start=True, stop=True).then_inc(sems["sP"], 1)
            bump("sP", 1)
            mark(f"bc_{tag}_{tau}", "sP")


        def rep_evac(v, tau, tag, dmu=None, dre=None):
            v.drain()
            v.tensor_copy(dmu if dmu is not None else sb["murep"][:],
                          ps["bc"][:, 0:TT])
            inc(v.tensor_copy(dre if dre is not None else sb["rrep"][:],
                              ps["bc"][:, TT:2 * TT]), "sV")
            mark(f"rep_{tag}_{tau}", "sV")


        def front_pieces(tau):
            fmu = sb["HC1"][:, 0:TT]
            fre = sb["HC1"][:, TT:2 * TT]

            def p0():
                @sec("sync")
                def _(sy, tau=tau):
                    if tau == 0:
                        wait_bar0(sy)
                    else:
                        wait(sy, f"r0c_{tau - 1}")
                    dma(sy, sb["x_t"][:].rearrange("p (c t) -> p c t", c=NC_DM),
                        xin[:, tsl(tau)].rearrange("(c p) t -> p c t", p=128), q="dX")
                    mark(f"xld_{tau}", "dX")

            def p1():
                def _scf1(sc, tau=tau):
                    wait(sc, f"xld_{tau}")
                    if tau == 0:
                        wait_bar0(sc)

                def _tef1(te, tau=tau):
                    if tau == 0:
                        wait_bar0(te)
                    else:
                        wait(te, f"post2_{tau - 1}")
                    wait(te, f"xld_{tau}")

                stats_round("s1", tau, "x_t", sc_first=_scf1, te_first=_tef1)
                alias(f"stats1_{tau}", f"stats_s1_{tau}")

            def p2a():
                @sec("vector")
                def _(v, tau=tau):
                    wait(v, f"stats1_{tau}")
                    stats_post(v, tau, "ln1")

                @sec("scalar")
                def _(sc, tau=tau):
                    wait(sc, f"vpre_ln1_{tau}")
                    sc.activation(out=sb["stt2"][:], in_=sb["stt1"][:], func=AF.Ln)
                    sc.drain()
                    sc.activation(out=sb["r_a"][:], in_=sb["stt2"][:], func=AF.Exp,
                                  scale=-0.5).then_inc(sems["sA"], 1)
                    bump("sA", 1)
                    mark(f"rstd_ln1_{tau}", "sA")

                @sec("vector")
                def _(v, tau=tau):
                    wait(v, f"rstd_ln1_{tau}")
                    newton(v, tau, "ln1")
                alias(f"post1_{tau}", f"post_ln1_{tau}")

                @sec("tensor")
                def _(te, tau=tau):
                    wait(te, f"post1_{tau}")
                    if tau > 0:
                        wait(te, f"rep2_{tau - 1}")
                    bcast(te, tau, "ln1")

            def p2b():
                @sec("vector")
                def _(v, tau=tau):
                    wait(v, f"bc_ln1_{tau}")
                    rep_evac(v, tau, "ln1", dmu=fmu, dre=fre)
                alias(f"rep1_{tau}", f"rep_ln1_{tau}")

            def p3():
                @sec("vector")
                def _(v, tau=tau):
                    if tau > 0:
                        wait(v, f"r0sp_{tau - 1}")
                    wait(v, f"rep1_{tau}")
                    v.drain()
                    for c in range(NC_DM):
                        v.tensor_tensor(out=csl(sb["xn_t"], c), in0=csl(sb["x_t"], c),
                                        in1=fmu, op=ALU.subtract)
                    v.drain()
                    for c in range(NC_DM):
                        v.tensor_tensor(out=csl(sb["xn_t"], c), in0=csl(sb["xn_t"], c),
                                        in1=fre, op=ALU.mult)
                    if tau > 0:
                        wait(v, f"xzmm_{tau - 1}_15")
                    v.drain()
                    for c in range(NC_DM):
                        v.tensor_scalar(out=csl(sb["h1_t"], c), in0=csl(sb["xn_t"], c),
                                        scalar1=vcol("ca1", c), scalar2=vcol("sa", c),
                                        op0=ALU.mult, op1=ALU.add)
                    v.drain()
                    for c in range(NC_DM):
                        li = v.scalar_tensor_tensor(out=csl(sb["r0_t"], c),
                                                    in0=csl(sb["h1_t"], c),
                                                    scalar=vcol("gap", c),
                                                    in1=csl(sb["x_t"], c),
                                                    op0=ALU.mult, op1=ALU.add)
                    inc(li, "sV")
                    mark(f"r0c_{tau}", "sV")

            def p4a():
                def _scf2(sc, tau=tau):
                    wait(sc, f"r0c_{tau}")

                def _tef2(te, tau=tau):
                    wait(te, f"post1_{tau}")
                    wait(te, f"r0c_{tau}")

                stats_round("s2", tau, "h1_t", sc_first=_scf2, te_first=_tef2)
                alias(f"stats2_{tau}", f"stats_s2_{tau}")

            def p4b():
                @sec("vector")
                def _(v, tau=tau):
                    wait(v, f"stats2_{tau}")
                    stats_post(v, tau, "ln2i")

                @sec("scalar")
                def _(sc, tau=tau):
                    wait(sc, f"vpre_ln2i_{tau}")
                    sc.activation(out=sb["stt2"][:], in_=sb["stt1"][:], func=AF.Ln)
                    sc.drain()
                    sc.activation(out=sb["r_a"][:], in_=sb["stt2"][:], func=AF.Exp,
                                  scale=-0.5).then_inc(sems["sA"], 1)
                    bump("sA", 1)
                    mark(f"rstd_ln2i_{tau}", "sA")

                @sec("vector")
                def _(v, tau=tau):
                    wait(v, f"rstd_ln2i_{tau}")
                    newton(v, tau, "ln2i")
                alias(f"post2_{tau}", f"post_ln2i_{tau}")

                @sec("tensor")
                def _(te, tau=tau):
                    wait(te, f"post2_{tau}")
                    wait(te, f"rep1_{tau}")
                    bcast(te, tau, "ln2i")

            def p4c():
                @sec("vector")
                def _(v, tau=tau):
                    wait(v, f"bc_ln2i_{tau}")
                    rep_evac(v, tau, "ln2i", dmu=fmu, dre=fre)
                alias(f"rep2_{tau}", f"rep_ln2i_{tau}")

            def p5():
                @sec("vector")
                def _(v, tau=tau):
                    wait(v, f"rep2_{tau}")
                    v.drain()
                    for c in range(NC_DM):
                        v.tensor_tensor(out=csl(sb["h1_t"], c), in0=csl(sb["h1_t"], c),
                                        in1=fmu, op=ALU.subtract)
                    v.drain()
                    for c in range(NC_DM):
                        v.tensor_tensor(out=csl(sb["h1_t"], c), in0=csl(sb["h1_t"], c),
                                        in1=fre, op=ALU.mult)
                    v.drain()
                    for c in range(NC_DM):
                        li = v.tensor_scalar(out=csl(sb["h1_t"], c), in0=csl(sb["h1_t"], c),
                                             scalar1=vcol("lngam", c),
                                             scalar2=vcol("lnbet", c),
                                             op0=ALU.mult, op1=ALU.add)
                    inc(li, "sV")
                    mark(f"xn2_{tau}", "sV")

                @sec("vector")
                def _(v, tau=tau):
                    if tau > 0:
                        wait(v, f"conv_{tau - 1}")
                        src = AP(sb["xs_t"][:].tensor, TT,
                                 [[NC_DH * (TT + 3), 128], [TT + 3, NC_DH], [1, 3]])
                        dst = AP(sb["xs_t"][:].tensor, 0,
                                 [[NC_DH * (TT + 3), 128], [TT + 3, NC_DH], [1, 3]])
                        inc(v.tensor_copy(dst, src), "sV")
                    else:
                        inc(v.memset(AP(sb["xs_t"][:].tensor, 0,
                                        [[NC_DH * (TT + 3), 128], [TT + 3, NC_DH], [1, 3]]),
                                     0.0), "sV")
                    mark(f"ctx_{tau}", "sV")

                @sec("sync")
                def _(sy, tau=tau):
                    wait(sy, f"r0c_{tau}")
                    dma(sy, r0_buf[tau:tau + 1, :].rearrange("o (p c t) -> o p c t",
                                                             p=128, c=NC_DM),
                        sb["r0_t"][:].rearrange("p (c t) -> p c t", c=NC_DM), q="dR")
                    mark(f"r0sp_{tau}", "dR")

            return [p0, p1, p2a, p2b, p3, p4a, p4b, p4c, p5]

        def stage_front(tau):
            for p in front_pieces(tau):
                p()


        def emit_winevac(tau, o):
            @sec("scalar")
            def _(sc, tau=tau, o=o):
                wait(sc, f"xzmm_{tau}_{o}")
                pst = ps["xz0"] if o % 2 == 0 else ps["xz1"]
                if o == 0:
                    wait(sc, f"ctx_{tau}")
                li = sc.copy(sb["xs_t"][:, o * (TT + 3) + 3:(o + 1) * (TT + 3)],
                             pst[:])
                inc(li, "sA")
                mark(f"xzev_{tau}_{o}", "sA")
                if o == 7:
                    mark(f"xsv_{tau}", "sA")

        def stage_win(tau):
            for o in range(16):
                @sec("tensor")
                def _(te, tau=tau, o=o):
                    if o == 0:
                        wait(te, f"xn2_{tau}")
                    if o < 2 and tau > 0:
                        wait(te, f"sp_{tau - 1}_{o + 6}")
                    if o >= 2 or tau > 0:
                        oo, tt2 = o - 2, tau
                        if oo < 0:
                            oo += 16
                            tt2 -= 1
                        wait(te, f"xzev_{tt2}_{oo}")
                    pst = ps["xz0"] if o % 2 == 0 else ps["xz1"]
                    for k in range(NC_DM):
                        li = te.matmul(pst[:], sb["w_in_sb"][:, k * 2048 + o * 128:
                                                             k * 2048 + o * 128 + 128],
                                       csl(sb["h1_t"], k), start=(k == 0),
                                       stop=(k == NC_DM - 1))
                    inc(li, "sP")
                    mark(f"xzmm_{tau}_{o}", "sP")

                if o >= 8:
                    @sec("scalar")
                    def _(sc, tau=tau, o=o):
                        wait(sc, f"xzmm_{tau}_{o}")
                        pst = ps["xz0"] if o % 2 == 0 else ps["xz1"]
                        if tau > 0 and o == 8:
                            wait(sc, f"g_{tau - 1}")
                        li = sc.activation(out=csl(sb["sz_t"], o - 8), in_=pst[:],
                                           func=AF.Silu)
                        inc(li, "sA")
                        mark(f"xzev_{tau}_{o}", "sA")
                        if o == 15:
                            mark(f"xsev_{tau}", "sA")
                elif tau == 0:
                    emit_winevac(0, o)


        def stage_tail(tau):
            for p in range(4):
                @sec("vector")
                def _(v, tau=tau, p=p):
                    if p == 0:
                        wait(v, f"xsv_{tau}")
                        if tau > 0:
                            wait(v, f"exp_{tau - 1}_7_3")
                    v.drain()
                    for j in range(4):
                        if j > 0:
                            v.drain()
                        for c in (2 * p, 2 * p + 1):
                            base = c * (TT + 3)
                            if j == 0:
                                li = v.tensor_scalar(
                                    out=csl(sb["scrB"], c),
                                    in0=sb["xs_t"][:, base:base + TT],
                                    scalar1=sb["convw_sb"][:, 4 * c:4 * c + 1],
                                    scalar2=vcol("convb", c),
                                    op0=ALU.mult, op1=ALU.add)
                            else:
                                li = v.scalar_tensor_tensor(
                                    out=csl(sb["scrB"], c),
                                    in0=sb["xs_t"][:, base + j:base + j + TT],
                                    scalar=sb["convw_sb"][:, 4 * c + j:4 * c + j + 1],
                                    in1=csl(sb["scrB"], c),
                                    op0=ALU.mult, op1=ALU.add)
                    inc(li, "sV")
                    mark(f"conv_{tau}_{p}", "sV")
                    if p == 3:
                        mark(f"conv_{tau}", "sV")

                @sec("scalar")
                def _(sc, tau=tau, p=p):
                    wait(sc, f"conv_{tau}_{p}")
                    if p == 0 and tau > 0:
                        wait(sc, f"gsp_{tau - 1}")
                    inc(sc.activation(out=sb["scrA"][:, 2 * p * TT:(2 * p + 2) * TT],
                                      in_=sb["scrB"][:, 2 * p * TT:(2 * p + 2) * TT],
                                      func=AF.Silu), "sA")
                    mark(f"u_{tau}_{p}", "sA")
                    if p == 3:
                        mark(f"u_{tau}", "sA")

                @sec("tensor")
                def _(te, tau=tau, p=p):
                    wait(te, f"u_{tau}_{p}")
                    if p == 0 and tau > 0:
                        wait(te, f"projev_{tau - 1}")
                    for k in (2 * p, 2 * p + 1):
                        te.matmul(ps["pdt"][:], sb["wxp_sb"][:, k * 160:k * 160 + 128],
                                  csl(sb["scrA"], k), start=(k == 0),
                                  stop=(k == NC_DH - 1))
                        li = te.matmul(ps["pbc"][:],
                                       sb["wxp_sb"][:, k * 160 + 128:k * 160 + 160],
                                       csl(sb["scrA"], k), start=(k == 0),
                                       stop=(k == NC_DH - 1))
                    if p == 3:
                        inc(li, "sP")
                        mark(f"projmm_{tau}", "sP")

            @sec("vector")
            def _(v, tau=tau):
                wait(v, f"projmm_{tau}")
                if tau > 0:
                    wait(v, f"prst_{tau - 1}")
                v.tensor_copy(sb["pdt_t"][:], ps["pdt"][:])
                inc(v.tensor_copy(sb["pbc_t"][:], ps["pbc"][:]), "sV")
                mark(f"projev_{tau}", "sV")

            @sec("gpsimd")
            def _(g, tau=tau):
                wait(g, f"projev_{tau}")
                dma(g, arp_i[tau:tau + 1, :].rearrange("o (p t) -> o p t", p=160)[:, 0:128, :],
                    sb["pdt_t"][:], q="dG")
                dma(g, arp_i[tau:tau + 1, :].rearrange("o (p t) -> o p t", p=160)[:, 128:160, :],
                    sb["pbc_t"][:], q="dG")
                mark(f"prst_{tau}", "dG")
                wait(g, f"prst_{tau}")
                g.collective_compute(
                    "AllReduce", ALU.add,
                    replica_groups=[[0, 1], [2, 3], [4, 5], [6, 7]],
                    ins=[arp_i[tau:tau + 1, :].opt()],
                    outs=[arp_o[tau:tau + 1, :].opt()],
                ).then_inc(sems["sC"], 1)
                bump("sC", 1)
                mark(f"ccp_{tau}", "sC")
                if tau > 0:
                    wait(g, f"wdtmm_{tau - 1}")
                    wait(g, f"bcfl_{tau - 1}_3")
                wait(g, f"ccp_{tau}")
                dma(g, sb["dt_sb"][:],
                    arp_o[tau:tau + 1, :].rearrange("o (p t) -> o p t", p=160)[:, 0:128, :],
                    q="dG")
                dma(g, sb["bc_sb"][:],
                    arp_o[tau:tau + 1, :].rearrange("o (p t) -> o p t", p=160)[:, 128:160, :],
                    q="dG")
                mark(f"dtbc_{tau}", "dG")


        def punit(tau, h, k, n):
            idx = h * 8 + k - n
            t2 = tau
            if idx < 0:
                idx += 32
                t2 -= 1
            return t2, idx // 8, idx % 8

        def hctree(v, tau, h, k):
            hcp = sb["HC0"]
            Hp = sb["H_t"] if k % 2 == 0 else sb["H1"]
            bcr = sb["bcrep0"] if h % 2 == 0 else sb["bcrep1"]
            v.drain()
            v.tensor_tensor(out=hcp[:], in0=Hp[:],
                            in1=bcr[:, 4 * TT:8 * TT], op=ALU.mult)
            v.drain()
            hv = AP(hcp[:].tensor, 0, [[4 * TT, 128], [1, TT], [TT, 4]])
            if h == 0:
                li = v.tensor_reduce(out=csl(sb["y_t"], k), in_=hv, axis=AXX,
                                     op=ALU.add)
            else:
                v.tensor_reduce(out=sb["murep"][:], in_=hv, axis=AXX,
                                op=ALU.add)
                v.drain()
                li = v.tensor_tensor(out=csl(sb["y_t"], k), in0=csl(sb["y_t"], k),
                                     in1=sb["murep"][:], op=ALU.add)
            inc(li, "sV")
            mark(f"tree_{tau}_{k}_{h}", "sV")

        def stage_scanblk(tau):
            fp = front_pieces(tau + 1) if tau + 1 < NT else None
            fp_at = {(0, 0): 0, (0, 2): 1, (0, 4): 2, (0, 5): 3, (0, 6): 4,
                     (0, 7): 5, (1, 1): 6, (1, 2): 7, (1, 3): 8}
            for o in range(NC_DH):
                @sec("tensor")
                def _(te, tau=tau, o=o):
                    if o == 0:
                        wait(te, f"dtbc_{tau}")
                        if tau > 0:
                            wait(te, f"exp_{tau - 1}_7_3")
                    if o >= 2:
                        wait(te, f"sp_{tau}_{o - 2}")
                    pst = ps["dl0"] if o % 2 == 0 else ps["dl1"]
                    te.matmul(pst[:], sb["wdt_sb"][:, o * 128:(o + 1) * 128],
                              sb["dt_sb"][:], start=True, stop=True).then_inc(sems["sP"], 1)
                    bump("sP", 1)
                    mark(f"wdt_{tau}_{o}", "sP")
                    if o == NC_DH - 1:
                        mark(f"wdtmm_{tau}", "sP")

                @sec("scalar")
                def _(sc, tau=tau, o=o):
                    wait(sc, f"wdt_{tau}_{o}")
                    pst = ps["dl0"] if o % 2 == 0 else ps["dl1"]
                    sc.activation(out=csl(sb["xsq"], o % 2), in_=pst[:], func=AF.Exp,
                                  bias=vcol("bdt", o))
                    sc.drain()
                    sc.activation(out=csl(sb["delta_t"], o), in_=csl(sb["xsq"], o % 2),
                                  func=AF.Ln, bias=1.0).then_inc(sems["sA"], 1)
                    bump("sA", 1)
                    mark(f"sp_{tau}_{o}", "sA")
                    if o == NC_DH - 1:
                        mark(f"delta_{tau}", "sA")

            @sec("vector")
            def _(v, tau=tau):
                wait(v, f"delta_{tau}")
                if tau > 0:
                    wait(v, f"bw_{tau - 1}_7_3")
                v.drain()
                inc(v.tensor_tensor(out=sb["du_t"][:], in0=sb["delta_t"][:],
                                    in1=sb["scrA"][:], op=ALU.mult), "sV")
                mark(f"du_{tau}", "sV")

            for h in range(4):
                @sec("sync")
                def _(sy, tau=tau, h=h):
                    wait(sy, f"dtbc_{tau}")
                    bcr = sb["bcrep0"] if h % 2 == 0 else sb["bcrep1"]
                    qn = "dBC" if h % 2 == 0 else "dBC1"
                    if tau * 4 + h >= 2:
                        pt, ph = (tau, h - 2) if h >= 2 else (tau - 1, h + 2)
                        wait(sy, f"bw_{pt}_7_{ph}")
                        wait(sy, f"tree_{pt}_7_{ph}")
                    dma(sy, bcr[0:1, 0:4 * TT].rearrange("o (n t) -> o n t", n=4),
                        sb["bc_sb"][4 * h:4 * h + 4, :], q=qn)
                    dma(sy, bcr[0:1, 4 * TT:8 * TT].rearrange("o (n t) -> o n t", n=4),
                        sb["bc_sb"][16 + 4 * h:20 + 4 * h, :], q=qn)
                    nh = 1
                    step = 0
                    while nh < 128:
                        mark(f"bcq_{tau}_{h}_{step}", qn)
                        wait(sy, f"bcq_{tau}_{h}_{step}")
                        dma(sy, bcr[nh:2 * nh, :], bcr[0:nh, :], q=qn)
                        nh *= 2
                        step += 1
                    mark(f"bcfl_{tau}_{h}", qn)

                for k in range(NC_DH):
                    @sec("scalar")
                    def _(sc, tau=tau, h=h, k=k):
                        wait(sc, f"delta_{tau}")
                        p2t, p2h, p2k = punit(tau, h, k, 2)
                        if p2t >= 0:
                            wait(sc, f"scans_{p2t}_{p2k}_{p2h}")
                        jb = 4 * (k % 2)
                        for j in range(4):
                            n = 4 * h + j
                            li = sc.activation(
                                out=csl(sb["scrB"], jb + j), in_=csl(sb["delta_t"], k),
                                func=AF.Exp,
                                scale=sb["aneg"][:, k * NST + n:k * NST + n + 1])
                        inc(li, "sA")
                        mark(f"exp_{tau}_{k}_{h}", "sA")
                        p1t, p1h, p1k = punit(tau, h, k, 1)
                        if p1t >= 0:
                            wait(sc, f"scans_{p1t}_{p1k}_{p1h}")
                            Hp = sb["H_t"] if p1k % 2 == 0 else sb["H1"]
                            inc(sc.copy(
                                sb["lastst"][:, p1k * NST + 4 * p1h:
                                             p1k * NST + 4 * p1h + 4],
                                AP(Hp[:].tensor, TT - 1, [[4 * TT, 128], [TT, 4]])),
                                "sA")


                    @sec("gpsimd")
                    def _(g, tau=tau, h=h, k=k):
                        wait(g, f"du_{tau}")
                        wait(g, f"bcfl_{tau}_{h}")
                        p2t, p2h, p2k = punit(tau, h, k, 2)
                        if p2t == tau:
                            wait(g, f"scans_{p2t}_{p2k}_{p2h}")
                        elif (h, k) == (0, 1):
                            wait(g, f"scans_{tau}_0_0")
                        rep = AP(sb["du_t"][:].tensor, k * TT,
                                 [[NC_DH * TT, 128], [0, 4], [1, TT]])
                        bcr = sb["bcrep0"] if h % 2 == 0 else sb["bcrep1"]
                        scrCp = sb["scrC"] if k % 2 == 0 else sb["scrC1"]
                        g.tensor_tensor(out=scrCp[:].rearrange(
                                            "p (n t) -> p n t", n=4),
                                        in0=rep,
                                        in1=bcr[:, 0:4 * TT].rearrange(
                                            "p (n t) -> p n t", n=4),
                                        op=ALU.mult).then_inc(sems["sG"], 1)
                        bump("sG", 1)
                        mark(f"bw_{tau}_{k}_{h}", "sG")

                    @sec("vector")
                    def _(v, tau=tau, h=h, k=k):
                        wait(v, f"exp_{tau}_{k}_{h}")
                        wait(v, f"bw_{tau}_{k}_{h}")
                        jb = 4 * (k % 2)
                        Hp = sb["H_t"] if k % 2 == 0 else sb["H1"]
                        scrCp = sb["scrC"] if k % 2 == 0 else sb["scrC1"]
                        for j in range(4):
                            n = 4 * h + j
                            li = v.tensor_tensor_scan(
                                out=csl(Hp, j), data0=csl(sb["scrB"], jb + j),
                                data1=csl(scrCp, j),
                                initial=sb["lastst"][:, k * NST + n:k * NST + n + 1],
                                op0=ALU.mult, op1=ALU.add)
                        inc(li, "sV")
                        mark(f"scans_{tau}_{k}_{h}", "sV")
                        hctree(v, tau, h, k)

                    if fp is not None and (h, k) in fp_at and fp_at[(h, k)] <= 4:
                        fp[fp_at[(h, k)]]()

            if fp is not None:
                for i in (5, 6, 7, 8):
                    fp[i]()
                for o in range(NC_DH):
                    emit_winevac(tau + 1, o)

            @sec("vector")
            def _(v, tau=tau):
                v.drain()
                for k in range(NC_DH):
                    v.scalar_tensor_tensor(out=csl(sb["y_t"], k), in0=csl(sb["scrA"], k),
                                           scalar=vcol("dpar", k), in1=csl(sb["y_t"], k),
                                           op0=ALU.mult, op1=ALU.add)
                v.drain()
                for k in range(NC_DH):
                    li = v.tensor_tensor(out=csl(sb["scrA"], k), in0=csl(sb["y_t"], k),
                                         in1=csl(sb["sz_t"], k), op=ALU.mult)
                inc(li, "sV")
                mark(f"g_{tau}", "sV")

            @sec("sync")
            def _(sy, tau=tau):
                wait(sy, f"g_{tau}")
                dma(sy, g_buf[tau:tau + 1, :].rearrange("o (p c t) -> o p c t",
                                                        p=128, c=NC_DH),
                    sb["scrA"][:].rearrange("p (c t) -> p c t", c=NC_DH), q="dOut")
                mark(f"gsp_{tau}", "dOut")

            if DBG and tau == 0:
                @sec("sync")
                def _(sy, tau=tau):
                    wait(sy, f"g_{tau}")
                    dma(sy, dbg, sb["delta_t"][:])
                    mark("dbgw", "dS")


        stage_front(0)
        for tau in range(NT):
            if tau >= 1:
                stage_scanblk(tau - 1)
            stage_win(tau)
            stage_tail(tau)
        stage_scanblk(NT - 1)
        actx.close()
        alloc("wout_sb", [128, NC_DH * DM])
        alloc("g_ld", [128, NC_DH * TT])
        alloc("pout_t", [128, NC_DM * TT])
        alloc("qi_t", [128, NC_DM * TT], I32)
        alloc("qf_t", [128, NC_DM * TT])
        alloc("qb_t", [128, NC_DM * TT], BF16)
        alloc("wf8", [128, NC_DM * DM], I8)
        alloc("wfb", [128, NC_DM * DM], BF16)
        alloc("c2scr", [128, NC_DM * TT])
        alloc("ps_ld", [128, NC_DM * TT])
        alloc("r0_ld", [128, NC_DM * TT])

        mark_led("barrierB")

        def wait_barrier(eng):
            wait_led(eng, "barrierB")

        @sec("sync")
        def _(sy):
            wait_barrier(sy)
            dma(sy, sb["wout_sb"][:].rearrange("p (c o) -> p c o", c=NC_DH),
                wout_t[:, :].rearrange("(c p) o -> p c o", p=128), q="dW")
            dma(sy, sb["wf8"][:].rearrange("p (c o) -> p c o", c=NC_DM),
                wffn_t[:, :].rearrange("(c p) o -> p c o", p=128), q="dW")
            mark("bweights", "dW")

        @sec("vector")
        def _(v):
            wait_barrier(v)
            wait(v, "bweights")
            inc(v.tensor_copy(sb["wfb"][:], sb["wf8"][:]), "sV")
            mark("wfb_done", "sV")

        @sec("tensor")
        def _(te):
            wait_barrier(te)
            te.sem_inc(sems["sP"], 1)
            bump("sP", 1)
            mark("te_bar", "sP")

        @sec("scalar")
        def _(sc):
            wait_barrier(sc)
            sc.sem_inc(sems["sA"], 1)
            bump("sA", 1)
            mark("sc_bar", "sA")

        @sec("gpsimd")
        def _(g):
            wait_barrier(g)
            g.sem_inc(sems["sG"], 1)
            bump("sG", 1)
            mark("g_bar", "sG")

        def stage_B(tau):
            @sec("sync")
            def _(sy, tau=tau):
                if tau == 0:
                    wait(sy, f"gsp_{NT - 1}")
                else:
                    wait(sy, f"poutmm_{tau - 1}")
                dma(sy, sb["g_ld"][:].rearrange("p (c t) -> p c t", c=NC_DH),
                    g_buf[tau:tau + 1, :].rearrange("o (p c t) -> o p c t",
                                                    p=128, c=NC_DH), q="dX")
                mark(f"gld_{tau}", "dX")

            for o in range(NC_DM):
                @sec("tensor")
                def _(te, tau=tau, o=o):
                    if o == 0:
                        wait(te, f"gld_{tau}")
                        wait(te, "bweights")
                    if o >= 2 or tau > 0:
                        oo, tt2 = o - 2, tau
                        if oo < 0:
                            oo += NC_DM
                            tt2 -= 1
                        wait(te, f"poutev_{tt2}_{oo}")
                    pst = ps["xz0"] if o % 2 == 0 else ps["xz1"]
                    for k in range(NC_DH):
                        li = te.matmul(pst[:], sb["wout_sb"][:, k * DM + o * 128:
                                                             k * DM + o * 128 + 128],
                                       csl(sb["g_ld"], k), start=(k == 0),
                                       stop=(k == NC_DH - 1))
                    inc(li, "sP")
                    mark(f"poutm_{tau}_{o}", "sP")
                    if o == NC_DM - 1:
                        mark(f"poutmm_{tau}", "sP")

                @sec("vector")
                def _(v, tau=tau, o=o):
                    wait(v, f"poutm_{tau}_{o}")
                    if tau > 0 and o == 0:
                        wait(v, f"post_{tau - 1}")
                    pst = ps["xz0"] if o % 2 == 0 else ps["xz1"]
                    inc(v.tensor_copy(csl(sb["pout_t"], o), pst[:]), "sV")
                    mark(f"poutev_{tau}_{o}", "sV")
                    if o == NC_DM - 1:
                        mark(f"poutevall_{tau}", "sV")

            @sec("gpsimd")
            def _(g, tau=tau):
                wait(g, f"poutevall_{tau}")
                dma(g, aro_i[tau:tau + 1, :].rearrange("o (p c t) -> o p c t",
                                                       p=128, c=NC_DM),
                    sb["pout_t"][:].rearrange("p (c t) -> p c t", c=NC_DM), q="dG")
                mark(f"post_{tau}", "dG")
                wait(g, f"post_{tau}")
                g.collective_compute(
                    "AllReduce", ALU.add,
                    replica_groups=[[0, 1], [2, 3], [4, 5], [6, 7]],
                    ins=[aro_i[tau:tau + 1, :].opt()],
                    outs=[aro_o[tau:tau + 1, :].opt()],
                ).then_inc(sems["sC"], 1)
                bump("sC", 1)
                mark(f"cco_{tau}", "sC")


        def stage_C1(tau):
            @sec("gpsimd")
            def _(g, tau=tau):
                wait(g, f"cco_{tau}")
                if tau > 0:
                    wait(g, f"x2c_{tau - 1}")
                    wait(g, f"mx_{tau - 1}")
                    wait(g, f"stats3_{tau - 1}")
                    wait(g, f"s3sq_{tau - 1}_{NC_DM - 1}")
                    wait(g, f"x2sp_{tau - 1}")
                dma(g, sb["ps_ld"][:].rearrange("p (c t) -> p c t", c=NC_DM),
                    aro_o[tau:tau + 1, :].rearrange("o (p c t) -> o p c t",
                                                    p=128, c=NC_DM), q="dG")
                mark(f"psld_{tau}", "dG")

            @sec("sync")
            def _(sy, tau=tau):
                if tau > 0:
                    wait(sy, f"mx_{tau - 1}")
                dma(sy, sb["r0_ld"][:].rearrange("p (c t) -> p c t", c=NC_DM),
                    r0_buf[tau:tau + 1, :].rearrange("o (p c t) -> o p c t",
                                                     p=128, c=NC_DM), q="dR")
                mark(f"r0ld_{tau}", "dR")

            @sec("vector")
            def _(v, tau=tau):
                wait(v, f"psld_{tau}")
                wait(v, f"r0ld_{tau}")
                v.drain()
                for c in range(NC_DM):
                    li = v.scalar_tensor_tensor(out=csl(sb["ps_ld"], c),
                                                in0=csl(sb["ps_ld"], c),
                                                scalar=vcol("gap", c), in1=csl(sb["r0_ld"], c),
                                                op0=ALU.mult, op1=ALU.add)
                inc(li, "sV")
                mark(f"x2c_{tau}", "sV")

            @sec("sync")
            def _(sy, tau=tau):
                wait(sy, f"x2c_{tau}")
                dma(sy, x2_buf[tau:tau + 1, :].rearrange("o (p c t) -> o p c t",
                                                         p=128, c=NC_DM),
                    sb["ps_ld"][:].rearrange("p (c t) -> p c t", c=NC_DM), q="dOut")
                mark(f"x2sp_{tau}", "dOut")

            def _scf3(sc, tau=tau):
                if tau == 0:
                    wait_barrier(sc)
                wait(sc, f"x2c_{tau}")

            def _tef3(te, tau=tau):
                wait(te, f"x2c_{tau}")
                if tau > 0:
                    wait(te, f"p3_{tau - 1}")

            stats_round("s3", tau, "ps_ld", sc_first=_scf3, te_first=_tef3)
            alias(f"stats3_{tau}", f"stats_s3_{tau}")

            @sec("vector")
            def _(v, tau=tau):
                wait(v, f"stats3_{tau}")
                if tau > 0:
                    wait(v, f"stsp_{tau - 1}")
                stats_post(v, tau, "ln3")

            @sec("scalar")
            def _(sc, tau=tau):
                wait(sc, f"vpre_ln3_{tau}")
                sc.activation(out=sb["stt2"][:], in_=sb["stt1"][:], func=AF.Ln)
                sc.drain()
                sc.activation(out=sb["r_a"][:], in_=sb["stt2"][:], func=AF.Exp,
                              scale=-0.5).then_inc(sems["sA"], 1)
                bump("sA", 1)
                mark(f"rstd_ln3_{tau}", "sA")

            @sec("vector")
            def _(v, tau=tau):
                wait(v, f"rstd_ln3_{tau}")
                newton(v, tau, "ln3")
            alias(f"p3_{tau}", f"post_ln3_{tau}")

            @sec("sync")
            def _(sy, tau=tau):
                wait(sy, f"p3_{tau}")
                dma(sy, st_buf[tau:tau + 1, 0:TT], sb["mu_a"][:], q="dSt")
                dma(sy, st_buf[tau:tau + 1, TT:2 * TT], sb["r_a"][:], q="dSt")
                mark(f"stsp_{tau}", "dSt")

            @sec("tensor")
            def _(te, tau=tau):
                wait(te, f"p3_{tau}")
                if tau > 0:
                    wait(te, f"mx_{tau - 1}")
                bcast(te, tau, "ln3")

            @sec("vector")
            def _(v, tau=tau):
                wait(v, f"bc_ln3_{tau}")
                rep_evac(v, tau, "ln3")
                wait(v, f"x2sp_{tau}")
                v.drain()
                for c in range(NC_DM):
                    v.tensor_tensor(out=csl(sb["r0_ld"], c), in0=csl(sb["ps_ld"], c),
                                    in1=sb["murep"][:], op=ALU.subtract)
                v.drain()
                for c in range(NC_DM):
                    v.tensor_tensor(out=csl(sb["r0_ld"], c), in0=csl(sb["r0_ld"], c),
                                    in1=sb["rrep"][:], op=ALU.mult)
                v.drain()
                for c in range(NC_DM):
                    v.tensor_scalar(out=csl(sb["r0_ld"], c), in0=csl(sb["r0_ld"], c),
                                    scalar1=vcol("cb1", c), scalar2=vcol("sbv", c),
                                    op0=ALU.mult, op1=ALU.add)
                v.drain()
                v.reduce_max(out=sb["rrep"][:, 0:1], in_=sb["r0_ld"][:], axis=AXX,
                             apply_absolute_value=True)
                v.drain()
                inc(v.tensor_tensor(out=sb["mxacc"][:], in0=sb["mxacc"][:],
                                    in1=sb["rrep"][:, 0:1], op=ALU.max), "sV")
                mark(f"mx_{tau}", "sV")


        for tau in range(NT + 1):
            if tau < NT:
                stage_B(tau)
            if tau >= 1:
                stage_C1(tau - 1)

        @sec("sync")
        def _(sy):
            wait(sy, f"mx_{NT - 1}")
            dma(sy, sb["mxrow"][0:1, 0:128].rearrange("o (a b) -> o a b", a=128),
                sb["mxacc"][:], q="dSt")
            mark("mxfl", "dSt")

        @sec("vector")
        def _(v):
            wait(v, "mxfl")
            v.drain()
            inc(v.reduce_max(out=sb["mxrow"][0:1, 128:129], in_=sb["mxrow"][0:1, 0:128],
                             axis=AXX, apply_absolute_value=False), "sV")
            mark("mxloc", "sV")

        @sec("gpsimd")
        def _(g):
            wait(g, "mxloc")
            dma(g, mx_i, sb["mxrow"][0:1, 128:129], q="dG")
            mark("mxst", "dG")
            wait(g, "mxst")
            g.collective_compute(
                "AllReduce", ALU.max,
                replica_groups=[list(range(8))],
                ins=[mx_i.opt()], outs=[mx_o.opt()],
            ).then_inc(sems["sC"], 1)
            bump("sC", 1)
            mark("ccmx", "sC")
            wait(g, "ccmx")
            dma(g, sb["mxrow"][0:1, 129:130], mx_o, q="dG")
            mark("mxg", "dG")

        @sec("vector")
        def _(v):
            wait(v, "mxg")
            v.drain()
            v.tensor_scalar(out=sb["mxrow"][0:1, 129:130], in0=sb["mxrow"][0:1, 129:130],
                            scalar1=1.0 / 127.0, scalar2=EPS, op0=ALU.mult, op1=ALU.max)
            v.drain()
            inc(v.reciprocal(sb["mxrow"][0:1, 128:129], sb["mxrow"][0:1, 129:130]), "sV")
            mark("xscv", "sV")

        @sec("tensor")
        def _(te):
            wait(te, "xscv")
            te.matmul(ps["bc"][:, 0:1], sb["onesr"][:], sb["mxrow"][0:1, 128:129],
                      start=True, stop=True)
            te.matmul(ps["bc"][:, 1:2], sb["onesr"][:], sb["mxrow"][0:1, 129:130],
                      start=True, stop=True).then_inc(sems["sP"], 1)
            bump("sP", 1)
            mark("xscb", "sP")

        @sec("vector")
        def _(v):
            wait(v, "xscb")
            V = sb["vecs"]
            sl = lambda nm: V[:, VOFF[nm]:VOFF[nm] + NC_DM]
            v.drain()
            v.tensor_copy(sb["mxacc"][:], ps["bc"][:, 0:1])
            v.tensor_copy(sb["ones1"][:], ps["bc"][:, 1:2])
            v.drain()
            v.tensor_scalar(out=sl("cb1"), in0=sl("cb1"), scalar1=sb["mxacc"][:],
                            scalar2=None, op0=ALU.mult)
            v.tensor_scalar(out=sl("sbv"), in0=sl("sbv"), scalar1=sb["mxacc"][:],
                            scalar2=None, op0=ALU.mult)
            inc(v.tensor_scalar(out=sl("gbw"), in0=sl("gbw"), scalar1=sb["ones1"][:],
                                scalar2=None, op0=ALU.mult), "sV")
            mark("c2const", "sV")

        for tau in range(NT):
            @sec("sync")
            def _(sy, tau=tau):
                if tau == 0:
                    wait(sy, "c2const")
                else:
                    wait(sy, f"x3_{tau - 1}")
                if tau == 0:
                    wait(sy, "mxfl")
                dma(sy, sb["pout_t"][:].rearrange("p (c t) -> p c t", c=NC_DM),
                    x2_buf[tau:tau + 1, :].rearrange("o (p c t) -> o p c t",
                                                     p=128, c=NC_DM), q="dX")
                dma(sy, sb["mu_a"][:], st_buf[tau:tau + 1, 0:TT], q="dX")
                dma(sy, sb["r_a"][:], st_buf[tau:tau + 1, TT:2 * TT], q="dX")
                mark(f"x2ld_{tau}", "dX")

            @sec("tensor")
            def _(te, tau=tau):
                wait(te, f"x2ld_{tau}")
                if tau == 0:
                    wait(te, "c2const")
                else:
                    wait(te, f"vq_{tau - 1}")
                bcast(te, tau, f"c2_{tau}")

            @sec("vector")
            def _(v, tau=tau):
                wait(v, f"bc_c2_{tau}_{tau}")
                rep_evac(v, tau, f"c2r_{tau}")
                wait(v, "c2const")
                if tau > 0:
                    wait(v, f"fmm_{tau - 1}_{NC_DM - 1}")
                    wait(v, f"x3sp_{tau - 1}")
                v.drain()
                for c in range(NC_DM):
                    v.tensor_tensor(out=csl(sb["g_ld"], c), in0=csl(sb["pout_t"], c),
                                    in1=sb["murep"][:], op=ALU.subtract)
                v.drain()
                for c in range(NC_DM):
                    v.tensor_tensor(out=csl(sb["g_ld"], c), in0=csl(sb["g_ld"], c),
                                    in1=sb["rrep"][:], op=ALU.mult)
                v.drain()
                for c in range(NC_DM):
                    v.tensor_scalar(out=csl(sb["g_ld"], c), in0=csl(sb["g_ld"], c),
                                    scalar1=vcol("cb1", c), scalar2=vcol("sbv", c),
                                    op0=ALU.mult, op1=ALU.add)
                v.drain()
                v.tensor_copy(sb["qi_t"][:], sb["g_ld"][:])
                v.drain()
                v.tensor_copy(sb["qf_t"][:], sb["qi_t"][:])
                v.drain()
                v.tensor_tensor(out=sb["c2scr"][:], in0=sb["g_ld"][:], in1=sb["qf_t"][:],
                                op=ALU.subtract)
                v.drain()
                v.scalar_tensor_tensor(out=sb["qf_t"][:], in0=sb["c2scr"][:], scalar=0.5,
                                       in1=sb["qf_t"][:], op0=ALU.is_ge, op1=ALU.add)
                v.drain()
                v.scalar_tensor_tensor(out=sb["qf_t"][:], in0=sb["c2scr"][:], scalar=-0.5,
                                       in1=sb["qf_t"][:], op0=ALU.is_ge, op1=ALU.add)
                v.drain()
                inc(v.tensor_scalar(out=sb["qb_t"][:], in0=sb["qf_t"][:],
                                    scalar1=-1.0, scalar2=127.0, op0=ALU.add,
                                    op1=ALU.min), "sV")
                mark(f"vq_{tau}", "sV")

            for o in range(NC_DM):
                @sec("tensor")
                def _(te, tau=tau, o=o):
                    if o == 0:
                        wait(te, f"vq_{tau}")
                        if tau == 0:
                            wait(te, "wfb_done")
                    if o >= 2 or tau > 0:
                        oo, tt2 = o - 2, tau
                        if oo < 0:
                            oo += NC_DM
                            tt2 -= 1
                        wait(te, f"fev_{tt2}_{oo}")
                    pst = ps["xz0"] if o % 2 == 0 else ps["xz1"]
                    for k in range(NC_DM):
                        li = te.matmul(pst[:], sb["wfb"][:, k * DM + o * 128:
                                                         k * DM + o * 128 + 128],
                                       csl(sb["qb_t"], k), start=(k == 0),
                                       stop=(k == NC_DM - 1))
                    inc(li, "sP")
                    mark(f"fmm_{tau}_{o}", "sP")

                @sec("vector")
                def _(v, tau=tau, o=o):
                    wait(v, f"fmm_{tau}_{o}")
                    if tau > 0 and o == 0:
                        wait(v, f"x3sp_{tau - 1}")
                    pst = ps["xz0"] if o % 2 == 0 else ps["xz1"]
                    v.drain()
                    v.tensor_scalar(out=csl(sb["qf_t"], o), in0=pst[:],
                                    scalar1=vcol("gbw", o), scalar2=vcol("gbb", o),
                                    op0=ALU.mult, op1=ALU.add)
                    v.drain()
                    inc(v.tensor_tensor(out=csl(sb["qf_t"], o), in0=csl(sb["qf_t"], o),
                                        in1=csl(sb["pout_t"], o), op=ALU.add), "sV")
                    mark(f"fev_{tau}_{o}", "sV")
                    if o == NC_DM - 1:
                        mark(f"x3_{tau}", "sV")

            @sec("sync")
            def _(sy, tau=tau):
                wait(sy, f"x3_{tau}")
                dma(sy, xout[:, tsl(tau)].rearrange("(c p) t -> p c t", p=128),
                    sb["qf_t"][:].rearrange("p (c t) -> p c t", c=NC_DM), q="dOut")
                mark(f"x3sp_{tau}", "dOut")

        mark_led("final")

        @sec("sync")
        def _(sy):
            wait_led(sy, "final", only={"dW", "dX", "dR", "dBC", "dBC1", "dOut",
                                        "dMsc", "dSt", "dM0", "dM1", "dS"})

        @sec("gpsimd")
        def _(g):
            wait_led(g, "final", only={"dG"})

        if pass2:
            for engname in ["sync", "tensor", "vector", "scalar", "gpsimd"]:
                fns = thunks[engname]

                @getattr(block, engname)
                def _(eng, fns=fns):
                    for fn in fns:
                        fn(eng)

    return nc, ev


def kernel(L0_attn_x, L0_attn_ln_gamma, L0_attn_ln_beta, L0_attn_W_in,
           L0_attn_conv_w, L0_attn_conv_b, L0_attn_W_xproj, L0_attn_W_dt,
           L0_attn_b_dt, L0_attn_A_log, L0_attn_D_param, L0_attn_W_out,
           L0_ffn_weight_int8, L0_ffn_weight_scale, L0_ffn_bias, L0_c, L0_wmod):
    _, ev1 = build_program()
    nc, _ = build_program(ev1)

    def vec_pack(v):
        return np.ascontiguousarray(np.asarray(v, np.float32).reshape(NC_DM, 128).T)

    x = np.asarray(L0_attn_x, np.float32)
    W_in = np.asarray(L0_attn_W_in, np.float32)
    conv_w = np.asarray(L0_attn_conv_w, np.float32)
    conv_b = np.asarray(L0_attn_conv_b, np.float32)
    W_xp = np.asarray(L0_attn_W_xproj, np.float32)
    W_dt = np.asarray(L0_attn_W_dt, np.float32)
    b_dt = np.asarray(L0_attn_b_dt, np.float32)
    A_log = np.asarray(L0_attn_A_log, np.float32)
    D_par = np.asarray(L0_attn_D_param, np.float32)
    W_out = np.asarray(L0_attn_W_out, np.float32)
    w8 = np.asarray(L0_ffn_weight_int8, np.int8)
    wscale = np.asarray(L0_ffn_weight_scale, np.float32)
    fb = np.asarray(L0_ffn_bias, np.float32)
    cc = np.asarray(L0_c, np.float32)
    wmod = np.ascontiguousarray(np.asarray(L0_wmod, np.float32))

    in_maps = []
    for core in range(8):
        b = core // 2
        hh = core % 2
        dsl = slice(hh * DH, (hh + 1) * DH)
        w_in_sl = np.concatenate([W_in[hh * DH:(hh + 1) * DH],
                                  W_in[2048 + hh * DH:2048 + (hh + 1) * DH]], 0)
        cw = conv_w[dsl, 0, :]
        m = dict(
            xin=np.ascontiguousarray(x[b].T),
            w_in_t=np.ascontiguousarray(w_in_sl.T),
            wxp_t=np.ascontiguousarray(W_xp[:, dsl].T),
            wdt_t=np.ascontiguousarray(W_dt[dsl].T),
            wout_t=np.ascontiguousarray(W_out[:, dsl].T),
            wmod=wmod,
            wffn_t=np.ascontiguousarray(w8.T),
            convw=np.ascontiguousarray(
                cw.reshape(NC_DH, 128, 4).transpose(1, 0, 2).reshape(128, NC_DH * 4)),
            convb=vec_pack(conv_b[dsl]),
            bdt=vec_pack(b_dt[dsl]),
            alog=np.ascontiguousarray(
                A_log[dsl].reshape(NC_DH, 128, NST).transpose(1, 0, 2)
                .reshape(128, NC_DH * NST)),
            dpar=vec_pack(D_par[dsl]),
            lngam=vec_pack(np.asarray(L0_attn_ln_gamma, np.float32)),
            lnbet=vec_pack(np.asarray(L0_attn_ln_beta, np.float32)),
            wsc=vec_pack(wscale[:, 0]),
            fbias=vec_pack(fb),
            cvec=vec_pack(cc[b]),
        )
        in_maps.append(m)

    res = run_bass_kernel_spmd(nc, in_maps, core_ids=list(range(8)))
    out = np.empty((4, S, DM), np.float32)
    for b in range(4):
        out[b] = res.results[2 * b]["xout"].T
    return out


# version 16
# speedup vs baseline: 1.2067x; 1.1246x over previous
import numpy as np
import concourse.bass as bass
import concourse.mybir as mybir
from concourse.bass import AP
from concourse.bass_utils import run_bass_kernel_spmd

F32 = mybir.dt.float32
BF16 = mybir.dt.bfloat16
I8 = mybir.dt.int8
I32 = mybir.dt.int32
AF = mybir.ActivationFunctionType
ALU = mybir.AluOpType
AXX = mybir.AxisListType.X

DM = 1024
DH = 1024
NC_DM = DM // 128
NC_DH = DH // 128
NST = 16
S = 8192
TT = 256
NT = S // TT
EPS = 1e-5

DBG = False


class _DummyInstr:
    def then_inc(self, *a, **k):
        return self


class _DummyEngine:
    def __getattr__(self, name):
        def f(*a, **k):
            return _DummyInstr()
        return f


def build_program(ev_table=None):
    pass2 = ev_table is not None
    nc = bass.Bass("TRN2", target_bir_lowering=False)

    xin = nc.dram_tensor("xin", [DM, S], F32, kind="ExternalInput").ap()
    w_in_t = nc.dram_tensor("w_in_t", [DM, 2 * DH], F32, kind="ExternalInput").ap()
    wxp_t = nc.dram_tensor("wxp_t", [DH, 160], F32, kind="ExternalInput").ap()
    wdt_t = nc.dram_tensor("wdt_t", [128, DH], F32, kind="ExternalInput").ap()
    wout_t = nc.dram_tensor("wout_t", [DH, DM], F32, kind="ExternalInput").ap()
    wmod = nc.dram_tensor("wmod", [DM, 6 * DM], F32, kind="ExternalInput").ap()
    wffn_t = nc.dram_tensor("wffn_t", [DM, DM], I8, kind="ExternalInput").ap()
    convw = nc.dram_tensor("convw", [128, NC_DH * 4], F32, kind="ExternalInput").ap()
    convb = nc.dram_tensor("convb", [128, NC_DH], F32, kind="ExternalInput").ap()
    bdt = nc.dram_tensor("bdt", [128, NC_DH], F32, kind="ExternalInput").ap()
    alog = nc.dram_tensor("alog", [128, NC_DH * NST], F32, kind="ExternalInput").ap()
    dpar = nc.dram_tensor("dpar", [128, NC_DH], F32, kind="ExternalInput").ap()
    lngam = nc.dram_tensor("lngam", [128, NC_DM], F32, kind="ExternalInput").ap()
    lnbet = nc.dram_tensor("lnbet", [128, NC_DM], F32, kind="ExternalInput").ap()
    wsc = nc.dram_tensor("wsc", [128, NC_DM], F32, kind="ExternalInput").ap()
    fbias = nc.dram_tensor("fbias", [128, NC_DM], F32, kind="ExternalInput").ap()
    cvec = nc.dram_tensor("cvec", [128, NC_DM], F32, kind="ExternalInput").ap()

    xout = nc.dram_tensor("xout", [DM, S], F32, kind="ExternalOutput").ap()
    dbg = nc.dram_tensor("dbg", [128, 8 * TT], F32, kind="ExternalOutput").ap() if DBG else None

    g_buf = nc.dram_tensor("g_buf", [NT, DH * TT], F32).ap()
    r0_buf = nc.dram_tensor("r0_buf", [NT, DM * TT], F32).ap()
    x2_buf = nc.dram_tensor("x2_buf", [NT, DM * TT], F32).ap()
    arp_i = nc.dram_tensor("arp_i", [NT, 160 * TT], F32).ap()
    arp_o = nc.dram_tensor("arp_o", [NT, 160 * TT], F32).ap()
    aro_i = nc.dram_tensor("aro_i", [NT, DM * TT], F32).ap()
    aro_o = nc.dram_tensor("aro_o", [NT, DM * TT], F32).ap()
    mx_i = nc.dram_tensor("mx_i", [1, 1], F32).ap()
    mx_o = nc.dram_tensor("mx_o", [1, 1], F32).ap()
    st_buf = nc.dram_tensor("st_buf", [NT, 2 * TT], F32).ap()

    from contextlib import ExitStack

    led = {}
    ev = {}

    def bump(sem_name, amt):
        if pass2:
            return 0
        led[sem_name] = led.get(sem_name, 0) + amt
        return led[sem_name]

    def mark(name, sem_name):
        if pass2:
            return
        ev[name] = (sem_name, led.get(sem_name, 0))

    def alias(dst, src):
        if pass2:
            return
        ev[dst] = ev[src]

    def mark_led(name):
        if pass2:
            return
        ev[name] = dict(led)

    def wait(eng, name):
        tbl = ev_table or ev
        if not pass2 and name not in tbl:
            return
        s, v = tbl[name]
        if v > 0:
            eng.wait_ge(sems[s], v)

    def wait_led(eng, name, only=None):
        for snm, val in (ev_table or ev)[name].items():
            if val > 0 and (only is None or snm in only):
                eng.wait_ge(sems[snm], val)

    sems = {}


    with ExitStack() as ctx:
        for sname in ["dS", "dG", "dM0", "dM1", "dW", "dX", "dR", "dBC", "dBC1",
                      "dOut", "dMsc", "dSt", "sP", "sV", "sA", "sG", "sC"]:
            sems[sname] = ctx.enter_context(nc.semaphore(sname))

        def inc(instr, sname):
            instr.then_inc(sems[sname], 1)
            return bump(sname, 1)

        def dma(eng, out, in_, q="dS"):
            eng.dma_start(out=out, in_=in_).then_inc(sems[q], 16)
            return bump(q, 16)

        sb = {}
        actx = ctx.enter_context(ExitStack())

        def alloc(name, shape, dtype=F32, scope=None):
            tgt = scope if scope is not None else ctx
            sb[name] = tgt.enter_context(nc.sbuf_tensor(name, shape, dtype))
            return sb[name]

        vecs = alloc("vecs", [128, 16 * NC_DM])
        aneg = alloc("aneg", [128, NC_DH * NST])
        ones1 = alloc("ones1", [128, 1])
        onesr = alloc("onesr", [1, 128])
        lastst = alloc("lastst", [128, NC_DH * NST])

        VOFF = {}
        for i, nm in enumerate(["convb", "bdt", "dpar", "lngam", "lnbet", "wsc",
                                "fbias", "sa", "ca1", "gap", "sbv", "cb1", "gbp",
                                "gbw", "gbb", "cvec"]):
            VOFF[nm] = i * NC_DM

        def vcol(nm, c):
            return sb["vecs"][:, VOFF[nm] + c:VOFF[nm] + c + 1]

        mu_a = alloc("mu_a", [1, TT])
        r_a = alloc("r_a", [1, TT])
        stt1 = alloc("stt1", [1, TT])
        stt2 = alloc("stt2", [1, TT])
        murep = alloc("murep", [128, TT])
        rrep = alloc("rrep", [128, TT])
        mxacc = alloc("mxacc", [128, 1])
        mxrow = alloc("mxrow", [1, 130])
        xsq = alloc("xsq", [128, 2 * TT])

        alloc("w_in_sb", [128, NC_DM * 2 * DH], scope=actx)
        alloc("wxp_sb", [128, NC_DH * 160], scope=actx)
        alloc("wdt_sb", [128, DH], scope=actx)
        alloc("convw_sb", [128, NC_DH * 4], scope=actx)
        alloc("x_t", [128, NC_DM * TT], scope=actx)
        alloc("h1_t", [128, NC_DM * TT], scope=actx)
        alloc("r0_t", [128, NC_DM * TT], scope=actx)
        sb["xn_t"] = sb["r0_t"]
        alloc("xs_t", [128, NC_DH * (TT + 3)], scope=actx)
        alloc("scrA", [128, NC_DH * TT], scope=actx)
        alloc("scrB", [128, 8 * TT], scope=actx)
        alloc("scrC", [128, 4 * TT], scope=actx)
        alloc("delta_t", [128, NC_DH * TT], scope=actx)
        alloc("du_t", [128, NC_DH * TT], scope=actx)
        alloc("HC0", [128, 4 * TT], scope=actx)
        alloc("HC1", [128, 4 * TT], scope=actx)
        alloc("sz_t", [128, NC_DH * TT], scope=actx)
        alloc("H_t", [128, 4 * TT], scope=actx)
        alloc("H1", [128, 4 * TT], scope=actx)
        alloc("scrC1", [128, 4 * TT], scope=actx)
        alloc("y_t", [128, NC_DH * TT], scope=actx)
        alloc("pdt_t", [128, TT], scope=actx)
        alloc("pbc_t", [32, TT], scope=actx)
        alloc("dt_sb", [128, TT], scope=actx)
        alloc("bc_sb", [32, TT], scope=actx)
        s0ctx = actx.enter_context(ExitStack())
        alloc("cs_t", [128, NC_DM], scope=s0ctx)
        alloc("modrow", [1, 1024], scope=s0ctx)
        alloc("wm0", [128, 512], scope=s0ctx)
        alloc("wm1", [128, 512], scope=s0ctx)

        ps = {}
        for nm, shape in [("smu", [1, TT]), ("sm2", [1, TT]), ("bc", [128, 2 * TT]),
                          ("xz0", [128, TT]), ("xz1", [128, TT]),
                          ("pdt", [128, TT]), ("pbc", [32, TT])]:
            ps[nm] = ctx.enter_context(nc.psum_tensor("ps_" + nm, shape, F32))
        ps["mu"] = ps["smu"][0:1, :]
        ps["m2"] = ps["sm2"][0:1, :]
        ps["mod"] = ps["bc"][0:1, 0:512]
        ps["dl0"] = ps["xz0"]
        ps["dl1"] = ps["xz1"]

        block = ctx.enter_context(nc.Block())
        thunks = {k: [] for k in ["sync", "tensor", "vector", "scalar", "gpsimd"]}
        _dummy = _DummyEngine()

        def sec(engname):
            if not pass2:
                def dec(fn):
                    fn(_dummy)
                return dec

            def dec(fn):
                thunks[engname].append(fn)
            return dec

        @sec("sync")
        def _(sy):
            dma(sy, sb["w_in_sb"][:].rearrange("p (c o) -> p c o", c=NC_DM),
                w_in_t[:, :].rearrange("(c p) o -> p c o", p=128), q="dW")
            dma(sy, sb["wxp_sb"][:].rearrange("p (c o) -> p c o", c=NC_DH),
                wxp_t[:, :].rearrange("(c p) o -> p c o", p=128), q="dW")
            dma(sy, sb["wdt_sb"][:], wdt_t, q="dW")
            dma(sy, sb["convw_sb"][:], convw, q="dW")
            dma(sy, sb["aneg"][:], alog, q="dW")
            for nm, src in [("convb", convb), ("bdt", bdt), ("dpar", dpar),
                            ("lngam", lngam), ("lnbet", lnbet), ("wsc", wsc),
                            ("fbias", fbias), ("cvec", cvec)]:
                dma(sy, sb["vecs"][:, VOFF[nm]:VOFF[nm] + NC_DM], src, q="dW")
            mark("weights", "dW")

        @sec("vector")
        def _(v):
            v.memset(sb["ones1"][:], 1.0)
            v.memset(sb["onesr"][:], 1.0)
            v.memset(sb["lastst"][:], 0.0)
            wait(v, "weights")
            inc(v.memset(sb["mxacc"][:], 0.0), "sV")
            mark("v_setup", "sV")

        @sec("scalar")
        def _(sc):
            wait(sc, "weights")
            sc.activation(out=sb["cs_t"][:], in_=sb["vecs"][:, VOFF["cvec"]:VOFF["cvec"] + NC_DM],
                          func=AF.Silu)
            inc(sc.activation(out=sb["aneg"][:], in_=sb["aneg"][:], func=AF.Exp), "sA")
            mark("a_setup", "sA")

        @sec("vector")
        def _(v):
            wait(v, "a_setup")
            inc(v.tensor_scalar(out=sb["aneg"][:], in0=sb["aneg"][:], scalar1=-1.0,
                                scalar2=None, op0=ALU.mult), "sV")
            mark("aneg_done", "sV")

        NJ = 6 * DM // 512
        QNAMES = ["sa", "ca1", "gap", "sbv", "cb1", "gbp"]
        for j in range(NJ):
            for k in range(NC_DM):
                idx = j * NC_DM + k

                @sec("sync")
                def _(sy, j=j, k=k, idx=idx):
                    if idx == 0:
                        wait(sy, "a_setup")
                    if idx >= 2:
                        wait(sy, f"modmm_{idx - 2}")
                    wm = [sb["wm0"], sb["wm1"]][idx % 2]
                    dma(sy, wm[:], wmod[k * 128:(k + 1) * 128, j * 512:(j + 1) * 512],
                        q="dM%d" % (idx % 2))
                    mark(f"modld_{idx}", "dM%d" % (idx % 2))

                @sec("tensor")
                def _(te, j=j, k=k, idx=idx):
                    wm = [sb["wm0"], sb["wm1"]][idx % 2]
                    wait(te, f"modld_{idx}")
                    if k == 0 and j > 0:
                        wait(te, f"modev_{j - 1}")
                    te.matmul(ps["mod"], sb["cs_t"][:, k:k + 1], wm[:],
                              start=(k == 0), stop=(k == NC_DM - 1)).then_inc(sems["sP"], 1)
                    bump("sP", 1)
                    mark(f"modmm_{idx}", "sP")
                    if k == NC_DM - 1:
                        mark(f"modcol_{j}", "sP")

            @sec("vector")
            def _(v, j=j):
                wait(v, f"modcol_{j}")
                if j >= 2:
                    wait(v, f"modsc_{(j - 2) // 2}")
                inc(v.tensor_copy(sb["modrow"][0:1, (j % 2) * 512:(j % 2) * 512 + 512],
                                  ps["mod"]), "sV")
                mark(f"modev_{j}", "sV")

            if j % 2 == 1:
                q = j // 2

                @sec("sync")
                def _(sy, q=q, j=j):
                    wait(sy, f"modev_{j}")
                    for jj in range(NC_DM):
                        sy.dma_start(out=sb["vecs"][:, VOFF[QNAMES[q]] + jj:VOFF[QNAMES[q]] + jj + 1],
                                     in_=sb["modrow"][0:1, jj * 128:(jj + 1) * 128],
                                     allow_slow_non_contiguous=True).then_inc(sems["dMsc"], 16)
                        bump("dMsc", 16)
                    mark(f"modsc_{q}", "dMsc")
        mark("mod_scattered", "dMsc")

        @sec("vector")
        def _(v):
            wait(v, "mod_scattered")
            V = sb["vecs"]
            sl = lambda nm: V[:, VOFF[nm]:VOFF[nm] + NC_DM]
            v.tensor_scalar(out=sl("ca1"), in0=sl("ca1"), scalar1=1.0, scalar2=None, op0=ALU.add)
            v.tensor_scalar(out=sl("cb1"), in0=sl("cb1"), scalar1=1.0, scalar2=None, op0=ALU.add)
            for nm in ["gap", "gbp"]:
                g = sl(nm)
                v.drain()
                v.tensor_scalar(out=sb["murep"][:, 0:NC_DM], in0=g, scalar1=0.0,
                                scalar2=None, op0=ALU.is_ge)
                v.drain()
                v.tensor_scalar(out=sb["murep"][:, 0:NC_DM], in0=sb["murep"][:, 0:NC_DM],
                                scalar1=2.0, scalar2=-1.0, op0=ALU.mult, op1=ALU.add)
                v.tensor_scalar(out=sb["rrep"][:, 0:NC_DM], in0=g, scalar1=-1.0,
                                scalar2=None, op0=ALU.mult)
                v.drain()
                v.tensor_tensor(out=sb["rrep"][:, 0:NC_DM], in0=g, in1=sb["rrep"][:, 0:NC_DM],
                                op=ALU.max)
                v.drain()
                v.tensor_scalar(out=sb["rrep"][:, 0:NC_DM], in0=sb["rrep"][:, 0:NC_DM],
                                scalar1=0.25, scalar2=None, op0=ALU.max)
                v.drain()
                v.tensor_tensor(out=g, in0=sb["murep"][:, 0:NC_DM],
                                in1=sb["rrep"][:, 0:NC_DM], op=ALU.mult)
            v.drain()
            v.tensor_tensor(out=sl("gbw"), in0=sl("gbp"), in1=sl("wsc"), op=ALU.mult)
            inc(v.tensor_tensor(out=sl("gbb"), in0=sl("gbp"), in1=sl("fbias"),
                                op=ALU.mult), "sV")
            mark("mod_done", "sV")

        s0ctx.close()
        alloc("bcrep0", [128, 2 * 4 * TT], scope=actx)
        alloc("bcrep1", [128, 2 * 4 * TT], scope=actx)
        mark_led("bar0")

        def wait_bar0(eng):
            wait_led(eng, "bar0")

        def tsl(tau):
            return slice(tau * TT, (tau + 1) * TT)

        def csl(tile, c, width=TT):
            return tile[:, c * width:(c + 1) * width]

        def stats_round(tag, tau, src_name, sc_first=None, te_first=None):
            for c in range(NC_DM):
                @sec("scalar")
                def _(sc, c=c):
                    if c == 0 and sc_first is not None:
                        sc_first(sc)
                    if c >= 2:
                        wait(sc, f"{tag}m2_{tau}_{c - 2}")
                    sc.activation(out=csl(sb["xsq"], c % 2), in_=csl(sb[src_name], c),
                                  func=AF.Square).then_inc(sems["sA"], 1)
                    bump("sA", 1)
                    mark(f"{tag}sq_{tau}_{c}", "sA")

                @sec("tensor")
                def _(te, c=c):
                    if c == 0 and te_first is not None:
                        te_first(te)
                    te.matmul(ps["mu"], sb["ones1"][:], csl(sb[src_name], c),
                              start=(c == 0), stop=(c == NC_DM - 1)).then_inc(sems["sP"], 1)
                    bump("sP", 1)
                    wait(te, f"{tag}sq_{tau}_{c}")
                    te.matmul(ps["m2"], sb["ones1"][:], csl(sb["xsq"], c % 2),
                              start=(c == 0), stop=(c == NC_DM - 1)).then_inc(sems["sP"], 1)
                    bump("sP", 1)
                    mark(f"{tag}m2_{tau}_{c}", "sP")
            mark(f"stats_{tag}_{tau}", "sP")

        def stats_post(v, tau, tag):
            v.drain()
            v.tensor_scalar(out=sb["mu_a"][:], in0=ps["mu"], scalar1=1.0 / DM,
                            scalar2=None, op0=ALU.mult)
            v.tensor_scalar(out=sb["stt1"][:], in0=ps["m2"], scalar1=1.0 / DM,
                            scalar2=None, op0=ALU.mult)
            v.drain()
            v.tensor_tensor(out=sb["stt2"][:], in0=sb["mu_a"][:], in1=sb["mu_a"][:],
                            op=ALU.mult)
            v.drain()
            v.tensor_tensor(out=sb["stt1"][:], in0=sb["stt1"][:], in1=sb["stt2"][:],
                            op=ALU.subtract)
            v.drain()
            inc(v.tensor_scalar(out=sb["stt1"][:], in0=sb["stt1"][:], scalar1=EPS,
                                scalar2=None, op0=ALU.add), "sV")
            mark(f"vpre_{tag}_{tau}", "sV")


        def newton(v, tau, tag):
            alias(f"post_{tag}_{tau}", f"rstd_{tag}_{tau}")


        def bcast(te, tau, tag):
            te.matmul(ps["bc"][:, 0:TT], sb["onesr"][:], sb["mu_a"][:],
                      start=True, stop=True).then_inc(sems["sP"], 1)
            bump("sP", 1)
            te.matmul(ps["bc"][:, TT:2 * TT], sb["onesr"][:], sb["r_a"][:],
                      start=True, stop=True).then_inc(sems["sP"], 1)
            bump("sP", 1)
            mark(f"bc_{tag}_{tau}", "sP")


        def rep_evac(v, tau, tag):
            v.drain()
            v.tensor_copy(sb["murep"][:], ps["bc"][:, 0:TT])
            inc(v.tensor_copy(sb["rrep"][:], ps["bc"][:, TT:2 * TT]), "sV")
            mark(f"rep_{tag}_{tau}", "sV")


        def stage_front(tau):
            @sec("sync")
            def _(sy, tau=tau):
                if tau == 0:
                    wait_bar0(sy)
                else:
                    wait(sy, f"r0c_{tau - 1}")
                dma(sy, sb["x_t"][:].rearrange("p (c t) -> p c t", c=NC_DM),
                    xin[:, tsl(tau)].rearrange("(c p) t -> p c t", p=128), q="dX")
                mark(f"xld_{tau}", "dX")

            def _scf1(sc, tau=tau):
                wait(sc, f"xld_{tau}")
                if tau == 0:
                    wait_bar0(sc)

            def _tef1(te, tau=tau):
                if tau == 0:
                    wait_bar0(te)
                else:
                    wait(te, f"post2_{tau - 1}")
                wait(te, f"xld_{tau}")

            stats_round("s1", tau, "x_t", sc_first=_scf1, te_first=_tef1)
            alias(f"stats1_{tau}", f"stats_s1_{tau}")

            @sec("vector")
            def _(v, tau=tau):
                wait(v, f"stats1_{tau}")
                stats_post(v, tau, "ln1")

            @sec("scalar")
            def _(sc, tau=tau):
                wait(sc, f"vpre_ln1_{tau}")
                sc.activation(out=sb["stt2"][:], in_=sb["stt1"][:], func=AF.Ln)
                sc.drain()
                sc.activation(out=sb["r_a"][:], in_=sb["stt2"][:], func=AF.Exp,
                              scale=-0.5).then_inc(sems["sA"], 1)
                bump("sA", 1)
                mark(f"rstd_ln1_{tau}", "sA")

            @sec("vector")
            def _(v, tau=tau):
                wait(v, f"rstd_ln1_{tau}")
                newton(v, tau, "ln1")
            alias(f"post1_{tau}", f"post_ln1_{tau}")

            @sec("tensor")
            def _(te, tau=tau):
                wait(te, f"post1_{tau}")
                if tau > 0:
                    wait(te, f"rep2_{tau - 1}")
                bcast(te, tau, "ln1")

            @sec("vector")
            def _(v, tau=tau):
                wait(v, f"bc_ln1_{tau}")
                rep_evac(v, tau, "ln1")
            alias(f"rep1_{tau}", f"rep_ln1_{tau}")

            @sec("vector")
            def _(v, tau=tau):
                V = sb["vecs"]
                if tau > 0:
                    wait(v, f"r0sp_{tau - 1}")
                v.drain()
                for c in range(NC_DM):
                    v.tensor_tensor(out=csl(sb["xn_t"], c), in0=csl(sb["x_t"], c),
                                    in1=sb["murep"][:], op=ALU.subtract)
                v.drain()
                for c in range(NC_DM):
                    v.tensor_tensor(out=csl(sb["xn_t"], c), in0=csl(sb["xn_t"], c),
                                    in1=sb["rrep"][:], op=ALU.mult)
                if tau > 0:
                    wait(v, f"xzmm_{tau - 1}_15")
                v.drain()
                for c in range(NC_DM):
                    v.tensor_scalar(out=csl(sb["h1_t"], c), in0=csl(sb["xn_t"], c),
                                    scalar1=vcol("ca1", c), scalar2=vcol("sa", c),
                                    op0=ALU.mult, op1=ALU.add)
                v.drain()
                for c in range(NC_DM):
                    li = v.scalar_tensor_tensor(out=csl(sb["r0_t"], c), in0=csl(sb["h1_t"], c),
                                                scalar=vcol("gap", c), in1=csl(sb["x_t"], c),
                                                op0=ALU.mult, op1=ALU.add)
                inc(li, "sV")
                mark(f"r0c_{tau}", "sV")

            @sec("sync")
            def _(sy, tau=tau):
                wait(sy, f"r0c_{tau}")
                dma(sy, r0_buf[tau:tau + 1, :].rearrange("o (p c t) -> o p c t", p=128, c=NC_DM),
                    sb["r0_t"][:].rearrange("p (c t) -> p c t", c=NC_DM), q="dR")
                mark(f"r0sp_{tau}", "dR")

            def _scf2(sc, tau=tau):
                wait(sc, f"r0c_{tau}")

            def _tef2(te, tau=tau):
                wait(te, f"post1_{tau}")
                wait(te, f"r0c_{tau}")

            stats_round("s2", tau, "h1_t", sc_first=_scf2, te_first=_tef2)
            alias(f"stats2_{tau}", f"stats_s2_{tau}")

            @sec("vector")
            def _(v, tau=tau):
                wait(v, f"stats2_{tau}")
                stats_post(v, tau, "ln2i")

            @sec("scalar")
            def _(sc, tau=tau):
                wait(sc, f"vpre_ln2i_{tau}")
                sc.activation(out=sb["stt2"][:], in_=sb["stt1"][:], func=AF.Ln)
                sc.drain()
                sc.activation(out=sb["r_a"][:], in_=sb["stt2"][:], func=AF.Exp,
                              scale=-0.5).then_inc(sems["sA"], 1)
                bump("sA", 1)
                mark(f"rstd_ln2i_{tau}", "sA")

            @sec("vector")
            def _(v, tau=tau):
                wait(v, f"rstd_ln2i_{tau}")
                newton(v, tau, "ln2i")
            alias(f"post2_{tau}", f"post_ln2i_{tau}")

            @sec("tensor")
            def _(te, tau=tau):
                wait(te, f"post2_{tau}")
                wait(te, f"rep1_{tau}")
                bcast(te, tau, "ln2i")

            @sec("vector")
            def _(v, tau=tau):
                wait(v, f"bc_ln2i_{tau}")
                rep_evac(v, tau, "ln2i")
            alias(f"rep2_{tau}", f"rep_ln2i_{tau}")

            @sec("vector")
            def _(v, tau=tau):
                v.drain()
                for c in range(NC_DM):
                    v.tensor_tensor(out=csl(sb["h1_t"], c), in0=csl(sb["h1_t"], c),
                                    in1=sb["murep"][:], op=ALU.subtract)
                v.drain()
                for c in range(NC_DM):
                    v.tensor_tensor(out=csl(sb["h1_t"], c), in0=csl(sb["h1_t"], c),
                                    in1=sb["rrep"][:], op=ALU.mult)
                v.drain()
                for c in range(NC_DM):
                    li = v.tensor_scalar(out=csl(sb["h1_t"], c), in0=csl(sb["h1_t"], c),
                                         scalar1=vcol("lngam", c), scalar2=vcol("lnbet", c),
                                         op0=ALU.mult, op1=ALU.add)
                inc(li, "sV")
                mark(f"xn2_{tau}", "sV")

            @sec("vector")
            def _(v, tau=tau):
                if tau > 0:
                    wait(v, f"conv_{tau - 1}")
                    src = AP(sb["xs_t"][:].tensor, TT,
                             [[NC_DH * (TT + 3), 128], [TT + 3, NC_DH], [1, 3]])
                    dst = AP(sb["xs_t"][:].tensor, 0,
                             [[NC_DH * (TT + 3), 128], [TT + 3, NC_DH], [1, 3]])
                    inc(v.tensor_copy(dst, src), "sV")
                else:
                    inc(v.memset(AP(sb["xs_t"][:].tensor, 0,
                                    [[NC_DH * (TT + 3), 128], [TT + 3, NC_DH], [1, 3]]),
                                 0.0), "sV")
                mark(f"ctx_{tau}", "sV")


        def emit_winevac(tau, o):
            @sec("scalar")
            def _(sc, tau=tau, o=o):
                wait(sc, f"xzmm_{tau}_{o}")
                pst = ps["xz0"] if o % 2 == 0 else ps["xz1"]
                if o == 0:
                    wait(sc, f"ctx_{tau}")
                li = sc.copy(sb["xs_t"][:, o * (TT + 3) + 3:(o + 1) * (TT + 3)],
                             pst[:])
                inc(li, "sA")
                mark(f"xzev_{tau}_{o}", "sA")
                if o == 7:
                    mark(f"xsv_{tau}", "sA")

        def stage_win(tau):
            for o in range(16):
                @sec("tensor")
                def _(te, tau=tau, o=o):
                    if o == 0:
                        wait(te, f"xn2_{tau}")
                    if o < 2 and tau > 0:
                        wait(te, f"sp_{tau - 1}_{o + 6}")
                    if o >= 2 or tau > 0:
                        oo, tt2 = o - 2, tau
                        if oo < 0:
                            oo += 16
                            tt2 -= 1
                        wait(te, f"xzev_{tt2}_{oo}")
                    pst = ps["xz0"] if o % 2 == 0 else ps["xz1"]
                    for k in range(NC_DM):
                        li = te.matmul(pst[:], sb["w_in_sb"][:, k * 2048 + o * 128:
                                                             k * 2048 + o * 128 + 128],
                                       csl(sb["h1_t"], k), start=(k == 0),
                                       stop=(k == NC_DM - 1))
                    inc(li, "sP")
                    mark(f"xzmm_{tau}_{o}", "sP")

                if o >= 8:
                    @sec("scalar")
                    def _(sc, tau=tau, o=o):
                        wait(sc, f"xzmm_{tau}_{o}")
                        pst = ps["xz0"] if o % 2 == 0 else ps["xz1"]
                        if tau > 0 and o == 8:
                            wait(sc, f"g_{tau - 1}")
                        li = sc.activation(out=csl(sb["sz_t"], o - 8), in_=pst[:],
                                           func=AF.Silu)
                        inc(li, "sA")
                        mark(f"xzev_{tau}_{o}", "sA")
                        if o == 15:
                            mark(f"xsev_{tau}", "sA")
                elif tau == 0:
                    emit_winevac(0, o)


        def stage_tail(tau):
            for p in range(4):
                @sec("vector")
                def _(v, tau=tau, p=p):
                    if p == 0:
                        wait(v, f"xsv_{tau}")
                        if tau > 0:
                            wait(v, f"exp_{tau - 1}_7_3")
                    v.drain()
                    for j in range(4):
                        if j > 0:
                            v.drain()
                        for c in (2 * p, 2 * p + 1):
                            base = c * (TT + 3)
                            if j == 0:
                                li = v.tensor_scalar(
                                    out=csl(sb["scrB"], c),
                                    in0=sb["xs_t"][:, base:base + TT],
                                    scalar1=sb["convw_sb"][:, 4 * c:4 * c + 1],
                                    scalar2=vcol("convb", c),
                                    op0=ALU.mult, op1=ALU.add)
                            else:
                                li = v.scalar_tensor_tensor(
                                    out=csl(sb["scrB"], c),
                                    in0=sb["xs_t"][:, base + j:base + j + TT],
                                    scalar=sb["convw_sb"][:, 4 * c + j:4 * c + j + 1],
                                    in1=csl(sb["scrB"], c),
                                    op0=ALU.mult, op1=ALU.add)
                    inc(li, "sV")
                    mark(f"conv_{tau}_{p}", "sV")
                    if p == 3:
                        mark(f"conv_{tau}", "sV")

                @sec("scalar")
                def _(sc, tau=tau, p=p):
                    wait(sc, f"conv_{tau}_{p}")
                    if p == 0 and tau > 0:
                        wait(sc, f"gsp_{tau - 1}")
                    inc(sc.activation(out=sb["scrA"][:, 2 * p * TT:(2 * p + 2) * TT],
                                      in_=sb["scrB"][:, 2 * p * TT:(2 * p + 2) * TT],
                                      func=AF.Silu), "sA")
                    mark(f"u_{tau}_{p}", "sA")
                    if p == 3:
                        mark(f"u_{tau}", "sA")

                @sec("tensor")
                def _(te, tau=tau, p=p):
                    wait(te, f"u_{tau}_{p}")
                    if p == 0 and tau > 0:
                        wait(te, f"projev_{tau - 1}")
                    for k in (2 * p, 2 * p + 1):
                        te.matmul(ps["pdt"][:], sb["wxp_sb"][:, k * 160:k * 160 + 128],
                                  csl(sb["scrA"], k), start=(k == 0),
                                  stop=(k == NC_DH - 1))
                        li = te.matmul(ps["pbc"][:],
                                       sb["wxp_sb"][:, k * 160 + 128:k * 160 + 160],
                                       csl(sb["scrA"], k), start=(k == 0),
                                       stop=(k == NC_DH - 1))
                    if p == 3:
                        inc(li, "sP")
                        mark(f"projmm_{tau}", "sP")

            @sec("vector")
            def _(v, tau=tau):
                wait(v, f"projmm_{tau}")
                if tau > 0:
                    wait(v, f"prst_{tau - 1}")
                v.tensor_copy(sb["pdt_t"][:], ps["pdt"][:])
                inc(v.tensor_copy(sb["pbc_t"][:], ps["pbc"][:]), "sV")
                mark(f"projev_{tau}", "sV")

            @sec("gpsimd")
            def _(g, tau=tau):
                wait(g, f"projev_{tau}")
                dma(g, arp_i[tau:tau + 1, :].rearrange("o (p t) -> o p t", p=160)[:, 0:128, :],
                    sb["pdt_t"][:], q="dG")
                dma(g, arp_i[tau:tau + 1, :].rearrange("o (p t) -> o p t", p=160)[:, 128:160, :],
                    sb["pbc_t"][:], q="dG")
                mark(f"prst_{tau}", "dG")
                wait(g, f"prst_{tau}")
                g.collective_compute(
                    "AllReduce", ALU.add,
                    replica_groups=[[0, 1], [2, 3], [4, 5], [6, 7]],
                    ins=[arp_i[tau:tau + 1, :].opt()],
                    outs=[arp_o[tau:tau + 1, :].opt()],
                ).then_inc(sems["sC"], 1)
                bump("sC", 1)
                mark(f"ccp_{tau}", "sC")
                if tau > 0:
                    wait(g, f"wdtmm_{tau - 1}")
                    wait(g, f"bcfl_{tau - 1}_3")
                wait(g, f"ccp_{tau}")
                dma(g, sb["dt_sb"][:],
                    arp_o[tau:tau + 1, :].rearrange("o (p t) -> o p t", p=160)[:, 0:128, :],
                    q="dG")
                dma(g, sb["bc_sb"][:],
                    arp_o[tau:tau + 1, :].rearrange("o (p t) -> o p t", p=160)[:, 128:160, :],
                    q="dG")
                mark(f"dtbc_{tau}", "dG")


        def punit(tau, h, k, n):
            idx = h * 8 + k - n
            t2 = tau
            if idx < 0:
                idx += 32
                t2 -= 1
            return t2, idx // 8, idx % 8

        def hctree(v, tau, h, k):
            hcp = sb["HC0"]
            Hp = sb["H_t"] if k % 2 == 0 else sb["H1"]
            bcr = sb["bcrep0"] if h % 2 == 0 else sb["bcrep1"]
            v.drain()
            v.tensor_tensor(out=hcp[:], in0=Hp[:],
                            in1=bcr[:, 4 * TT:8 * TT], op=ALU.mult)
            v.drain()
            hv = AP(hcp[:].tensor, 0, [[4 * TT, 128], [1, TT], [TT, 4]])
            if h == 0:
                li = v.tensor_reduce(out=csl(sb["y_t"], k), in_=hv, axis=AXX,
                                     op=ALU.add)
            else:
                v.tensor_reduce(out=sb["murep"][:], in_=hv, axis=AXX,
                                op=ALU.add)
                v.drain()
                li = v.tensor_tensor(out=csl(sb["y_t"], k), in0=csl(sb["y_t"], k),
                                     in1=sb["murep"][:], op=ALU.add)
            inc(li, "sV")
            mark(f"tree_{tau}_{k}_{h}", "sV")

        def stage_scanblk(tau):
            for o in range(NC_DH):
                @sec("tensor")
                def _(te, tau=tau, o=o):
                    if o == 0:
                        wait(te, f"dtbc_{tau}")
                        if tau > 0:
                            wait(te, f"exp_{tau - 1}_7_3")
                    if o >= 2:
                        wait(te, f"sp_{tau}_{o - 2}")
                    pst = ps["dl0"] if o % 2 == 0 else ps["dl1"]
                    te.matmul(pst[:], sb["wdt_sb"][:, o * 128:(o + 1) * 128],
                              sb["dt_sb"][:], start=True, stop=True).then_inc(sems["sP"], 1)
                    bump("sP", 1)
                    mark(f"wdt_{tau}_{o}", "sP")
                    if o == NC_DH - 1:
                        mark(f"wdtmm_{tau}", "sP")

                @sec("scalar")
                def _(sc, tau=tau, o=o):
                    wait(sc, f"wdt_{tau}_{o}")
                    pst = ps["dl0"] if o % 2 == 0 else ps["dl1"]
                    sc.activation(out=csl(sb["xsq"], o % 2), in_=pst[:], func=AF.Exp,
                                  bias=vcol("bdt", o))
                    sc.drain()
                    sc.activation(out=csl(sb["delta_t"], o), in_=csl(sb["xsq"], o % 2),
                                  func=AF.Ln, bias=1.0).then_inc(sems["sA"], 1)
                    bump("sA", 1)
                    mark(f"sp_{tau}_{o}", "sA")
                    if o == NC_DH - 1:
                        mark(f"delta_{tau}", "sA")

            @sec("vector")
            def _(v, tau=tau):
                wait(v, f"delta_{tau}")
                if tau > 0:
                    wait(v, f"bw_{tau - 1}_7_3")
                v.drain()
                inc(v.tensor_tensor(out=sb["du_t"][:], in0=sb["delta_t"][:],
                                    in1=sb["scrA"][:], op=ALU.mult), "sV")
                mark(f"du_{tau}", "sV")

            for h in range(4):
                @sec("sync")
                def _(sy, tau=tau, h=h):
                    wait(sy, f"dtbc_{tau}")
                    bcr = sb["bcrep0"] if h % 2 == 0 else sb["bcrep1"]
                    qn = "dBC" if h % 2 == 0 else "dBC1"
                    if tau * 4 + h >= 2:
                        pt, ph = (tau, h - 2) if h >= 2 else (tau - 1, h + 2)
                        wait(sy, f"bw_{pt}_7_{ph}")
                        wait(sy, f"tree_{pt}_7_{ph}")
                    dma(sy, bcr[0:1, 0:4 * TT].rearrange("o (n t) -> o n t", n=4),
                        sb["bc_sb"][4 * h:4 * h + 4, :], q=qn)
                    dma(sy, bcr[0:1, 4 * TT:8 * TT].rearrange("o (n t) -> o n t", n=4),
                        sb["bc_sb"][16 + 4 * h:20 + 4 * h, :], q=qn)
                    nh = 1
                    step = 0
                    while nh < 128:
                        mark(f"bcq_{tau}_{h}_{step}", qn)
                        wait(sy, f"bcq_{tau}_{h}_{step}")
                        dma(sy, bcr[nh:2 * nh, :], bcr[0:nh, :], q=qn)
                        nh *= 2
                        step += 1
                    mark(f"bcfl_{tau}_{h}", qn)

                for k in range(NC_DH):
                    @sec("scalar")
                    def _(sc, tau=tau, h=h, k=k):
                        wait(sc, f"delta_{tau}")
                        p2t, p2h, p2k = punit(tau, h, k, 2)
                        if p2t >= 0:
                            wait(sc, f"scans_{p2t}_{p2k}_{p2h}")
                        jb = 4 * (k % 2)
                        for j in range(4):
                            n = 4 * h + j
                            li = sc.activation(
                                out=csl(sb["scrB"], jb + j), in_=csl(sb["delta_t"], k),
                                func=AF.Exp,
                                scale=sb["aneg"][:, k * NST + n:k * NST + n + 1])
                        inc(li, "sA")
                        mark(f"exp_{tau}_{k}_{h}", "sA")
                        p1t, p1h, p1k = punit(tau, h, k, 1)
                        if p1t >= 0:
                            wait(sc, f"scans_{p1t}_{p1k}_{p1h}")
                            Hp = sb["H_t"] if p1k % 2 == 0 else sb["H1"]
                            inc(sc.copy(
                                sb["lastst"][:, p1k * NST + 4 * p1h:
                                             p1k * NST + 4 * p1h + 4],
                                AP(Hp[:].tensor, TT - 1, [[4 * TT, 128], [TT, 4]])),
                                "sA")

                    if h == 1 and tau + 1 < NT:
                        emit_winevac(tau + 1, k)

                    @sec("gpsimd")
                    def _(g, tau=tau, h=h, k=k):
                        wait(g, f"du_{tau}")
                        wait(g, f"bcfl_{tau}_{h}")
                        p2t, p2h, p2k = punit(tau, h, k, 2)
                        if p2t == tau:
                            wait(g, f"scans_{p2t}_{p2k}_{p2h}")
                        elif (h, k) == (0, 1):
                            wait(g, f"scans_{tau}_0_0")
                        rep = AP(sb["du_t"][:].tensor, k * TT,
                                 [[NC_DH * TT, 128], [0, 4], [1, TT]])
                        bcr = sb["bcrep0"] if h % 2 == 0 else sb["bcrep1"]
                        scrCp = sb["scrC"] if k % 2 == 0 else sb["scrC1"]
                        g.tensor_tensor(out=scrCp[:].rearrange(
                                            "p (n t) -> p n t", n=4),
                                        in0=rep,
                                        in1=bcr[:, 0:4 * TT].rearrange(
                                            "p (n t) -> p n t", n=4),
                                        op=ALU.mult).then_inc(sems["sG"], 1)
                        bump("sG", 1)
                        mark(f"bw_{tau}_{k}_{h}", "sG")

                    @sec("vector")
                    def _(v, tau=tau, h=h, k=k):
                        wait(v, f"exp_{tau}_{k}_{h}")
                        wait(v, f"bw_{tau}_{k}_{h}")
                        jb = 4 * (k % 2)
                        Hp = sb["H_t"] if k % 2 == 0 else sb["H1"]
                        scrCp = sb["scrC"] if k % 2 == 0 else sb["scrC1"]
                        for j in range(4):
                            n = 4 * h + j
                            li = v.tensor_tensor_scan(
                                out=csl(Hp, j), data0=csl(sb["scrB"], jb + j),
                                data1=csl(scrCp, j),
                                initial=sb["lastst"][:, k * NST + n:k * NST + n + 1],
                                op0=ALU.mult, op1=ALU.add)
                        inc(li, "sV")
                        mark(f"scans_{tau}_{k}_{h}", "sV")
                        hctree(v, tau, h, k)

            @sec("vector")
            def _(v, tau=tau):
                v.drain()
                for k in range(NC_DH):
                    v.scalar_tensor_tensor(out=csl(sb["y_t"], k), in0=csl(sb["scrA"], k),
                                           scalar=vcol("dpar", k), in1=csl(sb["y_t"], k),
                                           op0=ALU.mult, op1=ALU.add)
                v.drain()
                for k in range(NC_DH):
                    li = v.tensor_tensor(out=csl(sb["scrA"], k), in0=csl(sb["y_t"], k),
                                         in1=csl(sb["sz_t"], k), op=ALU.mult)
                inc(li, "sV")
                mark(f"g_{tau}", "sV")

            @sec("sync")
            def _(sy, tau=tau):
                wait(sy, f"g_{tau}")
                dma(sy, g_buf[tau:tau + 1, :].rearrange("o (p c t) -> o p c t",
                                                        p=128, c=NC_DH),
                    sb["scrA"][:].rearrange("p (c t) -> p c t", c=NC_DH), q="dOut")
                mark(f"gsp_{tau}", "dOut")

            if DBG and tau == 0:
                @sec("sync")
                def _(sy, tau=tau):
                    wait(sy, f"g_{tau}")
                    dma(sy, dbg, sb["delta_t"][:])
                    mark("dbgw", "dS")


        for tau in range(NT):
            stage_front(tau)
            if tau >= 1:
                stage_scanblk(tau - 1)
            stage_win(tau)
            stage_tail(tau)
        stage_scanblk(NT - 1)
        actx.close()
        alloc("wout_sb", [128, NC_DH * DM])
        alloc("g_ld", [128, NC_DH * TT])
        alloc("pout_t", [128, NC_DM * TT])
        alloc("qi_t", [128, NC_DM * TT], I32)
        alloc("qf_t", [128, NC_DM * TT])
        alloc("qb_t", [128, NC_DM * TT], BF16)
        alloc("wf8", [128, NC_DM * DM], I8)
        alloc("wfb", [128, NC_DM * DM], BF16)
        alloc("c2scr", [128, NC_DM * TT])
        alloc("ps_ld0", [128, NC_DM * TT])
        alloc("ps_ld1", [128, NC_DM * TT])
        alloc("r0_ld0", [128, NC_DM * TT])
        alloc("r0_ld1", [128, NC_DM * TT])

        mark_led("barrierB")

        def wait_barrier(eng):
            wait_led(eng, "barrierB")

        @sec("sync")
        def _(sy):
            wait_barrier(sy)
            dma(sy, sb["wout_sb"][:].rearrange("p (c o) -> p c o", c=NC_DH),
                wout_t[:, :].rearrange("(c p) o -> p c o", p=128), q="dW")
            dma(sy, sb["wf8"][:].rearrange("p (c o) -> p c o", c=NC_DM),
                wffn_t[:, :].rearrange("(c p) o -> p c o", p=128), q="dW")
            mark("bweights", "dW")

        @sec("vector")
        def _(v):
            wait_barrier(v)
            wait(v, "bweights")
            inc(v.tensor_copy(sb["wfb"][:], sb["wf8"][:]), "sV")
            mark("wfb_done", "sV")

        @sec("tensor")
        def _(te):
            wait_barrier(te)
            te.sem_inc(sems["sP"], 1)
            bump("sP", 1)
            mark("te_bar", "sP")

        @sec("scalar")
        def _(sc):
            wait_barrier(sc)
            sc.sem_inc(sems["sA"], 1)
            bump("sA", 1)
            mark("sc_bar", "sA")

        @sec("gpsimd")
        def _(g):
            wait_barrier(g)
            g.sem_inc(sems["sG"], 1)
            bump("sG", 1)
            mark("g_bar", "sG")

        def stage_B(tau):
            @sec("sync")
            def _(sy, tau=tau):
                if tau == 0:
                    wait(sy, f"gsp_{NT - 1}")
                else:
                    wait(sy, f"poutmm_{tau - 1}")
                dma(sy, sb["g_ld"][:].rearrange("p (c t) -> p c t", c=NC_DH),
                    g_buf[tau:tau + 1, :].rearrange("o (p c t) -> o p c t",
                                                    p=128, c=NC_DH), q="dX")
                mark(f"gld_{tau}", "dX")

            for o in range(NC_DM):
                @sec("tensor")
                def _(te, tau=tau, o=o):
                    if o == 0:
                        wait(te, f"gld_{tau}")
                        wait(te, "bweights")
                    if o >= 2 or tau > 0:
                        oo, tt2 = o - 2, tau
                        if oo < 0:
                            oo += NC_DM
                            tt2 -= 1
                        wait(te, f"poutev_{tt2}_{oo}")
                    pst = ps["xz0"] if o % 2 == 0 else ps["xz1"]
                    for k in range(NC_DH):
                        li = te.matmul(pst[:], sb["wout_sb"][:, k * DM + o * 128:
                                                             k * DM + o * 128 + 128],
                                       csl(sb["g_ld"], k), start=(k == 0),
                                       stop=(k == NC_DH - 1))
                    inc(li, "sP")
                    mark(f"poutm_{tau}_{o}", "sP")
                    if o == NC_DM - 1:
                        mark(f"poutmm_{tau}", "sP")

                @sec("vector")
                def _(v, tau=tau, o=o):
                    wait(v, f"poutm_{tau}_{o}")
                    if tau > 0 and o == 0:
                        wait(v, f"post_{tau - 1}")
                    pst = ps["xz0"] if o % 2 == 0 else ps["xz1"]
                    inc(v.tensor_copy(csl(sb["pout_t"], o), pst[:]), "sV")
                    mark(f"poutev_{tau}_{o}", "sV")
                    if o == NC_DM - 1:
                        mark(f"poutevall_{tau}", "sV")

            @sec("gpsimd")
            def _(g, tau=tau):
                wait(g, f"poutevall_{tau}")
                dma(g, aro_i[tau:tau + 1, :].rearrange("o (p c t) -> o p c t",
                                                       p=128, c=NC_DM),
                    sb["pout_t"][:].rearrange("p (c t) -> p c t", c=NC_DM), q="dG")
                mark(f"post_{tau}", "dG")
                wait(g, f"post_{tau}")
                g.collective_compute(
                    "AllReduce", ALU.add,
                    replica_groups=[[0, 1], [2, 3], [4, 5], [6, 7]],
                    ins=[aro_i[tau:tau + 1, :].opt()],
                    outs=[aro_o[tau:tau + 1, :].opt()],
                ).then_inc(sems["sC"], 1)
                bump("sC", 1)
                mark(f"cco_{tau}", "sC")


        def stage_C1(tau):
            pl = "ps_ld0" if tau % 2 == 0 else "ps_ld1"
            rl = "r0_ld0" if tau % 2 == 0 else "r0_ld1"

            @sec("sync")
            def _(sy, tau=tau, pl=pl):
                wait(sy, f"cco_{tau}")
                if tau > 1:
                    wait(sy, f"x2c_{tau - 2}")
                    wait(sy, f"mx_{tau - 2}")
                    wait(sy, f"stats3_{tau - 2}")
                    wait(sy, f"s3sq_{tau - 2}_{NC_DM - 1}")
                    wait(sy, f"x2sp_{tau - 2}")
                qn = "dMsc" if tau % 2 == 0 else "dM1"
                dma(sy, sb[pl][:].rearrange("p (c t) -> p c t", c=NC_DM),
                    aro_o[tau:tau + 1, :].rearrange("o (p c t) -> o p c t",
                                                    p=128, c=NC_DM), q=qn)
                mark(f"psld_{tau}", qn)

            @sec("sync")
            def _(sy, tau=tau, rl=rl):
                if tau > 1:
                    wait(sy, f"mx_{tau - 2}")
                qn = "dR" if tau % 2 == 0 else "dM0"
                dma(sy, sb[rl][:].rearrange("p (c t) -> p c t", c=NC_DM),
                    r0_buf[tau:tau + 1, :].rearrange("o (p c t) -> o p c t",
                                                     p=128, c=NC_DM), q=qn)
                mark(f"r0ld_{tau}", qn)

            @sec("vector")
            def _(v, tau=tau, pl=pl, rl=rl):
                wait(v, f"psld_{tau}")
                wait(v, f"r0ld_{tau}")
                v.drain()
                for c in range(NC_DM):
                    li = v.scalar_tensor_tensor(out=csl(sb[pl], c),
                                                in0=csl(sb[pl], c),
                                                scalar=vcol("gap", c), in1=csl(sb[rl], c),
                                                op0=ALU.mult, op1=ALU.add)
                inc(li, "sV")
                mark(f"x2c_{tau}", "sV")

            @sec("sync")
            def _(sy, tau=tau, pl=pl):
                wait(sy, f"x2c_{tau}")
                qn = "dOut" if tau % 2 == 0 else "dS"
                dma(sy, x2_buf[tau:tau + 1, :].rearrange("o (p c t) -> o p c t",
                                                         p=128, c=NC_DM),
                    sb[pl][:].rearrange("p (c t) -> p c t", c=NC_DM), q=qn)
                mark(f"x2sp_{tau}", qn)

            def _scf3(sc, tau=tau):
                if tau == 0:
                    wait_barrier(sc)
                wait(sc, f"x2c_{tau}")

            def _tef3(te, tau=tau):
                wait(te, f"x2c_{tau}")
                if tau > 0:
                    wait(te, f"p3_{tau - 1}")

            stats_round("s3", tau, pl, sc_first=_scf3, te_first=_tef3)
            alias(f"stats3_{tau}", f"stats_s3_{tau}")

            @sec("vector")
            def _(v, tau=tau):
                wait(v, f"stats3_{tau}")
                if tau > 0:
                    wait(v, f"stsp_{tau - 1}")
                stats_post(v, tau, "ln3")

            @sec("scalar")
            def _(sc, tau=tau):
                wait(sc, f"vpre_ln3_{tau}")
                sc.activation(out=sb["stt2"][:], in_=sb["stt1"][:], func=AF.Ln)
                sc.drain()
                sc.activation(out=sb["r_a"][:], in_=sb["stt2"][:], func=AF.Exp,
                              scale=-0.5).then_inc(sems["sA"], 1)
                bump("sA", 1)
                mark(f"rstd_ln3_{tau}", "sA")

            @sec("vector")
            def _(v, tau=tau):
                wait(v, f"rstd_ln3_{tau}")
                newton(v, tau, "ln3")
            alias(f"p3_{tau}", f"post_ln3_{tau}")

            @sec("sync")
            def _(sy, tau=tau):
                wait(sy, f"p3_{tau}")
                dma(sy, st_buf[tau:tau + 1, 0:TT], sb["mu_a"][:], q="dSt")
                dma(sy, st_buf[tau:tau + 1, TT:2 * TT], sb["r_a"][:], q="dSt")
                mark(f"stsp_{tau}", "dSt")

            @sec("tensor")
            def _(te, tau=tau):
                wait(te, f"p3_{tau}")
                if tau > 0:
                    wait(te, f"mx_{tau - 1}")
                bcast(te, tau, "ln3")

            @sec("vector")
            def _(v, tau=tau, pl=pl, rl=rl):
                wait(v, f"bc_ln3_{tau}")
                rep_evac(v, tau, "ln3")
                wait(v, f"x2sp_{tau}")
                v.drain()
                for c in range(NC_DM):
                    v.tensor_tensor(out=csl(sb[rl], c), in0=csl(sb[pl], c),
                                    in1=sb["murep"][:], op=ALU.subtract)
                v.drain()
                for c in range(NC_DM):
                    v.tensor_tensor(out=csl(sb[rl], c), in0=csl(sb[rl], c),
                                    in1=sb["rrep"][:], op=ALU.mult)
                v.drain()
                for c in range(NC_DM):
                    v.tensor_scalar(out=csl(sb[rl], c), in0=csl(sb[rl], c),
                                    scalar1=vcol("cb1", c), scalar2=vcol("sbv", c),
                                    op0=ALU.mult, op1=ALU.add)
                v.drain()
                v.reduce_max(out=sb["rrep"][:, 0:1], in_=sb[rl][:], axis=AXX,
                             apply_absolute_value=True)
                v.drain()
                inc(v.tensor_tensor(out=sb["mxacc"][:], in0=sb["mxacc"][:],
                                    in1=sb["rrep"][:, 0:1], op=ALU.max), "sV")
                mark(f"mx_{tau}", "sV")


        for tau in range(NT + 1):
            if tau < NT:
                stage_B(tau)
            if tau >= 1:
                stage_C1(tau - 1)

        @sec("sync")
        def _(sy):
            wait(sy, f"mx_{NT - 1}")
            dma(sy, sb["mxrow"][0:1, 0:128].rearrange("o (a b) -> o a b", a=128),
                sb["mxacc"][:], q="dSt")
            mark("mxfl", "dSt")

        @sec("vector")
        def _(v):
            wait(v, "mxfl")
            v.drain()
            inc(v.reduce_max(out=sb["mxrow"][0:1, 128:129], in_=sb["mxrow"][0:1, 0:128],
                             axis=AXX, apply_absolute_value=False), "sV")
            mark("mxloc", "sV")

        @sec("gpsimd")
        def _(g):
            wait(g, "mxloc")
            dma(g, mx_i, sb["mxrow"][0:1, 128:129], q="dG")
            mark("mxst", "dG")
            wait(g, "mxst")
            g.collective_compute(
                "AllReduce", ALU.max,
                replica_groups=[list(range(8))],
                ins=[mx_i.opt()], outs=[mx_o.opt()],
            ).then_inc(sems["sC"], 1)
            bump("sC", 1)
            mark("ccmx", "sC")
            wait(g, "ccmx")
            dma(g, sb["mxrow"][0:1, 129:130], mx_o, q="dG")
            mark("mxg", "dG")

        @sec("vector")
        def _(v):
            wait(v, "mxg")
            v.drain()
            v.tensor_scalar(out=sb["mxrow"][0:1, 129:130], in0=sb["mxrow"][0:1, 129:130],
                            scalar1=1.0 / 127.0, scalar2=EPS, op0=ALU.mult, op1=ALU.max)
            v.drain()
            inc(v.reciprocal(sb["mxrow"][0:1, 128:129], sb["mxrow"][0:1, 129:130]), "sV")
            mark("xscv", "sV")

        @sec("tensor")
        def _(te):
            wait(te, "xscv")
            te.matmul(ps["bc"][:, 0:1], sb["onesr"][:], sb["mxrow"][0:1, 128:129],
                      start=True, stop=True)
            te.matmul(ps["bc"][:, 1:2], sb["onesr"][:], sb["mxrow"][0:1, 129:130],
                      start=True, stop=True).then_inc(sems["sP"], 1)
            bump("sP", 1)
            mark("xscb", "sP")

        @sec("vector")
        def _(v):
            wait(v, "xscb")
            V = sb["vecs"]
            sl = lambda nm: V[:, VOFF[nm]:VOFF[nm] + NC_DM]
            v.drain()
            v.tensor_copy(sb["mxacc"][:], ps["bc"][:, 0:1])
            v.tensor_copy(sb["ones1"][:], ps["bc"][:, 1:2])
            v.drain()
            v.tensor_scalar(out=sl("cb1"), in0=sl("cb1"), scalar1=sb["mxacc"][:],
                            scalar2=None, op0=ALU.mult)
            v.tensor_scalar(out=sl("sbv"), in0=sl("sbv"), scalar1=sb["mxacc"][:],
                            scalar2=None, op0=ALU.mult)
            inc(v.tensor_scalar(out=sl("gbw"), in0=sl("gbw"), scalar1=sb["ones1"][:],
                                scalar2=None, op0=ALU.mult), "sV")
            mark("c2const", "sV")

        for tau in range(NT):
            @sec("sync")
            def _(sy, tau=tau):
                if tau == 0:
                    wait(sy, "c2const")
                else:
                    wait(sy, f"x3_{tau - 1}")
                if tau == 0:
                    wait(sy, "mxfl")
                dma(sy, sb["pout_t"][:].rearrange("p (c t) -> p c t", c=NC_DM),
                    x2_buf[tau:tau + 1, :].rearrange("o (p c t) -> o p c t",
                                                     p=128, c=NC_DM), q="dX")
                dma(sy, sb["mu_a"][:], st_buf[tau:tau + 1, 0:TT], q="dX")
                dma(sy, sb["r_a"][:], st_buf[tau:tau + 1, TT:2 * TT], q="dX")
                mark(f"x2ld_{tau}", "dX")

            @sec("tensor")
            def _(te, tau=tau):
                wait(te, f"x2ld_{tau}")
                if tau == 0:
                    wait(te, "c2const")
                else:
                    wait(te, f"vq_{tau - 1}")
                bcast(te, tau, f"c2_{tau}")

            @sec("vector")
            def _(v, tau=tau):
                wait(v, f"bc_c2_{tau}_{tau}")
                rep_evac(v, tau, f"c2r_{tau}")
                wait(v, "c2const")
                if tau > 0:
                    wait(v, f"fmm_{tau - 1}_{NC_DM - 1}")
                    wait(v, f"x3sp_{tau - 1}")
                v.drain()
                for c in range(NC_DM):
                    v.tensor_tensor(out=csl(sb["g_ld"], c), in0=csl(sb["pout_t"], c),
                                    in1=sb["murep"][:], op=ALU.subtract)
                v.drain()
                for c in range(NC_DM):
                    v.tensor_tensor(out=csl(sb["g_ld"], c), in0=csl(sb["g_ld"], c),
                                    in1=sb["rrep"][:], op=ALU.mult)
                v.drain()
                for c in range(NC_DM):
                    v.tensor_scalar(out=csl(sb["g_ld"], c), in0=csl(sb["g_ld"], c),
                                    scalar1=vcol("cb1", c), scalar2=vcol("sbv", c),
                                    op0=ALU.mult, op1=ALU.add)
                v.drain()
                v.tensor_copy(sb["qi_t"][:], sb["g_ld"][:])
                v.drain()
                v.tensor_copy(sb["qf_t"][:], sb["qi_t"][:])
                v.drain()
                v.tensor_tensor(out=sb["c2scr"][:], in0=sb["g_ld"][:], in1=sb["qf_t"][:],
                                op=ALU.subtract)
                v.drain()
                v.scalar_tensor_tensor(out=sb["qf_t"][:], in0=sb["c2scr"][:], scalar=0.5,
                                       in1=sb["qf_t"][:], op0=ALU.is_ge, op1=ALU.add)
                v.drain()
                v.scalar_tensor_tensor(out=sb["qf_t"][:], in0=sb["c2scr"][:], scalar=-0.5,
                                       in1=sb["qf_t"][:], op0=ALU.is_ge, op1=ALU.add)
                v.drain()
                inc(v.tensor_scalar(out=sb["qb_t"][:], in0=sb["qf_t"][:],
                                    scalar1=-1.0, scalar2=127.0, op0=ALU.add,
                                    op1=ALU.min), "sV")
                mark(f"vq_{tau}", "sV")

            for o in range(NC_DM):
                @sec("tensor")
                def _(te, tau=tau, o=o):
                    if o == 0:
                        wait(te, f"vq_{tau}")
                        if tau == 0:
                            wait(te, "wfb_done")
                    if o >= 2 or tau > 0:
                        oo, tt2 = o - 2, tau
                        if oo < 0:
                            oo += NC_DM
                            tt2 -= 1
                        wait(te, f"fev_{tt2}_{oo}")
                    pst = ps["xz0"] if o % 2 == 0 else ps["xz1"]
                    for k in range(NC_DM):
                        li = te.matmul(pst[:], sb["wfb"][:, k * DM + o * 128:
                                                         k * DM + o * 128 + 128],
                                       csl(sb["qb_t"], k), start=(k == 0),
                                       stop=(k == NC_DM - 1))
                    inc(li, "sP")
                    mark(f"fmm_{tau}_{o}", "sP")

                @sec("vector")
                def _(v, tau=tau, o=o):
                    wait(v, f"fmm_{tau}_{o}")
                    if tau > 0 and o == 0:
                        wait(v, f"x3sp_{tau - 1}")
                    pst = ps["xz0"] if o % 2 == 0 else ps["xz1"]
                    v.drain()
                    v.tensor_scalar(out=csl(sb["qf_t"], o), in0=pst[:],
                                    scalar1=vcol("gbw", o), scalar2=vcol("gbb", o),
                                    op0=ALU.mult, op1=ALU.add)
                    v.drain()
                    inc(v.tensor_tensor(out=csl(sb["qf_t"], o), in0=csl(sb["qf_t"], o),
                                        in1=csl(sb["pout_t"], o), op=ALU.add), "sV")
                    mark(f"fev_{tau}_{o}", "sV")
                    if o == NC_DM - 1:
                        mark(f"x3_{tau}", "sV")

            @sec("sync")
            def _(sy, tau=tau):
                wait(sy, f"x3_{tau}")
                dma(sy, xout[:, tsl(tau)].rearrange("(c p) t -> p c t", p=128),
                    sb["qf_t"][:].rearrange("p (c t) -> p c t", c=NC_DM), q="dOut")
                mark(f"x3sp_{tau}", "dOut")

        mark_led("final")

        @sec("sync")
        def _(sy):
            wait_led(sy, "final", only={"dW", "dX", "dR", "dBC", "dBC1", "dOut",
                                        "dMsc", "dSt", "dM0", "dM1", "dS"})

        @sec("gpsimd")
        def _(g):
            wait_led(g, "final", only={"dG"})

        if pass2:
            for engname in ["sync", "tensor", "vector", "scalar", "gpsimd"]:
                fns = thunks[engname]

                @getattr(block, engname)
                def _(eng, fns=fns):
                    for fn in fns:
                        fn(eng)

    return nc, ev


def kernel(L0_attn_x, L0_attn_ln_gamma, L0_attn_ln_beta, L0_attn_W_in,
           L0_attn_conv_w, L0_attn_conv_b, L0_attn_W_xproj, L0_attn_W_dt,
           L0_attn_b_dt, L0_attn_A_log, L0_attn_D_param, L0_attn_W_out,
           L0_ffn_weight_int8, L0_ffn_weight_scale, L0_ffn_bias, L0_c, L0_wmod):
    _, ev1 = build_program()
    nc, _ = build_program(ev1)

    def vec_pack(v):
        return np.ascontiguousarray(np.asarray(v, np.float32).reshape(NC_DM, 128).T)

    x = np.asarray(L0_attn_x, np.float32)
    W_in = np.asarray(L0_attn_W_in, np.float32)
    conv_w = np.asarray(L0_attn_conv_w, np.float32)
    conv_b = np.asarray(L0_attn_conv_b, np.float32)
    W_xp = np.asarray(L0_attn_W_xproj, np.float32)
    W_dt = np.asarray(L0_attn_W_dt, np.float32)
    b_dt = np.asarray(L0_attn_b_dt, np.float32)
    A_log = np.asarray(L0_attn_A_log, np.float32)
    D_par = np.asarray(L0_attn_D_param, np.float32)
    W_out = np.asarray(L0_attn_W_out, np.float32)
    w8 = np.asarray(L0_ffn_weight_int8, np.int8)
    wscale = np.asarray(L0_ffn_weight_scale, np.float32)
    fb = np.asarray(L0_ffn_bias, np.float32)
    cc = np.asarray(L0_c, np.float32)
    wmod = np.ascontiguousarray(np.asarray(L0_wmod, np.float32))

    in_maps = []
    for core in range(8):
        b = core // 2
        hh = core % 2
        dsl = slice(hh * DH, (hh + 1) * DH)
        w_in_sl = np.concatenate([W_in[hh * DH:(hh + 1) * DH],
                                  W_in[2048 + hh * DH:2048 + (hh + 1) * DH]], 0)
        cw = conv_w[dsl, 0, :]
        m = dict(
            xin=np.ascontiguousarray(x[b].T),
            w_in_t=np.ascontiguousarray(w_in_sl.T),
            wxp_t=np.ascontiguousarray(W_xp[:, dsl].T),
            wdt_t=np.ascontiguousarray(W_dt[dsl].T),
            wout_t=np.ascontiguousarray(W_out[:, dsl].T),
            wmod=wmod,
            wffn_t=np.ascontiguousarray(w8.T),
            convw=np.ascontiguousarray(
                cw.reshape(NC_DH, 128, 4).transpose(1, 0, 2).reshape(128, NC_DH * 4)),
            convb=vec_pack(conv_b[dsl]),
            bdt=vec_pack(b_dt[dsl]),
            alog=np.ascontiguousarray(
                A_log[dsl].reshape(NC_DH, 128, NST).transpose(1, 0, 2)
                .reshape(128, NC_DH * NST)),
            dpar=vec_pack(D_par[dsl]),
            lngam=vec_pack(np.asarray(L0_attn_ln_gamma, np.float32)),
            lnbet=vec_pack(np.asarray(L0_attn_ln_beta, np.float32)),
            wsc=vec_pack(wscale[:, 0]),
            fbias=vec_pack(fb),
            cvec=vec_pack(cc[b]),
        )
        in_maps.append(m)

    res = run_bass_kernel_spmd(nc, in_maps, core_ids=list(range(8)))
    out = np.empty((4, S, DM), np.float32)
    for b in range(4):
        out[b] = res.results[2 * b]["xout"].T
    return out


# version 17 (final)
# speedup vs baseline: 1.2136x; 1.0057x over previous
import numpy as np
import concourse.bass as bass
import concourse.mybir as mybir
from concourse.bass import AP
from concourse.bass_utils import run_bass_kernel_spmd

F32 = mybir.dt.float32
BF16 = mybir.dt.bfloat16
I8 = mybir.dt.int8
I32 = mybir.dt.int32
AF = mybir.ActivationFunctionType
ALU = mybir.AluOpType
AXX = mybir.AxisListType.X

DM = 1024
DH = 1024
NC_DM = DM // 128
NC_DH = DH // 128
NST = 16
S = 8192
TT = 256
NT = S // TT
EPS = 1e-5

DBG = False


class _DummyInstr:
    def then_inc(self, *a, **k):
        return self


class _DummyEngine:
    def __getattr__(self, name):
        def f(*a, **k):
            return _DummyInstr()
        return f


def build_program(ev_table=None):
    pass2 = ev_table is not None
    nc = bass.Bass("TRN2", target_bir_lowering=False)

    xin = nc.dram_tensor("xin", [DM, S], F32, kind="ExternalInput").ap()
    w_in_t = nc.dram_tensor("w_in_t", [DM, 2 * DH], F32, kind="ExternalInput").ap()
    wxp_t = nc.dram_tensor("wxp_t", [DH, 160], F32, kind="ExternalInput").ap()
    wdt_t = nc.dram_tensor("wdt_t", [128, DH], F32, kind="ExternalInput").ap()
    wout_t = nc.dram_tensor("wout_t", [DH, DM], F32, kind="ExternalInput").ap()
    wmod = nc.dram_tensor("wmod", [DM, 6 * DM], F32, kind="ExternalInput").ap()
    wffn_t = nc.dram_tensor("wffn_t", [DM, DM], I8, kind="ExternalInput").ap()
    convw = nc.dram_tensor("convw", [128, NC_DH * 4], F32, kind="ExternalInput").ap()
    convb = nc.dram_tensor("convb", [128, NC_DH], F32, kind="ExternalInput").ap()
    bdt = nc.dram_tensor("bdt", [128, NC_DH], F32, kind="ExternalInput").ap()
    alog = nc.dram_tensor("alog", [128, NC_DH * NST], F32, kind="ExternalInput").ap()
    dpar = nc.dram_tensor("dpar", [128, NC_DH], F32, kind="ExternalInput").ap()
    lngam = nc.dram_tensor("lngam", [128, NC_DM], F32, kind="ExternalInput").ap()
    lnbet = nc.dram_tensor("lnbet", [128, NC_DM], F32, kind="ExternalInput").ap()
    wsc = nc.dram_tensor("wsc", [128, NC_DM], F32, kind="ExternalInput").ap()
    fbias = nc.dram_tensor("fbias", [128, NC_DM], F32, kind="ExternalInput").ap()
    cvec = nc.dram_tensor("cvec", [128, NC_DM], F32, kind="ExternalInput").ap()

    xout = nc.dram_tensor("xout", [DM, S], F32, kind="ExternalOutput").ap()
    dbg = nc.dram_tensor("dbg", [128, 8 * TT], F32, kind="ExternalOutput").ap() if DBG else None

    g_buf = nc.dram_tensor("g_buf", [NT, DH * TT], F32).ap()
    r0_buf = nc.dram_tensor("r0_buf", [NT, DM * TT], F32).ap()
    x2_buf = nc.dram_tensor("x2_buf", [NT, DM * TT], F32).ap()
    arp_i = nc.dram_tensor("arp_i", [NT, 160 * TT], F32).ap()
    arp_o = nc.dram_tensor("arp_o", [NT, 160 * TT], F32).ap()
    aro_i = nc.dram_tensor("aro_i", [NT, DM * TT], F32).ap()
    aro_o = nc.dram_tensor("aro_o", [NT, DM * TT], F32).ap()
    mx_i = nc.dram_tensor("mx_i", [1, 1], F32).ap()
    mx_o = nc.dram_tensor("mx_o", [1, 1], F32).ap()
    st_buf = nc.dram_tensor("st_buf", [NT, 2 * TT], F32).ap()

    from contextlib import ExitStack

    led = {}
    ev = {}

    def bump(sem_name, amt):
        if pass2:
            return 0
        led[sem_name] = led.get(sem_name, 0) + amt
        return led[sem_name]

    def mark(name, sem_name):
        if pass2:
            return
        ev[name] = (sem_name, led.get(sem_name, 0))

    def alias(dst, src):
        if pass2:
            return
        ev[dst] = ev[src]

    def mark_led(name):
        if pass2:
            return
        ev[name] = dict(led)

    def wait(eng, name):
        tbl = ev_table or ev
        if not pass2 and name not in tbl:
            return
        s, v = tbl[name]
        if v > 0:
            eng.wait_ge(sems[s], v)

    def wait_led(eng, name, only=None):
        for snm, val in (ev_table or ev)[name].items():
            if val > 0 and (only is None or snm in only):
                eng.wait_ge(sems[snm], val)

    sems = {}


    with ExitStack() as ctx:
        for sname in ["dS", "dG", "dM0", "dM1", "dW", "dX", "dR", "dBC", "dBC1",
                      "dOut", "dMsc", "dSt", "sP", "sV", "sA", "sG", "sC"]:
            sems[sname] = ctx.enter_context(nc.semaphore(sname))

        def inc(instr, sname):
            instr.then_inc(sems[sname], 1)
            return bump(sname, 1)

        def dma(eng, out, in_, q="dS"):
            eng.dma_start(out=out, in_=in_).then_inc(sems[q], 16)
            return bump(q, 16)

        sb = {}
        actx = ctx.enter_context(ExitStack())

        def alloc(name, shape, dtype=F32, scope=None):
            tgt = scope if scope is not None else ctx
            sb[name] = tgt.enter_context(nc.sbuf_tensor(name, shape, dtype))
            return sb[name]

        vecs = alloc("vecs", [128, 16 * NC_DM])
        aneg = alloc("aneg", [128, NC_DH * NST])
        ones1 = alloc("ones1", [128, 1])
        onesr = alloc("onesr", [1, 128])
        lastst = alloc("lastst", [128, NC_DH * NST])

        VOFF = {}
        for i, nm in enumerate(["convb", "bdt", "dpar", "lngam", "lnbet", "wsc",
                                "fbias", "sa", "ca1", "gap", "sbv", "cb1", "gbp",
                                "gbw", "gbb", "cvec"]):
            VOFF[nm] = i * NC_DM

        def vcol(nm, c):
            return sb["vecs"][:, VOFF[nm] + c:VOFF[nm] + c + 1]

        mu_a = alloc("mu_a", [1, TT])
        r_a = alloc("r_a", [1, TT])
        stt1 = alloc("stt1", [1, TT])
        stt2 = alloc("stt2", [1, TT])
        murep = alloc("murep", [128, TT])
        rrep = alloc("rrep", [128, TT])
        mxacc = alloc("mxacc", [128, 1])
        mxrow = alloc("mxrow", [1, 130])
        xsq = alloc("xsq", [128, 2 * TT])

        alloc("w_in_sb", [128, NC_DM * 2 * DH], scope=actx)
        alloc("wxp_sb", [128, NC_DH * 160], scope=actx)
        alloc("wdt_sb", [128, DH], scope=actx)
        alloc("convw_sb", [128, NC_DH * 4], scope=actx)
        alloc("x_t", [128, NC_DM * TT], scope=actx)
        alloc("h1_t", [128, NC_DM * TT], scope=actx)
        alloc("r0_t", [128, NC_DM * TT], scope=actx)
        sb["xn_t"] = sb["r0_t"]
        alloc("xs_t", [128, NC_DH * (TT + 3)], scope=actx)
        alloc("scrA", [128, NC_DH * TT], scope=actx)
        alloc("scrB", [128, 8 * TT], scope=actx)
        alloc("scrC", [128, 4 * TT], scope=actx)
        alloc("delta_t", [128, NC_DH * TT], scope=actx)
        alloc("du_t", [128, NC_DH * TT], scope=actx)
        alloc("HC0", [128, 4 * TT], scope=actx)
        alloc("HC1", [128, 4 * TT], scope=actx)
        alloc("sz_t", [128, NC_DH * TT], scope=actx)
        alloc("H_t", [128, 4 * TT], scope=actx)
        alloc("H1", [128, 4 * TT], scope=actx)
        alloc("scrC1", [128, 4 * TT], scope=actx)
        alloc("y_t", [128, NC_DH * TT], scope=actx)
        alloc("pdt_t", [128, TT], scope=actx)
        alloc("pbc_t", [32, TT], scope=actx)
        alloc("dt_sb", [128, TT], scope=actx)
        alloc("bc_sb", [32, TT], scope=actx)
        s0ctx = actx.enter_context(ExitStack())
        alloc("cs_t", [128, NC_DM], scope=s0ctx)
        alloc("modrow", [1, 1024], scope=s0ctx)
        alloc("wm0", [128, 512], scope=s0ctx)
        alloc("wm1", [128, 512], scope=s0ctx)

        ps = {}
        for nm, shape in [("smu", [1, TT]), ("sm2", [1, TT]), ("bc", [128, 2 * TT]),
                          ("xz0", [128, TT]), ("xz1", [128, TT]),
                          ("pdt", [128, TT]), ("pbc", [32, TT])]:
            ps[nm] = ctx.enter_context(nc.psum_tensor("ps_" + nm, shape, F32))
        ps["mu"] = ps["smu"][0:1, :]
        ps["m2"] = ps["sm2"][0:1, :]
        ps["mod"] = ps["bc"][0:1, 0:512]
        ps["dl0"] = ps["xz0"]
        ps["dl1"] = ps["xz1"]

        block = ctx.enter_context(nc.Block())
        thunks = {k: [] for k in ["sync", "tensor", "vector", "scalar", "gpsimd"]}
        _dummy = _DummyEngine()

        def sec(engname):
            if not pass2:
                def dec(fn):
                    fn(_dummy)
                return dec

            def dec(fn):
                thunks[engname].append(fn)
            return dec

        @sec("sync")
        def _(sy):
            dma(sy, sb["w_in_sb"][:].rearrange("p (c o) -> p c o", c=NC_DM),
                w_in_t[:, :].rearrange("(c p) o -> p c o", p=128), q="dW")
            dma(sy, sb["wxp_sb"][:].rearrange("p (c o) -> p c o", c=NC_DH),
                wxp_t[:, :].rearrange("(c p) o -> p c o", p=128), q="dW")
            dma(sy, sb["wdt_sb"][:], wdt_t, q="dW")
            dma(sy, sb["convw_sb"][:], convw, q="dW")
            dma(sy, sb["aneg"][:], alog, q="dW")
            for nm, src in [("convb", convb), ("bdt", bdt), ("dpar", dpar),
                            ("lngam", lngam), ("lnbet", lnbet), ("wsc", wsc),
                            ("fbias", fbias), ("cvec", cvec)]:
                dma(sy, sb["vecs"][:, VOFF[nm]:VOFF[nm] + NC_DM], src, q="dW")
            mark("weights", "dW")

        @sec("vector")
        def _(v):
            v.memset(sb["ones1"][:], 1.0)
            v.memset(sb["onesr"][:], 1.0)
            v.memset(sb["lastst"][:], 0.0)
            wait(v, "weights")
            inc(v.memset(sb["mxacc"][:], 0.0), "sV")
            mark("v_setup", "sV")

        @sec("scalar")
        def _(sc):
            wait(sc, "weights")
            sc.activation(out=sb["cs_t"][:], in_=sb["vecs"][:, VOFF["cvec"]:VOFF["cvec"] + NC_DM],
                          func=AF.Silu)
            inc(sc.activation(out=sb["aneg"][:], in_=sb["aneg"][:], func=AF.Exp), "sA")
            mark("a_setup", "sA")

        @sec("vector")
        def _(v):
            wait(v, "a_setup")
            inc(v.tensor_scalar(out=sb["aneg"][:], in0=sb["aneg"][:], scalar1=-1.0,
                                scalar2=None, op0=ALU.mult), "sV")
            mark("aneg_done", "sV")

        NJ = 6 * DM // 512
        QNAMES = ["sa", "ca1", "gap", "sbv", "cb1", "gbp"]
        for j in range(NJ):
            for k in range(NC_DM):
                idx = j * NC_DM + k

                @sec("sync")
                def _(sy, j=j, k=k, idx=idx):
                    if idx == 0:
                        wait(sy, "a_setup")
                    if idx >= 2:
                        wait(sy, f"modmm_{idx - 2}")
                    wm = [sb["wm0"], sb["wm1"]][idx % 2]
                    dma(sy, wm[:], wmod[k * 128:(k + 1) * 128, j * 512:(j + 1) * 512],
                        q="dM%d" % (idx % 2))
                    mark(f"modld_{idx}", "dM%d" % (idx % 2))

                @sec("tensor")
                def _(te, j=j, k=k, idx=idx):
                    wm = [sb["wm0"], sb["wm1"]][idx % 2]
                    wait(te, f"modld_{idx}")
                    if k == 0 and j > 0:
                        wait(te, f"modev_{j - 1}")
                    te.matmul(ps["mod"], sb["cs_t"][:, k:k + 1], wm[:],
                              start=(k == 0), stop=(k == NC_DM - 1)).then_inc(sems["sP"], 1)
                    bump("sP", 1)
                    mark(f"modmm_{idx}", "sP")
                    if k == NC_DM - 1:
                        mark(f"modcol_{j}", "sP")

            @sec("vector")
            def _(v, j=j):
                wait(v, f"modcol_{j}")
                if j >= 2:
                    wait(v, f"modsc_{(j - 2) // 2}")
                inc(v.tensor_copy(sb["modrow"][0:1, (j % 2) * 512:(j % 2) * 512 + 512],
                                  ps["mod"]), "sV")
                mark(f"modev_{j}", "sV")

            if j % 2 == 1:
                q = j // 2

                @sec("sync")
                def _(sy, q=q, j=j):
                    wait(sy, f"modev_{j}")
                    for jj in range(NC_DM):
                        sy.dma_start(out=sb["vecs"][:, VOFF[QNAMES[q]] + jj:VOFF[QNAMES[q]] + jj + 1],
                                     in_=sb["modrow"][0:1, jj * 128:(jj + 1) * 128],
                                     allow_slow_non_contiguous=True).then_inc(sems["dMsc"], 16)
                        bump("dMsc", 16)
                    mark(f"modsc_{q}", "dMsc")
        mark("mod_scattered", "dMsc")

        @sec("vector")
        def _(v):
            wait(v, "mod_scattered")
            V = sb["vecs"]
            sl = lambda nm: V[:, VOFF[nm]:VOFF[nm] + NC_DM]
            v.tensor_scalar(out=sl("ca1"), in0=sl("ca1"), scalar1=1.0, scalar2=None, op0=ALU.add)
            v.tensor_scalar(out=sl("cb1"), in0=sl("cb1"), scalar1=1.0, scalar2=None, op0=ALU.add)
            for nm in ["gap", "gbp"]:
                g = sl(nm)
                v.drain()
                v.tensor_scalar(out=sb["murep"][:, 0:NC_DM], in0=g, scalar1=0.0,
                                scalar2=None, op0=ALU.is_ge)
                v.drain()
                v.tensor_scalar(out=sb["murep"][:, 0:NC_DM], in0=sb["murep"][:, 0:NC_DM],
                                scalar1=2.0, scalar2=-1.0, op0=ALU.mult, op1=ALU.add)
                v.tensor_scalar(out=sb["rrep"][:, 0:NC_DM], in0=g, scalar1=-1.0,
                                scalar2=None, op0=ALU.mult)
                v.drain()
                v.tensor_tensor(out=sb["rrep"][:, 0:NC_DM], in0=g, in1=sb["rrep"][:, 0:NC_DM],
                                op=ALU.max)
                v.drain()
                v.tensor_scalar(out=sb["rrep"][:, 0:NC_DM], in0=sb["rrep"][:, 0:NC_DM],
                                scalar1=0.25, scalar2=None, op0=ALU.max)
                v.drain()
                v.tensor_tensor(out=g, in0=sb["murep"][:, 0:NC_DM],
                                in1=sb["rrep"][:, 0:NC_DM], op=ALU.mult)
            v.drain()
            v.tensor_tensor(out=sl("gbw"), in0=sl("gbp"), in1=sl("wsc"), op=ALU.mult)
            inc(v.tensor_tensor(out=sl("gbb"), in0=sl("gbp"), in1=sl("fbias"),
                                op=ALU.mult), "sV")
            mark("mod_done", "sV")

        s0ctx.close()
        alloc("bcrep0", [128, 2 * 4 * TT], scope=actx)
        alloc("bcrep1", [128, 2 * 4 * TT], scope=actx)
        mark_led("bar0")

        def wait_bar0(eng):
            wait_led(eng, "bar0")

        def tsl(tau):
            return slice(tau * TT, (tau + 1) * TT)

        def csl(tile, c, width=TT):
            return tile[:, c * width:(c + 1) * width]

        def stats_round(tag, tau, src_name, sc_first=None, te_first=None):
            for c in range(NC_DM):
                @sec("scalar")
                def _(sc, c=c):
                    if c == 0 and sc_first is not None:
                        sc_first(sc)
                    if c >= 2:
                        wait(sc, f"{tag}m2_{tau}_{c - 2}")
                    sc.activation(out=csl(sb["xsq"], c % 2), in_=csl(sb[src_name], c),
                                  func=AF.Square).then_inc(sems["sA"], 1)
                    bump("sA", 1)
                    mark(f"{tag}sq_{tau}_{c}", "sA")

                @sec("tensor")
                def _(te, c=c):
                    if c == 0 and te_first is not None:
                        te_first(te)
                    te.matmul(ps["mu"], sb["ones1"][:], csl(sb[src_name], c),
                              start=(c == 0), stop=(c == NC_DM - 1)).then_inc(sems["sP"], 1)
                    bump("sP", 1)
                    wait(te, f"{tag}sq_{tau}_{c}")
                    te.matmul(ps["m2"], sb["ones1"][:], csl(sb["xsq"], c % 2),
                              start=(c == 0), stop=(c == NC_DM - 1)).then_inc(sems["sP"], 1)
                    bump("sP", 1)
                    mark(f"{tag}m2_{tau}_{c}", "sP")
            mark(f"stats_{tag}_{tau}", "sP")

        def stats_post(v, tau, tag):
            v.drain()
            v.tensor_scalar(out=sb["mu_a"][:], in0=ps["mu"], scalar1=1.0 / DM,
                            scalar2=None, op0=ALU.mult)
            v.tensor_scalar(out=sb["stt1"][:], in0=ps["m2"], scalar1=1.0 / DM,
                            scalar2=None, op0=ALU.mult)
            v.drain()
            v.tensor_tensor(out=sb["stt2"][:], in0=sb["mu_a"][:], in1=sb["mu_a"][:],
                            op=ALU.mult)
            v.drain()
            v.tensor_tensor(out=sb["stt1"][:], in0=sb["stt1"][:], in1=sb["stt2"][:],
                            op=ALU.subtract)
            v.drain()
            inc(v.tensor_scalar(out=sb["stt1"][:], in0=sb["stt1"][:], scalar1=EPS,
                                scalar2=None, op0=ALU.add), "sV")
            mark(f"vpre_{tag}_{tau}", "sV")


        def newton(v, tau, tag):
            alias(f"post_{tag}_{tau}", f"rstd_{tag}_{tau}")


        def bcast(te, tau, tag):
            te.matmul(ps["bc"][:, 0:TT], sb["onesr"][:], sb["mu_a"][:],
                      start=True, stop=True).then_inc(sems["sP"], 1)
            bump("sP", 1)
            te.matmul(ps["bc"][:, TT:2 * TT], sb["onesr"][:], sb["r_a"][:],
                      start=True, stop=True).then_inc(sems["sP"], 1)
            bump("sP", 1)
            mark(f"bc_{tag}_{tau}", "sP")


        def rep_evac(v, tau, tag):
            v.drain()
            v.tensor_copy(sb["murep"][:], ps["bc"][:, 0:TT])
            inc(v.tensor_copy(sb["rrep"][:], ps["bc"][:, TT:2 * TT]), "sV")
            mark(f"rep_{tag}_{tau}", "sV")


        def stage_front(tau):
            @sec("sync")
            def _(sy, tau=tau):
                if tau == 0:
                    wait_bar0(sy)
                else:
                    wait(sy, f"r0c_{tau - 1}")
                dma(sy, sb["x_t"][:].rearrange("p (c t) -> p c t", c=NC_DM),
                    xin[:, tsl(tau)].rearrange("(c p) t -> p c t", p=128), q="dX")
                mark(f"xld_{tau}", "dX")

            def _scf1(sc, tau=tau):
                wait(sc, f"xld_{tau}")
                if tau == 0:
                    wait_bar0(sc)

            def _tef1(te, tau=tau):
                if tau == 0:
                    wait_bar0(te)
                else:
                    wait(te, f"post2_{tau - 1}")
                wait(te, f"xld_{tau}")

            stats_round("s1", tau, "x_t", sc_first=_scf1, te_first=_tef1)
            alias(f"stats1_{tau}", f"stats_s1_{tau}")

            @sec("vector")
            def _(v, tau=tau):
                wait(v, f"stats1_{tau}")
                stats_post(v, tau, "ln1")

            @sec("scalar")
            def _(sc, tau=tau):
                wait(sc, f"vpre_ln1_{tau}")
                sc.activation(out=sb["stt2"][:], in_=sb["stt1"][:], func=AF.Ln)
                sc.drain()
                sc.activation(out=sb["r_a"][:], in_=sb["stt2"][:], func=AF.Exp,
                              scale=-0.5).then_inc(sems["sA"], 1)
                bump("sA", 1)
                mark(f"rstd_ln1_{tau}", "sA")

            @sec("vector")
            def _(v, tau=tau):
                wait(v, f"rstd_ln1_{tau}")
                newton(v, tau, "ln1")
            alias(f"post1_{tau}", f"post_ln1_{tau}")

            @sec("tensor")
            def _(te, tau=tau):
                wait(te, f"post1_{tau}")
                if tau > 0:
                    wait(te, f"rep2_{tau - 1}")
                bcast(te, tau, "ln1")

            @sec("vector")
            def _(v, tau=tau):
                wait(v, f"bc_ln1_{tau}")
                rep_evac(v, tau, "ln1")
            alias(f"rep1_{tau}", f"rep_ln1_{tau}")

            @sec("vector")
            def _(v, tau=tau):
                V = sb["vecs"]
                if tau > 0:
                    wait(v, f"r0sp_{tau - 1}")
                v.drain()
                for c in range(NC_DM):
                    v.tensor_tensor(out=csl(sb["xn_t"], c), in0=csl(sb["x_t"], c),
                                    in1=sb["murep"][:], op=ALU.subtract)
                v.drain()
                for c in range(NC_DM):
                    v.tensor_tensor(out=csl(sb["xn_t"], c), in0=csl(sb["xn_t"], c),
                                    in1=sb["rrep"][:], op=ALU.mult)
                if tau > 0:
                    wait(v, f"xzmm_{tau - 1}_15")
                v.drain()
                for c in range(NC_DM):
                    v.tensor_scalar(out=csl(sb["h1_t"], c), in0=csl(sb["xn_t"], c),
                                    scalar1=vcol("ca1", c), scalar2=vcol("sa", c),
                                    op0=ALU.mult, op1=ALU.add)
                v.drain()
                for c in range(NC_DM):
                    li = v.scalar_tensor_tensor(out=csl(sb["r0_t"], c), in0=csl(sb["h1_t"], c),
                                                scalar=vcol("gap", c), in1=csl(sb["x_t"], c),
                                                op0=ALU.mult, op1=ALU.add)
                inc(li, "sV")
                mark(f"r0c_{tau}", "sV")

            @sec("sync")
            def _(sy, tau=tau):
                wait(sy, f"r0c_{tau}")
                dma(sy, r0_buf[tau:tau + 1, :].rearrange("o (p c t) -> o p c t", p=128, c=NC_DM),
                    sb["r0_t"][:].rearrange("p (c t) -> p c t", c=NC_DM), q="dR")
                mark(f"r0sp_{tau}", "dR")

            def _scf2(sc, tau=tau):
                wait(sc, f"r0c_{tau}")

            def _tef2(te, tau=tau):
                wait(te, f"post1_{tau}")
                wait(te, f"r0c_{tau}")

            stats_round("s2", tau, "h1_t", sc_first=_scf2, te_first=_tef2)
            alias(f"stats2_{tau}", f"stats_s2_{tau}")

            @sec("vector")
            def _(v, tau=tau):
                wait(v, f"stats2_{tau}")
                stats_post(v, tau, "ln2i")

            @sec("scalar")
            def _(sc, tau=tau):
                wait(sc, f"vpre_ln2i_{tau}")
                sc.activation(out=sb["stt2"][:], in_=sb["stt1"][:], func=AF.Ln)
                sc.drain()
                sc.activation(out=sb["r_a"][:], in_=sb["stt2"][:], func=AF.Exp,
                              scale=-0.5).then_inc(sems["sA"], 1)
                bump("sA", 1)
                mark(f"rstd_ln2i_{tau}", "sA")

            @sec("vector")
            def _(v, tau=tau):
                wait(v, f"rstd_ln2i_{tau}")
                newton(v, tau, "ln2i")
            alias(f"post2_{tau}", f"post_ln2i_{tau}")

            @sec("tensor")
            def _(te, tau=tau):
                wait(te, f"post2_{tau}")
                wait(te, f"rep1_{tau}")
                bcast(te, tau, "ln2i")

            @sec("vector")
            def _(v, tau=tau):
                wait(v, f"bc_ln2i_{tau}")
                rep_evac(v, tau, "ln2i")
            alias(f"rep2_{tau}", f"rep_ln2i_{tau}")

            @sec("vector")
            def _(v, tau=tau):
                v.drain()
                for c in range(NC_DM):
                    v.tensor_tensor(out=csl(sb["h1_t"], c), in0=csl(sb["h1_t"], c),
                                    in1=sb["murep"][:], op=ALU.subtract)
                v.drain()
                for c in range(NC_DM):
                    v.tensor_tensor(out=csl(sb["h1_t"], c), in0=csl(sb["h1_t"], c),
                                    in1=sb["rrep"][:], op=ALU.mult)
                v.drain()
                for c in range(NC_DM):
                    li = v.tensor_scalar(out=csl(sb["h1_t"], c), in0=csl(sb["h1_t"], c),
                                         scalar1=vcol("lngam", c), scalar2=vcol("lnbet", c),
                                         op0=ALU.mult, op1=ALU.add)
                inc(li, "sV")
                mark(f"xn2_{tau}", "sV")

            @sec("vector")
            def _(v, tau=tau):
                if tau > 0:
                    wait(v, f"conv_{tau - 1}")
                    src = AP(sb["xs_t"][:].tensor, TT,
                             [[NC_DH * (TT + 3), 128], [TT + 3, NC_DH], [1, 3]])
                    dst = AP(sb["xs_t"][:].tensor, 0,
                             [[NC_DH * (TT + 3), 128], [TT + 3, NC_DH], [1, 3]])
                    inc(v.tensor_copy(dst, src), "sV")
                else:
                    inc(v.memset(AP(sb["xs_t"][:].tensor, 0,
                                    [[NC_DH * (TT + 3), 128], [TT + 3, NC_DH], [1, 3]]),
                                 0.0), "sV")
                mark(f"ctx_{tau}", "sV")


        def emit_winevac(tau, o):
            @sec("scalar")
            def _(sc, tau=tau, o=o):
                wait(sc, f"xzmm_{tau}_{o}")
                pst = ps["xz0"] if o % 2 == 0 else ps["xz1"]
                if o == 0:
                    wait(sc, f"ctx_{tau}")
                li = sc.copy(sb["xs_t"][:, o * (TT + 3) + 3:(o + 1) * (TT + 3)],
                             pst[:])
                inc(li, "sA")
                mark(f"xzev_{tau}_{o}", "sA")
                if o == 7:
                    mark(f"xsv_{tau}", "sA")

        def stage_win(tau):
            for o in range(16):
                @sec("tensor")
                def _(te, tau=tau, o=o):
                    if o == 0:
                        wait(te, f"xn2_{tau}")
                    if o < 2 and tau > 0:
                        wait(te, f"sp_{tau - 1}_{o + 6}")
                    if o >= 2 or tau > 0:
                        oo, tt2 = o - 2, tau
                        if oo < 0:
                            oo += 16
                            tt2 -= 1
                        wait(te, f"xzev_{tt2}_{oo}")
                    pst = ps["xz0"] if o % 2 == 0 else ps["xz1"]
                    for k in range(NC_DM):
                        li = te.matmul(pst[:], sb["w_in_sb"][:, k * 2048 + o * 128:
                                                             k * 2048 + o * 128 + 128],
                                       csl(sb["h1_t"], k), start=(k == 0),
                                       stop=(k == NC_DM - 1))
                    inc(li, "sP")
                    mark(f"xzmm_{tau}_{o}", "sP")

                if o >= 8:
                    @sec("scalar")
                    def _(sc, tau=tau, o=o):
                        wait(sc, f"xzmm_{tau}_{o}")
                        pst = ps["xz0"] if o % 2 == 0 else ps["xz1"]
                        if tau > 0 and o == 8:
                            wait(sc, f"g_{tau - 1}")
                        li = sc.activation(out=csl(sb["sz_t"], o - 8), in_=pst[:],
                                           func=AF.Silu)
                        inc(li, "sA")
                        mark(f"xzev_{tau}_{o}", "sA")
                        if o == 15:
                            mark(f"xsev_{tau}", "sA")
                elif tau == 0:
                    emit_winevac(0, o)


        def stage_tail(tau):
            for p in range(4):
                @sec("vector")
                def _(v, tau=tau, p=p):
                    if p == 0:
                        wait(v, f"xsv_{tau}")
                        if tau > 0:
                            wait(v, f"exp_{tau - 1}_7_3")
                    v.drain()
                    for j in range(4):
                        if j > 0:
                            v.drain()
                        for c in (2 * p, 2 * p + 1):
                            base = c * (TT + 3)
                            if j == 0:
                                li = v.tensor_scalar(
                                    out=csl(sb["scrB"], c),
                                    in0=sb["xs_t"][:, base:base + TT],
                                    scalar1=sb["convw_sb"][:, 4 * c:4 * c + 1],
                                    scalar2=vcol("convb", c),
                                    op0=ALU.mult, op1=ALU.add)
                            else:
                                li = v.scalar_tensor_tensor(
                                    out=csl(sb["scrB"], c),
                                    in0=sb["xs_t"][:, base + j:base + j + TT],
                                    scalar=sb["convw_sb"][:, 4 * c + j:4 * c + j + 1],
                                    in1=csl(sb["scrB"], c),
                                    op0=ALU.mult, op1=ALU.add)
                    inc(li, "sV")
                    mark(f"conv_{tau}_{p}", "sV")
                    if p == 3:
                        mark(f"conv_{tau}", "sV")

                @sec("scalar")
                def _(sc, tau=tau, p=p):
                    wait(sc, f"conv_{tau}_{p}")
                    if p == 0 and tau > 0:
                        wait(sc, f"gsp_{tau - 1}")
                    inc(sc.activation(out=sb["scrA"][:, 2 * p * TT:(2 * p + 2) * TT],
                                      in_=sb["scrB"][:, 2 * p * TT:(2 * p + 2) * TT],
                                      func=AF.Silu), "sA")
                    mark(f"u_{tau}_{p}", "sA")
                    if p == 3:
                        mark(f"u_{tau}", "sA")

                @sec("tensor")
                def _(te, tau=tau, p=p):
                    wait(te, f"u_{tau}_{p}")
                    if p == 0 and tau > 0:
                        wait(te, f"projev_{tau - 1}")
                    for k in (2 * p, 2 * p + 1):
                        te.matmul(ps["pdt"][:], sb["wxp_sb"][:, k * 160:k * 160 + 128],
                                  csl(sb["scrA"], k), start=(k == 0),
                                  stop=(k == NC_DH - 1))
                        li = te.matmul(ps["pbc"][:],
                                       sb["wxp_sb"][:, k * 160 + 128:k * 160 + 160],
                                       csl(sb["scrA"], k), start=(k == 0),
                                       stop=(k == NC_DH - 1))
                    if p == 3:
                        inc(li, "sP")
                        mark(f"projmm_{tau}", "sP")

            @sec("vector")
            def _(v, tau=tau):
                wait(v, f"projmm_{tau}")
                if tau > 0:
                    wait(v, f"prst_{tau - 1}")
                v.tensor_copy(sb["pdt_t"][:], ps["pdt"][:])
                inc(v.tensor_copy(sb["pbc_t"][:], ps["pbc"][:]), "sV")
                mark(f"projev_{tau}", "sV")

            @sec("gpsimd")
            def _(g, tau=tau):
                wait(g, f"projev_{tau}")
                dma(g, arp_i[tau:tau + 1, :].rearrange("o (p t) -> o p t", p=160)[:, 0:128, :],
                    sb["pdt_t"][:], q="dG")
                dma(g, arp_i[tau:tau + 1, :].rearrange("o (p t) -> o p t", p=160)[:, 128:160, :],
                    sb["pbc_t"][:], q="dG")
                mark(f"prst_{tau}", "dG")
                wait(g, f"prst_{tau}")
                g.collective_compute(
                    "AllReduce", ALU.add,
                    replica_groups=[[0, 1], [2, 3], [4, 5], [6, 7]],
                    ins=[arp_i[tau:tau + 1, :].opt()],
                    outs=[arp_o[tau:tau + 1, :].opt()],
                ).then_inc(sems["sC"], 1)
                bump("sC", 1)
                mark(f"ccp_{tau}", "sC")
                if tau > 0:
                    wait(g, f"wdtmm_{tau - 1}")
                    wait(g, f"bcfl_{tau - 1}_3")
                wait(g, f"ccp_{tau}")
                dma(g, sb["dt_sb"][:],
                    arp_o[tau:tau + 1, :].rearrange("o (p t) -> o p t", p=160)[:, 0:128, :],
                    q="dG")
                dma(g, sb["bc_sb"][:],
                    arp_o[tau:tau + 1, :].rearrange("o (p t) -> o p t", p=160)[:, 128:160, :],
                    q="dG")
                mark(f"dtbc_{tau}", "dG")


        def punit(tau, h, k, n):
            idx = h * 8 + k - n
            t2 = tau
            if idx < 0:
                idx += 32
                t2 -= 1
            return t2, idx // 8, idx % 8

        def hctree(v, tau, h, k):
            hcp = sb["HC0"]
            Hp = sb["H_t"] if k % 2 == 0 else sb["H1"]
            bcr = sb["bcrep0"] if h % 2 == 0 else sb["bcrep1"]
            v.drain()
            v.tensor_tensor(out=hcp[:], in0=Hp[:],
                            in1=bcr[:, 4 * TT:8 * TT], op=ALU.mult)
            v.drain()
            hv = AP(hcp[:].tensor, 0, [[4 * TT, 128], [1, TT], [TT, 4]])
            if h == 0:
                li = v.tensor_reduce(out=csl(sb["y_t"], k), in_=hv, axis=AXX,
                                     op=ALU.add)
            else:
                v.tensor_reduce(out=sb["murep"][:], in_=hv, axis=AXX,
                                op=ALU.add)
                v.drain()
                li = v.tensor_tensor(out=csl(sb["y_t"], k), in0=csl(sb["y_t"], k),
                                     in1=sb["murep"][:], op=ALU.add)
            inc(li, "sV")
            mark(f"tree_{tau}_{k}_{h}", "sV")

        def stage_scanblk(tau):
            for o in range(NC_DH):
                @sec("tensor")
                def _(te, tau=tau, o=o):
                    if o == 0:
                        wait(te, f"dtbc_{tau}")
                        if tau > 0:
                            wait(te, f"exp_{tau - 1}_7_3")
                    if o >= 2:
                        wait(te, f"sp_{tau}_{o - 2}")
                    pst = ps["dl0"] if o % 2 == 0 else ps["dl1"]
                    te.matmul(pst[:], sb["wdt_sb"][:, o * 128:(o + 1) * 128],
                              sb["dt_sb"][:], start=True, stop=True).then_inc(sems["sP"], 1)
                    bump("sP", 1)
                    mark(f"wdt_{tau}_{o}", "sP")
                    if o == NC_DH - 1:
                        mark(f"wdtmm_{tau}", "sP")

                @sec("scalar")
                def _(sc, tau=tau, o=o):
                    wait(sc, f"wdt_{tau}_{o}")
                    pst = ps["dl0"] if o % 2 == 0 else ps["dl1"]
                    sc.activation(out=csl(sb["xsq"], o % 2), in_=pst[:], func=AF.Exp,
                                  bias=vcol("bdt", o))
                    sc.drain()
                    sc.activation(out=csl(sb["delta_t"], o), in_=csl(sb["xsq"], o % 2),
                                  func=AF.Ln, bias=1.0).then_inc(sems["sA"], 1)
                    bump("sA", 1)
                    mark(f"sp_{tau}_{o}", "sA")
                    if o == NC_DH - 1:
                        mark(f"delta_{tau}", "sA")

            @sec("vector")
            def _(v, tau=tau):
                wait(v, f"delta_{tau}")
                if tau > 0:
                    wait(v, f"bw_{tau - 1}_7_3")
                v.drain()
                inc(v.tensor_tensor(out=sb["du_t"][:], in0=sb["delta_t"][:],
                                    in1=sb["scrA"][:], op=ALU.mult), "sV")
                mark(f"du_{tau}", "sV")

            for h in range(4):
                @sec("sync")
                def _(sy, tau=tau, h=h):
                    wait(sy, f"dtbc_{tau}")
                    bcr = sb["bcrep0"] if h % 2 == 0 else sb["bcrep1"]
                    qn = "dBC" if h % 2 == 0 else "dBC1"
                    if tau * 4 + h >= 2:
                        pt, ph = (tau, h - 2) if h >= 2 else (tau - 1, h + 2)
                        wait(sy, f"bw_{pt}_7_{ph}")
                        wait(sy, f"tree_{pt}_7_{ph}")
                    dma(sy, bcr[0:1, 0:4 * TT].rearrange("o (n t) -> o n t", n=4),
                        sb["bc_sb"][4 * h:4 * h + 4, :], q=qn)
                    dma(sy, bcr[0:1, 4 * TT:8 * TT].rearrange("o (n t) -> o n t", n=4),
                        sb["bc_sb"][16 + 4 * h:20 + 4 * h, :], q=qn)
                    nh = 1
                    step = 0
                    while nh < 128:
                        mark(f"bcq_{tau}_{h}_{step}", qn)
                        wait(sy, f"bcq_{tau}_{h}_{step}")
                        dma(sy, bcr[nh:2 * nh, :], bcr[0:nh, :], q=qn)
                        nh *= 2
                        step += 1
                    mark(f"bcfl_{tau}_{h}", qn)

                for k in range(NC_DH):
                    @sec("scalar")
                    def _(sc, tau=tau, h=h, k=k):
                        wait(sc, f"delta_{tau}")
                        p2t, p2h, p2k = punit(tau, h, k, 2)
                        if p2t >= 0:
                            wait(sc, f"scans_{p2t}_{p2k}_{p2h}")
                        jb = 4 * (k % 2)
                        for j in range(4):
                            n = 4 * h + j
                            li = sc.activation(
                                out=csl(sb["scrB"], jb + j), in_=csl(sb["delta_t"], k),
                                func=AF.Exp,
                                scale=sb["aneg"][:, k * NST + n:k * NST + n + 1])
                        inc(li, "sA")
                        mark(f"exp_{tau}_{k}_{h}", "sA")
                        p1t, p1h, p1k = punit(tau, h, k, 1)
                        if p1t >= 0:
                            wait(sc, f"scans_{p1t}_{p1k}_{p1h}")
                            Hp = sb["H_t"] if p1k % 2 == 0 else sb["H1"]
                            inc(sc.copy(
                                sb["lastst"][:, p1k * NST + 4 * p1h:
                                             p1k * NST + 4 * p1h + 4],
                                AP(Hp[:].tensor, TT - 1, [[4 * TT, 128], [TT, 4]])),
                                "sA")

                    if h == 1 and tau + 1 < NT:
                        emit_winevac(tau + 1, k)

                    @sec("gpsimd")
                    def _(g, tau=tau, h=h, k=k):
                        wait(g, f"du_{tau}")
                        wait(g, f"bcfl_{tau}_{h}")
                        p2t, p2h, p2k = punit(tau, h, k, 2)
                        if p2t == tau:
                            wait(g, f"scans_{p2t}_{p2k}_{p2h}")
                        elif (h, k) == (0, 1):
                            wait(g, f"scans_{tau}_0_0")
                        rep = AP(sb["du_t"][:].tensor, k * TT,
                                 [[NC_DH * TT, 128], [0, 4], [1, TT]])
                        bcr = sb["bcrep0"] if h % 2 == 0 else sb["bcrep1"]
                        scrCp = sb["scrC"] if k % 2 == 0 else sb["scrC1"]
                        g.tensor_tensor(out=scrCp[:].rearrange(
                                            "p (n t) -> p n t", n=4),
                                        in0=rep,
                                        in1=bcr[:, 0:4 * TT].rearrange(
                                            "p (n t) -> p n t", n=4),
                                        op=ALU.mult).then_inc(sems["sG"], 1)
                        bump("sG", 1)
                        mark(f"bw_{tau}_{k}_{h}", "sG")

                    @sec("vector")
                    def _(v, tau=tau, h=h, k=k):
                        wait(v, f"exp_{tau}_{k}_{h}")
                        wait(v, f"bw_{tau}_{k}_{h}")
                        jb = 4 * (k % 2)
                        Hp = sb["H_t"] if k % 2 == 0 else sb["H1"]
                        scrCp = sb["scrC"] if k % 2 == 0 else sb["scrC1"]
                        for j in range(4):
                            n = 4 * h + j
                            li = v.tensor_tensor_scan(
                                out=csl(Hp, j), data0=csl(sb["scrB"], jb + j),
                                data1=csl(scrCp, j),
                                initial=sb["lastst"][:, k * NST + n:k * NST + n + 1],
                                op0=ALU.mult, op1=ALU.add)
                        inc(li, "sV")
                        mark(f"scans_{tau}_{k}_{h}", "sV")
                        hctree(v, tau, h, k)

            @sec("vector")
            def _(v, tau=tau):
                v.drain()
                for k in range(NC_DH):
                    v.scalar_tensor_tensor(out=csl(sb["y_t"], k), in0=csl(sb["scrA"], k),
                                           scalar=vcol("dpar", k), in1=csl(sb["y_t"], k),
                                           op0=ALU.mult, op1=ALU.add)
                v.drain()
                for k in range(NC_DH):
                    li = v.tensor_tensor(out=csl(sb["scrA"], k), in0=csl(sb["y_t"], k),
                                         in1=csl(sb["sz_t"], k), op=ALU.mult)
                inc(li, "sV")
                mark(f"g_{tau}", "sV")

            @sec("sync")
            def _(sy, tau=tau):
                wait(sy, f"g_{tau}")
                dma(sy, g_buf[tau:tau + 1, :].rearrange("o (p c t) -> o p c t",
                                                        p=128, c=NC_DH),
                    sb["scrA"][:].rearrange("p (c t) -> p c t", c=NC_DH), q="dOut")
                mark(f"gsp_{tau}", "dOut")

            if DBG and tau == 0:
                @sec("sync")
                def _(sy, tau=tau):
                    wait(sy, f"g_{tau}")
                    dma(sy, dbg, sb["delta_t"][:])
                    mark("dbgw", "dS")


        for tau in range(NT):
            stage_front(tau)
            if tau >= 1:
                stage_scanblk(tau - 1)
            stage_win(tau)
            stage_tail(tau)
        stage_scanblk(NT - 1)
        actx.close()
        alloc("wout_sb", [128, NC_DH * DM])
        alloc("g_ld", [128, NC_DH * TT])
        alloc("pout_t", [128, NC_DM * TT])
        alloc("pout_u", [128, NC_DM * TT])
        alloc("qi_t", [128, NC_DM * TT], I32)
        alloc("qf_t", [128, NC_DM * TT])
        alloc("qb_t", [128, NC_DM * TT], BF16)
        alloc("wf8", [128, NC_DM * DM], I8)
        alloc("wfb", [128, NC_DM * DM], BF16)
        alloc("c2scr", [128, NC_DM * TT])
        alloc("ps_ld0", [128, NC_DM * TT])
        alloc("ps_ld1", [128, NC_DM * TT])
        alloc("r0_ld0", [128, NC_DM * TT])
        alloc("r0_ld1", [128, NC_DM * TT])

        mark_led("barrierB")

        def wait_barrier(eng):
            wait_led(eng, "barrierB")

        @sec("sync")
        def _(sy):
            wait_barrier(sy)
            dma(sy, sb["wout_sb"][:].rearrange("p (c o) -> p c o", c=NC_DH),
                wout_t[:, :].rearrange("(c p) o -> p c o", p=128), q="dW")
            dma(sy, sb["wf8"][:].rearrange("p (c o) -> p c o", c=NC_DM),
                wffn_t[:, :].rearrange("(c p) o -> p c o", p=128), q="dW")
            mark("bweights", "dW")

        @sec("vector")
        def _(v):
            wait_barrier(v)
            wait(v, "bweights")
            inc(v.tensor_copy(sb["wfb"][:], sb["wf8"][:]), "sV")
            mark("wfb_done", "sV")

        @sec("tensor")
        def _(te):
            wait_barrier(te)
            te.sem_inc(sems["sP"], 1)
            bump("sP", 1)
            mark("te_bar", "sP")

        @sec("scalar")
        def _(sc):
            wait_barrier(sc)
            sc.sem_inc(sems["sA"], 1)
            bump("sA", 1)
            mark("sc_bar", "sA")

        @sec("gpsimd")
        def _(g):
            wait_barrier(g)
            g.sem_inc(sems["sG"], 1)
            bump("sG", 1)
            mark("g_bar", "sG")

        def stage_B(tau):
            @sec("sync")
            def _(sy, tau=tau):
                if tau == 0:
                    wait(sy, f"gsp_{NT - 1}")
                else:
                    wait(sy, f"poutmm_{tau - 1}")
                dma(sy, sb["g_ld"][:].rearrange("p (c t) -> p c t", c=NC_DH),
                    g_buf[tau:tau + 1, :].rearrange("o (p c t) -> o p c t",
                                                    p=128, c=NC_DH), q="dX")
                mark(f"gld_{tau}", "dX")

            for o in range(NC_DM):
                @sec("tensor")
                def _(te, tau=tau, o=o):
                    if o == 0:
                        wait(te, f"gld_{tau}")
                        wait(te, "bweights")
                    if o >= 2 or tau > 0:
                        oo, tt2 = o - 2, tau
                        if oo < 0:
                            oo += NC_DM
                            tt2 -= 1
                        wait(te, f"poutev_{tt2}_{oo}")
                    pst = ps["xz0"] if o % 2 == 0 else ps["xz1"]
                    for k in range(NC_DH):
                        li = te.matmul(pst[:], sb["wout_sb"][:, k * DM + o * 128:
                                                             k * DM + o * 128 + 128],
                                       csl(sb["g_ld"], k), start=(k == 0),
                                       stop=(k == NC_DH - 1))
                    inc(li, "sP")
                    mark(f"poutm_{tau}_{o}", "sP")
                    if o == NC_DM - 1:
                        mark(f"poutmm_{tau}", "sP")

                @sec("vector")
                def _(v, tau=tau, o=o):
                    wait(v, f"poutm_{tau}_{o}")
                    if tau > 0 and o == 0:
                        wait(v, f"post_{tau - 1}")
                    pst = ps["xz0"] if o % 2 == 0 else ps["xz1"]
                    inc(v.tensor_copy(csl(sb["pout_t"], o), pst[:]), "sV")
                    mark(f"poutev_{tau}_{o}", "sV")
                    if o == NC_DM - 1:
                        mark(f"poutevall_{tau}", "sV")

            @sec("gpsimd")
            def _(g, tau=tau):
                wait(g, f"poutevall_{tau}")
                dma(g, aro_i[tau:tau + 1, :].rearrange("o (p c t) -> o p c t",
                                                       p=128, c=NC_DM),
                    sb["pout_t"][:].rearrange("p (c t) -> p c t", c=NC_DM), q="dG")
                mark(f"post_{tau}", "dG")
                wait(g, f"post_{tau}")
                g.collective_compute(
                    "AllReduce", ALU.add,
                    replica_groups=[[0, 1], [2, 3], [4, 5], [6, 7]],
                    ins=[aro_i[tau:tau + 1, :].opt()],
                    outs=[aro_o[tau:tau + 1, :].opt()],
                ).then_inc(sems["sC"], 1)
                bump("sC", 1)
                mark(f"cco_{tau}", "sC")


        def stage_C1(tau):
            pl = "ps_ld0" if tau % 2 == 0 else "ps_ld1"
            rl = "r0_ld0" if tau % 2 == 0 else "r0_ld1"

            @sec("sync")
            def _(sy, tau=tau, pl=pl):
                wait(sy, f"cco_{tau}")
                if tau > 1:
                    wait(sy, f"x2c_{tau - 2}")
                    wait(sy, f"mx_{tau - 2}")
                    wait(sy, f"stats3_{tau - 2}")
                    wait(sy, f"s3sq_{tau - 2}_{NC_DM - 1}")
                    wait(sy, f"x2sp_{tau - 2}")
                qn = "dMsc" if tau % 2 == 0 else "dM1"
                dma(sy, sb[pl][:].rearrange("p (c t) -> p c t", c=NC_DM),
                    aro_o[tau:tau + 1, :].rearrange("o (p c t) -> o p c t",
                                                    p=128, c=NC_DM), q=qn)
                mark(f"psld_{tau}", qn)

            @sec("sync")
            def _(sy, tau=tau, rl=rl):
                if tau > 1:
                    wait(sy, f"mx_{tau - 2}")
                qn = "dR" if tau % 2 == 0 else "dM0"
                dma(sy, sb[rl][:].rearrange("p (c t) -> p c t", c=NC_DM),
                    r0_buf[tau:tau + 1, :].rearrange("o (p c t) -> o p c t",
                                                     p=128, c=NC_DM), q=qn)
                mark(f"r0ld_{tau}", qn)

            @sec("vector")
            def _(v, tau=tau, pl=pl, rl=rl):
                wait(v, f"psld_{tau}")
                wait(v, f"r0ld_{tau}")
                v.drain()
                for c in range(NC_DM):
                    li = v.scalar_tensor_tensor(out=csl(sb[pl], c),
                                                in0=csl(sb[pl], c),
                                                scalar=vcol("gap", c), in1=csl(sb[rl], c),
                                                op0=ALU.mult, op1=ALU.add)
                inc(li, "sV")
                mark(f"x2c_{tau}", "sV")

            @sec("sync")
            def _(sy, tau=tau, pl=pl):
                wait(sy, f"x2c_{tau}")
                qn = "dOut" if tau % 2 == 0 else "dS"
                dma(sy, x2_buf[tau:tau + 1, :].rearrange("o (p c t) -> o p c t",
                                                         p=128, c=NC_DM),
                    sb[pl][:].rearrange("p (c t) -> p c t", c=NC_DM), q=qn)
                mark(f"x2sp_{tau}", qn)

            def _scf3(sc, tau=tau):
                if tau == 0:
                    wait_barrier(sc)
                wait(sc, f"x2c_{tau}")

            def _tef3(te, tau=tau):
                wait(te, f"x2c_{tau}")
                if tau > 0:
                    wait(te, f"p3_{tau - 1}")

            stats_round("s3", tau, pl, sc_first=_scf3, te_first=_tef3)
            alias(f"stats3_{tau}", f"stats_s3_{tau}")

            @sec("vector")
            def _(v, tau=tau):
                wait(v, f"stats3_{tau}")
                if tau > 0:
                    wait(v, f"stsp_{tau - 1}")
                stats_post(v, tau, "ln3")

            @sec("scalar")
            def _(sc, tau=tau):
                wait(sc, f"vpre_ln3_{tau}")
                sc.activation(out=sb["stt2"][:], in_=sb["stt1"][:], func=AF.Ln)
                sc.drain()
                sc.activation(out=sb["r_a"][:], in_=sb["stt2"][:], func=AF.Exp,
                              scale=-0.5).then_inc(sems["sA"], 1)
                bump("sA", 1)
                mark(f"rstd_ln3_{tau}", "sA")

            @sec("vector")
            def _(v, tau=tau):
                wait(v, f"rstd_ln3_{tau}")
                newton(v, tau, "ln3")
            alias(f"p3_{tau}", f"post_ln3_{tau}")

            @sec("sync")
            def _(sy, tau=tau):
                wait(sy, f"p3_{tau}")
                dma(sy, st_buf[tau:tau + 1, 0:TT], sb["mu_a"][:], q="dSt")
                dma(sy, st_buf[tau:tau + 1, TT:2 * TT], sb["r_a"][:], q="dSt")
                mark(f"stsp_{tau}", "dSt")

            @sec("tensor")
            def _(te, tau=tau):
                wait(te, f"p3_{tau}")
                if tau > 0:
                    wait(te, f"mx_{tau - 1}")
                bcast(te, tau, "ln3")

            @sec("vector")
            def _(v, tau=tau, pl=pl, rl=rl):
                wait(v, f"bc_ln3_{tau}")
                rep_evac(v, tau, "ln3")
                wait(v, f"x2sp_{tau}")
                v.drain()
                for c in range(NC_DM):
                    v.tensor_tensor(out=csl(sb[rl], c), in0=csl(sb[pl], c),
                                    in1=sb["murep"][:], op=ALU.subtract)
                v.drain()
                for c in range(NC_DM):
                    v.tensor_tensor(out=csl(sb[rl], c), in0=csl(sb[rl], c),
                                    in1=sb["rrep"][:], op=ALU.mult)
                v.drain()
                for c in range(NC_DM):
                    v.tensor_scalar(out=csl(sb[rl], c), in0=csl(sb[rl], c),
                                    scalar1=vcol("cb1", c), scalar2=vcol("sbv", c),
                                    op0=ALU.mult, op1=ALU.add)
                v.drain()
                v.reduce_max(out=sb["rrep"][:, 0:1], in_=sb[rl][:], axis=AXX,
                             apply_absolute_value=True)
                v.drain()
                inc(v.tensor_tensor(out=sb["mxacc"][:], in0=sb["mxacc"][:],
                                    in1=sb["rrep"][:, 0:1], op=ALU.max), "sV")
                mark(f"mx_{tau}", "sV")


        for tau in range(NT + 1):
            if tau < NT:
                stage_B(tau)
            if tau >= 1:
                stage_C1(tau - 1)

        @sec("sync")
        def _(sy):
            wait(sy, f"mx_{NT - 1}")
            dma(sy, sb["mxrow"][0:1, 0:128].rearrange("o (a b) -> o a b", a=128),
                sb["mxacc"][:], q="dSt")
            mark("mxfl", "dSt")

        @sec("vector")
        def _(v):
            wait(v, "mxfl")
            v.drain()
            inc(v.reduce_max(out=sb["mxrow"][0:1, 128:129], in_=sb["mxrow"][0:1, 0:128],
                             axis=AXX, apply_absolute_value=False), "sV")
            mark("mxloc", "sV")

        @sec("gpsimd")
        def _(g):
            wait(g, "mxloc")
            dma(g, mx_i, sb["mxrow"][0:1, 128:129], q="dG")
            mark("mxst", "dG")
            wait(g, "mxst")
            g.collective_compute(
                "AllReduce", ALU.max,
                replica_groups=[list(range(8))],
                ins=[mx_i.opt()], outs=[mx_o.opt()],
            ).then_inc(sems["sC"], 1)
            bump("sC", 1)
            mark("ccmx", "sC")
            wait(g, "ccmx")
            dma(g, sb["mxrow"][0:1, 129:130], mx_o, q="dG")
            mark("mxg", "dG")

        @sec("vector")
        def _(v):
            wait(v, "mxg")
            v.drain()
            v.tensor_scalar(out=sb["mxrow"][0:1, 129:130], in0=sb["mxrow"][0:1, 129:130],
                            scalar1=1.0 / 127.0, scalar2=EPS, op0=ALU.mult, op1=ALU.max)
            v.drain()
            inc(v.reciprocal(sb["mxrow"][0:1, 128:129], sb["mxrow"][0:1, 129:130]), "sV")
            mark("xscv", "sV")

        @sec("tensor")
        def _(te):
            wait(te, "xscv")
            te.matmul(ps["bc"][:, 0:1], sb["onesr"][:], sb["mxrow"][0:1, 128:129],
                      start=True, stop=True)
            te.matmul(ps["bc"][:, 1:2], sb["onesr"][:], sb["mxrow"][0:1, 129:130],
                      start=True, stop=True).then_inc(sems["sP"], 1)
            bump("sP", 1)
            mark("xscb", "sP")

        @sec("vector")
        def _(v):
            wait(v, "xscb")
            V = sb["vecs"]
            sl = lambda nm: V[:, VOFF[nm]:VOFF[nm] + NC_DM]
            v.drain()
            v.tensor_copy(sb["mxacc"][:], ps["bc"][:, 0:1])
            v.tensor_copy(sb["ones1"][:], ps["bc"][:, 1:2])
            v.drain()
            v.tensor_scalar(out=sl("cb1"), in0=sl("cb1"), scalar1=sb["mxacc"][:],
                            scalar2=None, op0=ALU.mult)
            v.tensor_scalar(out=sl("sbv"), in0=sl("sbv"), scalar1=sb["mxacc"][:],
                            scalar2=None, op0=ALU.mult)
            inc(v.tensor_scalar(out=sl("gbw"), in0=sl("gbw"), scalar1=sb["ones1"][:],
                                scalar2=None, op0=ALU.mult), "sV")
            mark("c2const", "sV")

        for tau in range(NT):
            pp = "pout_t" if tau % 2 == 0 else "pout_u"

            @sec("sync")
            def _(sy, tau=tau, pp=pp):
                if tau == 0:
                    wait(sy, "c2const")
                    wait(sy, "mxfl")
                else:
                    wait(sy, f"bc_c2_{tau - 1}_{tau - 1}")
                if tau > 1:
                    wait(sy, f"x3_{tau - 2}")
                dma(sy, sb[pp][:].rearrange("p (c t) -> p c t", c=NC_DM),
                    x2_buf[tau:tau + 1, :].rearrange("o (p c t) -> o p c t",
                                                     p=128, c=NC_DM), q="dX")
                dma(sy, sb["mu_a"][:], st_buf[tau:tau + 1, 0:TT], q="dX")
                dma(sy, sb["r_a"][:], st_buf[tau:tau + 1, TT:2 * TT], q="dX")
                mark(f"x2ld_{tau}", "dX")

            @sec("tensor")
            def _(te, tau=tau):
                wait(te, f"x2ld_{tau}")
                if tau == 0:
                    wait(te, "c2const")
                else:
                    wait(te, f"rep_c2r_{tau - 1}_{tau - 1}")
                bcast(te, tau, f"c2_{tau}")

            @sec("vector")
            def _(v, tau=tau, pp=pp):
                wait(v, f"bc_c2_{tau}_{tau}")
                rep_evac(v, tau, f"c2r_{tau}")
                wait(v, "c2const")
                if tau > 0:
                    wait(v, f"fmm_{tau - 1}_{NC_DM - 1}")
                    wait(v, f"x3sp_{tau - 1}")
                v.drain()
                for c in range(NC_DM):
                    v.tensor_tensor(out=csl(sb["g_ld"], c), in0=csl(sb[pp], c),
                                    in1=sb["murep"][:], op=ALU.subtract)
                v.drain()
                for c in range(NC_DM):
                    v.tensor_tensor(out=csl(sb["g_ld"], c), in0=csl(sb["g_ld"], c),
                                    in1=sb["rrep"][:], op=ALU.mult)
                v.drain()
                for c in range(NC_DM):
                    v.tensor_scalar(out=csl(sb["g_ld"], c), in0=csl(sb["g_ld"], c),
                                    scalar1=vcol("cb1", c), scalar2=vcol("sbv", c),
                                    op0=ALU.mult, op1=ALU.add)
                v.drain()
                v.tensor_copy(sb["qi_t"][:], sb["g_ld"][:])
                v.drain()
                v.tensor_copy(sb["qf_t"][:], sb["qi_t"][:])
                v.drain()
                v.tensor_tensor(out=sb["c2scr"][:], in0=sb["g_ld"][:], in1=sb["qf_t"][:],
                                op=ALU.subtract)
                v.drain()
                v.scalar_tensor_tensor(out=sb["qf_t"][:], in0=sb["c2scr"][:], scalar=0.5,
                                       in1=sb["qf_t"][:], op0=ALU.is_ge, op1=ALU.add)
                v.drain()
                v.scalar_tensor_tensor(out=sb["qf_t"][:], in0=sb["c2scr"][:], scalar=-0.5,
                                       in1=sb["qf_t"][:], op0=ALU.is_ge, op1=ALU.add)
                v.drain()
                inc(v.tensor_scalar(out=sb["qb_t"][:], in0=sb["qf_t"][:],
                                    scalar1=-1.0, scalar2=127.0, op0=ALU.add,
                                    op1=ALU.min), "sV")
                mark(f"vq_{tau}", "sV")

            for o in range(NC_DM):
                @sec("tensor")
                def _(te, tau=tau, o=o):
                    if o == 0:
                        wait(te, f"vq_{tau}")
                        if tau == 0:
                            wait(te, "wfb_done")
                    if o >= 2 or tau > 0:
                        oo, tt2 = o - 2, tau
                        if oo < 0:
                            oo += NC_DM
                            tt2 -= 1
                        wait(te, f"fev_{tt2}_{oo}")
                    pst = ps["xz0"] if o % 2 == 0 else ps["xz1"]
                    for k in range(NC_DM):
                        li = te.matmul(pst[:], sb["wfb"][:, k * DM + o * 128:
                                                         k * DM + o * 128 + 128],
                                       csl(sb["qb_t"], k), start=(k == 0),
                                       stop=(k == NC_DM - 1))
                    inc(li, "sP")
                    mark(f"fmm_{tau}_{o}", "sP")

                @sec("vector")
                def _(v, tau=tau, o=o, pp=pp):
                    wait(v, f"fmm_{tau}_{o}")
                    if tau > 0 and o == 0:
                        wait(v, f"x3sp_{tau - 1}")
                    pst = ps["xz0"] if o % 2 == 0 else ps["xz1"]
                    v.drain()
                    v.tensor_scalar(out=csl(sb["qf_t"], o), in0=pst[:],
                                    scalar1=vcol("gbw", o), scalar2=vcol("gbb", o),
                                    op0=ALU.mult, op1=ALU.add)
                    v.drain()
                    inc(v.tensor_tensor(out=csl(sb["qf_t"], o), in0=csl(sb["qf_t"], o),
                                        in1=csl(sb[pp], o), op=ALU.add), "sV")
                    mark(f"fev_{tau}_{o}", "sV")
                    if o == NC_DM - 1:
                        mark(f"x3_{tau}", "sV")

            @sec("sync")
            def _(sy, tau=tau):
                wait(sy, f"x3_{tau}")
                dma(sy, xout[:, tsl(tau)].rearrange("(c p) t -> p c t", p=128),
                    sb["qf_t"][:].rearrange("p (c t) -> p c t", c=NC_DM), q="dOut")
                mark(f"x3sp_{tau}", "dOut")

        mark_led("final")

        @sec("sync")
        def _(sy):
            wait_led(sy, "final", only={"dW", "dX", "dR", "dBC", "dBC1", "dOut",
                                        "dMsc", "dSt", "dM0", "dM1", "dS"})

        @sec("gpsimd")
        def _(g):
            wait_led(g, "final", only={"dG"})

        if pass2:
            for engname in ["sync", "tensor", "vector", "scalar", "gpsimd"]:
                fns = thunks[engname]

                @getattr(block, engname)
                def _(eng, fns=fns):
                    for fn in fns:
                        fn(eng)

    return nc, ev


def kernel(L0_attn_x, L0_attn_ln_gamma, L0_attn_ln_beta, L0_attn_W_in,
           L0_attn_conv_w, L0_attn_conv_b, L0_attn_W_xproj, L0_attn_W_dt,
           L0_attn_b_dt, L0_attn_A_log, L0_attn_D_param, L0_attn_W_out,
           L0_ffn_weight_int8, L0_ffn_weight_scale, L0_ffn_bias, L0_c, L0_wmod):
    _, ev1 = build_program()
    nc, _ = build_program(ev1)

    def vec_pack(v):
        return np.ascontiguousarray(np.asarray(v, np.float32).reshape(NC_DM, 128).T)

    x = np.asarray(L0_attn_x, np.float32)
    W_in = np.asarray(L0_attn_W_in, np.float32)
    conv_w = np.asarray(L0_attn_conv_w, np.float32)
    conv_b = np.asarray(L0_attn_conv_b, np.float32)
    W_xp = np.asarray(L0_attn_W_xproj, np.float32)
    W_dt = np.asarray(L0_attn_W_dt, np.float32)
    b_dt = np.asarray(L0_attn_b_dt, np.float32)
    A_log = np.asarray(L0_attn_A_log, np.float32)
    D_par = np.asarray(L0_attn_D_param, np.float32)
    W_out = np.asarray(L0_attn_W_out, np.float32)
    w8 = np.asarray(L0_ffn_weight_int8, np.int8)
    wscale = np.asarray(L0_ffn_weight_scale, np.float32)
    fb = np.asarray(L0_ffn_bias, np.float32)
    cc = np.asarray(L0_c, np.float32)
    wmod = np.ascontiguousarray(np.asarray(L0_wmod, np.float32))

    in_maps = []
    for core in range(8):
        b = core // 2
        hh = core % 2
        dsl = slice(hh * DH, (hh + 1) * DH)
        w_in_sl = np.concatenate([W_in[hh * DH:(hh + 1) * DH],
                                  W_in[2048 + hh * DH:2048 + (hh + 1) * DH]], 0)
        cw = conv_w[dsl, 0, :]
        m = dict(
            xin=np.ascontiguousarray(x[b].T),
            w_in_t=np.ascontiguousarray(w_in_sl.T),
            wxp_t=np.ascontiguousarray(W_xp[:, dsl].T),
            wdt_t=np.ascontiguousarray(W_dt[dsl].T),
            wout_t=np.ascontiguousarray(W_out[:, dsl].T),
            wmod=wmod,
            wffn_t=np.ascontiguousarray(w8.T),
            convw=np.ascontiguousarray(
                cw.reshape(NC_DH, 128, 4).transpose(1, 0, 2).reshape(128, NC_DH * 4)),
            convb=vec_pack(conv_b[dsl]),
            bdt=vec_pack(b_dt[dsl]),
            alog=np.ascontiguousarray(
                A_log[dsl].reshape(NC_DH, 128, NST).transpose(1, 0, 2)
                .reshape(128, NC_DH * NST)),
            dpar=vec_pack(D_par[dsl]),
            lngam=vec_pack(np.asarray(L0_attn_ln_gamma, np.float32)),
            lnbet=vec_pack(np.asarray(L0_attn_ln_beta, np.float32)),
            wsc=vec_pack(wscale[:, 0]),
            fbias=vec_pack(fb),
            cvec=vec_pack(cc[b]),
        )
        in_maps.append(m)

    res = run_bass_kernel_spmd(nc, in_maps, core_ids=list(range(8)))
    out = np.empty((4, S, DM), np.float32)
    for b in range(4):
        out[b] = res.results[2 * b]["xout"].T
    return out


# version 18
# speedup vs baseline: 1.2141x; 1.0004x over previous
import numpy as np
import concourse.bass as bass
import concourse.mybir as mybir
from concourse.bass import AP
from concourse.bass_utils import run_bass_kernel_spmd

F32 = mybir.dt.float32
BF16 = mybir.dt.bfloat16
I8 = mybir.dt.int8
I32 = mybir.dt.int32
AF = mybir.ActivationFunctionType
ALU = mybir.AluOpType
AXX = mybir.AxisListType.X

DM = 1024
DH = 1024
NC_DM = DM // 128
NC_DH = DH // 128
NST = 16
S = 8192
TT = 256
NT = S // TT
EPS = 1e-5

DBG = False


class _DummyInstr:
    def then_inc(self, *a, **k):
        return self


class _DummyEngine:
    def __getattr__(self, name):
        def f(*a, **k):
            return _DummyInstr()
        return f


def build_program(ev_table=None):
    pass2 = ev_table is not None
    nc = bass.Bass("TRN2", target_bir_lowering=False)

    xin = nc.dram_tensor("xin", [DM, S], F32, kind="ExternalInput").ap()
    w_in_t = nc.dram_tensor("w_in_t", [DM, 2 * DH], F32, kind="ExternalInput").ap()
    wxp_t = nc.dram_tensor("wxp_t", [DH, 160], F32, kind="ExternalInput").ap()
    wdt_t = nc.dram_tensor("wdt_t", [128, DH], F32, kind="ExternalInput").ap()
    wout_t = nc.dram_tensor("wout_t", [DH, DM], F32, kind="ExternalInput").ap()
    wmod = nc.dram_tensor("wmod", [DM, 6 * DM], F32, kind="ExternalInput").ap()
    wffn_t = nc.dram_tensor("wffn_t", [DM, DM], I8, kind="ExternalInput").ap()
    convw = nc.dram_tensor("convw", [128, NC_DH * 4], F32, kind="ExternalInput").ap()
    convb = nc.dram_tensor("convb", [128, NC_DH], F32, kind="ExternalInput").ap()
    bdt = nc.dram_tensor("bdt", [128, NC_DH], F32, kind="ExternalInput").ap()
    alog = nc.dram_tensor("alog", [128, NC_DH * NST], F32, kind="ExternalInput").ap()
    dpar = nc.dram_tensor("dpar", [128, NC_DH], F32, kind="ExternalInput").ap()
    lngam = nc.dram_tensor("lngam", [128, NC_DM], F32, kind="ExternalInput").ap()
    lnbet = nc.dram_tensor("lnbet", [128, NC_DM], F32, kind="ExternalInput").ap()
    wsc = nc.dram_tensor("wsc", [128, NC_DM], F32, kind="ExternalInput").ap()
    fbias = nc.dram_tensor("fbias", [128, NC_DM], F32, kind="ExternalInput").ap()
    cvec = nc.dram_tensor("cvec", [128, NC_DM], F32, kind="ExternalInput").ap()

    xout = nc.dram_tensor("xout", [DM, S], F32, kind="ExternalOutput").ap()
    dbg = nc.dram_tensor("dbg", [128, 8 * TT], F32, kind="ExternalOutput").ap() if DBG else None

    g_buf = nc.dram_tensor("g_buf", [NT, DH * TT], F32).ap()
    r0_buf = nc.dram_tensor("r0_buf", [NT, DM * TT], F32).ap()
    x2_buf = nc.dram_tensor("x2_buf", [NT, DM * TT], F32).ap()
    arp_i = nc.dram_tensor("arp_i", [NT, 160 * TT], F32).ap()
    arp_o = nc.dram_tensor("arp_o", [NT, 160 * TT], F32).ap()
    aro_i = nc.dram_tensor("aro_i", [NT, DM * TT], F32).ap()
    aro_o = nc.dram_tensor("aro_o", [NT, DM * TT], F32).ap()
    mx_i = nc.dram_tensor("mx_i", [1, 1], F32).ap()
    mx_o = nc.dram_tensor("mx_o", [1, 1], F32).ap()
    st_buf = nc.dram_tensor("st_buf", [NT, 2 * TT], F32).ap()

    from contextlib import ExitStack

    led = {}
    ev = {}

    def bump(sem_name, amt):
        if pass2:
            return 0
        led[sem_name] = led.get(sem_name, 0) + amt
        return led[sem_name]

    def mark(name, sem_name):
        if pass2:
            return
        ev[name] = (sem_name, led.get(sem_name, 0))

    def alias(dst, src):
        if pass2:
            return
        ev[dst] = ev[src]

    def mark_led(name):
        if pass2:
            return
        ev[name] = dict(led)

    def wait(eng, name):
        tbl = ev_table or ev
        if not pass2 and name not in tbl:
            return
        s, v = tbl[name]
        if v > 0:
            eng.wait_ge(sems[s], v)

    def wait_led(eng, name, only=None):
        for snm, val in (ev_table or ev)[name].items():
            if val > 0 and (only is None or snm in only):
                eng.wait_ge(sems[snm], val)

    sems = {}


    with ExitStack() as ctx:
        for sname in ["dS", "dG", "dM0", "dM1", "dW", "dX", "dR", "dBC", "dBC1",
                      "dOut", "dMsc", "dSt", "sP", "sV", "sA", "sG", "sC"]:
            sems[sname] = ctx.enter_context(nc.semaphore(sname))

        def inc(instr, sname):
            instr.then_inc(sems[sname], 1)
            return bump(sname, 1)

        def dma(eng, out, in_, q="dS"):
            eng.dma_start(out=out, in_=in_).then_inc(sems[q], 16)
            return bump(q, 16)

        sb = {}
        actx = ctx.enter_context(ExitStack())

        def alloc(name, shape, dtype=F32, scope=None):
            tgt = scope if scope is not None else ctx
            sb[name] = tgt.enter_context(nc.sbuf_tensor(name, shape, dtype))
            return sb[name]

        vecs = alloc("vecs", [128, 16 * NC_DM])
        aneg = alloc("aneg", [128, NC_DH * NST])
        ones1 = alloc("ones1", [128, 1])
        onesr = alloc("onesr", [1, 128])
        lastst = alloc("lastst", [128, NC_DH * NST])

        VOFF = {}
        for i, nm in enumerate(["convb", "bdt", "dpar", "lngam", "lnbet", "wsc",
                                "fbias", "sa", "ca1", "gap", "sbv", "cb1", "gbp",
                                "gbw", "gbb", "cvec"]):
            VOFF[nm] = i * NC_DM

        def vcol(nm, c):
            return sb["vecs"][:, VOFF[nm] + c:VOFF[nm] + c + 1]

        mu_a = alloc("mu_a", [1, TT])
        r_a = alloc("r_a", [1, TT])
        stt1 = alloc("stt1", [1, TT])
        stt2 = alloc("stt2", [1, TT])
        murep = alloc("murep", [128, TT])
        rrep = alloc("rrep", [128, TT])
        mxacc = alloc("mxacc", [128, 1])
        mxrow = alloc("mxrow", [1, 130])
        xsq = alloc("xsq", [128, 2 * TT])

        alloc("w_in_sb", [128, NC_DM * 2 * DH], scope=actx)
        alloc("wxp_sb", [128, NC_DH * 160], scope=actx)
        alloc("wdt_sb", [128, DH], scope=actx)
        alloc("convw_sb", [128, NC_DH * 4], scope=actx)
        alloc("x_t", [128, NC_DM * TT], scope=actx)
        alloc("h1_t", [128, NC_DM * TT], scope=actx)
        alloc("r0_t", [128, NC_DM * TT], scope=actx)
        sb["xn_t"] = sb["r0_t"]
        alloc("xs_t", [128, NC_DH * (TT + 3)], scope=actx)
        alloc("scrA", [128, NC_DH * TT], scope=actx)
        alloc("scrB", [128, 8 * TT], scope=actx)
        alloc("scrC", [128, 4 * TT], scope=actx)
        alloc("delta_t", [128, NC_DH * TT], scope=actx)
        alloc("du_t", [128, NC_DH * TT], scope=actx)
        alloc("HC0", [128, 4 * TT], scope=actx)
        alloc("HC1", [128, 4 * TT], scope=actx)
        alloc("sz_t", [128, NC_DH * TT], scope=actx)
        alloc("H_t", [128, 4 * TT], scope=actx)
        alloc("H1", [128, 4 * TT], scope=actx)
        alloc("scrC1", [128, 4 * TT], scope=actx)
        alloc("y_t", [128, NC_DH * TT], scope=actx)
        alloc("pdt_t", [128, TT], scope=actx)
        alloc("pbc_t", [32, TT], scope=actx)
        alloc("dt_sb", [128, TT], scope=actx)
        alloc("bc_sb", [32, TT], scope=actx)
        s0ctx = actx.enter_context(ExitStack())
        alloc("cs_t", [128, NC_DM], scope=s0ctx)
        alloc("modrow", [1, 1024], scope=s0ctx)
        alloc("wm0", [128, 512], scope=s0ctx)
        alloc("wm1", [128, 512], scope=s0ctx)

        ps = {}
        for nm, shape in [("smu", [1, TT]), ("sm2", [1, TT]), ("bc", [128, 2 * TT]),
                          ("xz0", [128, TT]), ("xz1", [128, TT]),
                          ("pdt", [128, TT]), ("pbc", [32, TT])]:
            ps[nm] = ctx.enter_context(nc.psum_tensor("ps_" + nm, shape, F32))
        ps["mu"] = ps["smu"][0:1, :]
        ps["m2"] = ps["sm2"][0:1, :]
        ps["mod"] = ps["bc"][0:1, 0:512]
        ps["dl0"] = ps["xz0"]
        ps["dl1"] = ps["xz1"]

        block = ctx.enter_context(nc.Block())
        thunks = {k: [] for k in ["sync", "tensor", "vector", "scalar", "gpsimd"]}
        _dummy = _DummyEngine()

        def sec(engname):
            if not pass2:
                def dec(fn):
                    fn(_dummy)
                return dec

            def dec(fn):
                thunks[engname].append(fn)
            return dec

        @sec("sync")
        def _(sy):
            dma(sy, sb["w_in_sb"][:].rearrange("p (c o) -> p c o", c=NC_DM),
                w_in_t[:, :].rearrange("(c p) o -> p c o", p=128), q="dW")
            dma(sy, sb["wxp_sb"][:].rearrange("p (c o) -> p c o", c=NC_DH),
                wxp_t[:, :].rearrange("(c p) o -> p c o", p=128), q="dW")
            dma(sy, sb["wdt_sb"][:], wdt_t, q="dW")
            dma(sy, sb["convw_sb"][:], convw, q="dW")
            dma(sy, sb["aneg"][:], alog, q="dW")
            for nm, src in [("convb", convb), ("bdt", bdt), ("dpar", dpar),
                            ("lngam", lngam), ("lnbet", lnbet), ("wsc", wsc),
                            ("fbias", fbias), ("cvec", cvec)]:
                dma(sy, sb["vecs"][:, VOFF[nm]:VOFF[nm] + NC_DM], src, q="dW")
            mark("weights", "dW")

        @sec("vector")
        def _(v):
            v.memset(sb["ones1"][:], 1.0)
            v.memset(sb["onesr"][:], 1.0)
            v.memset(sb["lastst"][:], 0.0)
            wait(v, "weights")
            inc(v.memset(sb["mxacc"][:], 0.0), "sV")
            mark("v_setup", "sV")

        @sec("scalar")
        def _(sc):
            wait(sc, "weights")
            sc.activation(out=sb["cs_t"][:], in_=sb["vecs"][:, VOFF["cvec"]:VOFF["cvec"] + NC_DM],
                          func=AF.Silu)
            inc(sc.activation(out=sb["aneg"][:], in_=sb["aneg"][:], func=AF.Exp), "sA")
            mark("a_setup", "sA")

        @sec("vector")
        def _(v):
            wait(v, "a_setup")
            inc(v.tensor_scalar(out=sb["aneg"][:], in0=sb["aneg"][:], scalar1=-1.0,
                                scalar2=None, op0=ALU.mult), "sV")
            mark("aneg_done", "sV")

        NJ = 6 * DM // 512
        QNAMES = ["sa", "ca1", "gap", "sbv", "cb1", "gbp"]
        for j in range(NJ):
            for k in range(NC_DM):
                idx = j * NC_DM + k

                @sec("sync")
                def _(sy, j=j, k=k, idx=idx):
                    if idx == 0:
                        wait(sy, "a_setup")
                    if idx >= 2:
                        wait(sy, f"modmm_{idx - 2}")
                    wm = [sb["wm0"], sb["wm1"]][idx % 2]
                    dma(sy, wm[:], wmod[k * 128:(k + 1) * 128, j * 512:(j + 1) * 512],
                        q="dM%d" % (idx % 2))
                    mark(f"modld_{idx}", "dM%d" % (idx % 2))

                @sec("tensor")
                def _(te, j=j, k=k, idx=idx):
                    wm = [sb["wm0"], sb["wm1"]][idx % 2]
                    wait(te, f"modld_{idx}")
                    if k == 0 and j > 0:
                        wait(te, f"modev_{j - 1}")
                    te.matmul(ps["mod"], sb["cs_t"][:, k:k + 1], wm[:],
                              start=(k == 0), stop=(k == NC_DM - 1)).then_inc(sems["sP"], 1)
                    bump("sP", 1)
                    mark(f"modmm_{idx}", "sP")
                    if k == NC_DM - 1:
                        mark(f"modcol_{j}", "sP")

            @sec("vector")
            def _(v, j=j):
                wait(v, f"modcol_{j}")
                if j >= 2:
                    wait(v, f"modsc_{(j - 2) // 2}")
                inc(v.tensor_copy(sb["modrow"][0:1, (j % 2) * 512:(j % 2) * 512 + 512],
                                  ps["mod"]), "sV")
                mark(f"modev_{j}", "sV")

            if j % 2 == 1:
                q = j // 2

                @sec("sync")
                def _(sy, q=q, j=j):
                    wait(sy, f"modev_{j}")
                    for jj in range(NC_DM):
                        sy.dma_start(out=sb["vecs"][:, VOFF[QNAMES[q]] + jj:VOFF[QNAMES[q]] + jj + 1],
                                     in_=sb["modrow"][0:1, jj * 128:(jj + 1) * 128],
                                     allow_slow_non_contiguous=True).then_inc(sems["dMsc"], 16)
                        bump("dMsc", 16)
                    mark(f"modsc_{q}", "dMsc")
        mark("mod_scattered", "dMsc")

        @sec("vector")
        def _(v):
            wait(v, "mod_scattered")
            V = sb["vecs"]
            sl = lambda nm: V[:, VOFF[nm]:VOFF[nm] + NC_DM]
            v.tensor_scalar(out=sl("ca1"), in0=sl("ca1"), scalar1=1.0, scalar2=None, op0=ALU.add)
            v.tensor_scalar(out=sl("cb1"), in0=sl("cb1"), scalar1=1.0, scalar2=None, op0=ALU.add)
            for nm in ["gap", "gbp"]:
                g = sl(nm)
                v.drain()
                v.tensor_scalar(out=sb["murep"][:, 0:NC_DM], in0=g, scalar1=0.0,
                                scalar2=None, op0=ALU.is_ge)
                v.drain()
                v.tensor_scalar(out=sb["murep"][:, 0:NC_DM], in0=sb["murep"][:, 0:NC_DM],
                                scalar1=2.0, scalar2=-1.0, op0=ALU.mult, op1=ALU.add)
                v.tensor_scalar(out=sb["rrep"][:, 0:NC_DM], in0=g, scalar1=-1.0,
                                scalar2=None, op0=ALU.mult)
                v.drain()
                v.tensor_tensor(out=sb["rrep"][:, 0:NC_DM], in0=g, in1=sb["rrep"][:, 0:NC_DM],
                                op=ALU.max)
                v.drain()
                v.tensor_scalar(out=sb["rrep"][:, 0:NC_DM], in0=sb["rrep"][:, 0:NC_DM],
                                scalar1=0.25, scalar2=None, op0=ALU.max)
                v.drain()
                v.tensor_tensor(out=g, in0=sb["murep"][:, 0:NC_DM],
                                in1=sb["rrep"][:, 0:NC_DM], op=ALU.mult)
            v.drain()
            v.tensor_tensor(out=sl("gbw"), in0=sl("gbp"), in1=sl("wsc"), op=ALU.mult)
            inc(v.tensor_tensor(out=sl("gbb"), in0=sl("gbp"), in1=sl("fbias"),
                                op=ALU.mult), "sV")
            mark("mod_done", "sV")

        s0ctx.close()
        alloc("bcrep0", [128, 2 * 4 * TT], scope=actx)
        alloc("bcrep1", [128, 2 * 4 * TT], scope=actx)
        mark_led("bar0")

        def wait_bar0(eng):
            wait_led(eng, "bar0")

        def tsl(tau):
            return slice(tau * TT, (tau + 1) * TT)

        def csl(tile, c, width=TT):
            return tile[:, c * width:(c + 1) * width]

        def stats_round(tag, tau, src_name, sc_first=None, te_first=None):
            for c in range(NC_DM):
                @sec("scalar")
                def _(sc, c=c):
                    if c == 0 and sc_first is not None:
                        sc_first(sc)
                    if c >= 2:
                        wait(sc, f"{tag}m2_{tau}_{c - 2}")
                    sc.activation(out=csl(sb["xsq"], c % 2), in_=csl(sb[src_name], c),
                                  func=AF.Square).then_inc(sems["sA"], 1)
                    bump("sA", 1)
                    mark(f"{tag}sq_{tau}_{c}", "sA")

                @sec("tensor")
                def _(te, c=c):
                    if c == 0 and te_first is not None:
                        te_first(te)
                    te.matmul(ps["mu"], sb["ones1"][:], csl(sb[src_name], c),
                              start=(c == 0), stop=(c == NC_DM - 1)).then_inc(sems["sP"], 1)
                    bump("sP", 1)
                    wait(te, f"{tag}sq_{tau}_{c}")
                    te.matmul(ps["m2"], sb["ones1"][:], csl(sb["xsq"], c % 2),
                              start=(c == 0), stop=(c == NC_DM - 1)).then_inc(sems["sP"], 1)
                    bump("sP", 1)
                    mark(f"{tag}m2_{tau}_{c}", "sP")
            mark(f"stats_{tag}_{tau}", "sP")

        def stats_post(v, tau, tag):
            v.drain()
            v.tensor_scalar(out=sb["mu_a"][:], in0=ps["mu"], scalar1=1.0 / DM,
                            scalar2=None, op0=ALU.mult)
            v.tensor_scalar(out=sb["stt1"][:], in0=ps["m2"], scalar1=1.0 / DM,
                            scalar2=None, op0=ALU.mult)
            v.drain()
            v.tensor_tensor(out=sb["stt2"][:], in0=sb["mu_a"][:], in1=sb["mu_a"][:],
                            op=ALU.mult)
            v.drain()
            v.tensor_tensor(out=sb["stt1"][:], in0=sb["stt1"][:], in1=sb["stt2"][:],
                            op=ALU.subtract)
            v.drain()
            inc(v.tensor_scalar(out=sb["stt1"][:], in0=sb["stt1"][:], scalar1=EPS,
                                scalar2=None, op0=ALU.add), "sV")
            mark(f"vpre_{tag}_{tau}", "sV")


        def newton(v, tau, tag):
            alias(f"post_{tag}_{tau}", f"rstd_{tag}_{tau}")


        def bcast(te, tau, tag):
            te.matmul(ps["bc"][:, 0:TT], sb["onesr"][:], sb["mu_a"][:],
                      start=True, stop=True).then_inc(sems["sP"], 1)
            bump("sP", 1)
            te.matmul(ps["bc"][:, TT:2 * TT], sb["onesr"][:], sb["r_a"][:],
                      start=True, stop=True).then_inc(sems["sP"], 1)
            bump("sP", 1)
            mark(f"bc_{tag}_{tau}", "sP")


        def rep_evac(v, tau, tag):
            v.drain()
            v.tensor_copy(sb["murep"][:], ps["bc"][:, 0:TT])
            inc(v.tensor_copy(sb["rrep"][:], ps["bc"][:, TT:2 * TT]), "sV")
            mark(f"rep_{tag}_{tau}", "sV")


        def stage_front(tau):
            @sec("sync")
            def _(sy, tau=tau):
                if tau == 0:
                    wait_bar0(sy)
                else:
                    wait(sy, f"r0c_{tau - 1}")
                dma(sy, sb["x_t"][:].rearrange("p (c t) -> p c t", c=NC_DM),
                    xin[:, tsl(tau)].rearrange("(c p) t -> p c t", p=128), q="dX")
                mark(f"xld_{tau}", "dX")

            def _scf1(sc, tau=tau):
                wait(sc, f"xld_{tau}")
                if tau == 0:
                    wait_bar0(sc)

            def _tef1(te, tau=tau):
                if tau == 0:
                    wait_bar0(te)
                else:
                    wait(te, f"post2_{tau - 1}")
                wait(te, f"xld_{tau}")

            stats_round("s1", tau, "x_t", sc_first=_scf1, te_first=_tef1)
            alias(f"stats1_{tau}", f"stats_s1_{tau}")

            @sec("vector")
            def _(v, tau=tau):
                wait(v, f"stats1_{tau}")
                stats_post(v, tau, "ln1")

            @sec("scalar")
            def _(sc, tau=tau):
                wait(sc, f"vpre_ln1_{tau}")
                sc.activation(out=sb["stt2"][:], in_=sb["stt1"][:], func=AF.Ln)
                sc.drain()
                sc.activation(out=sb["r_a"][:], in_=sb["stt2"][:], func=AF.Exp,
                              scale=-0.5).then_inc(sems["sA"], 1)
                bump("sA", 1)
                mark(f"rstd_ln1_{tau}", "sA")

            @sec("vector")
            def _(v, tau=tau):
                wait(v, f"rstd_ln1_{tau}")
                newton(v, tau, "ln1")
            alias(f"post1_{tau}", f"post_ln1_{tau}")

            @sec("tensor")
            def _(te, tau=tau):
                wait(te, f"post1_{tau}")
                if tau > 0:
                    wait(te, f"rep2_{tau - 1}")
                bcast(te, tau, "ln1")

            @sec("vector")
            def _(v, tau=tau):
                wait(v, f"bc_ln1_{tau}")
                rep_evac(v, tau, "ln1")
            alias(f"rep1_{tau}", f"rep_ln1_{tau}")

            @sec("vector")
            def _(v, tau=tau):
                V = sb["vecs"]
                if tau > 0:
                    wait(v, f"r0sp_{tau - 1}")
                v.drain()
                for c in range(NC_DM):
                    v.tensor_tensor(out=csl(sb["xn_t"], c), in0=csl(sb["x_t"], c),
                                    in1=sb["murep"][:], op=ALU.subtract)
                v.drain()
                for c in range(NC_DM):
                    v.tensor_tensor(out=csl(sb["xn_t"], c), in0=csl(sb["xn_t"], c),
                                    in1=sb["rrep"][:], op=ALU.mult)
                if tau > 0:
                    wait(v, f"xzmm_{tau - 1}_15")
                v.drain()
                for c in range(NC_DM):
                    v.tensor_scalar(out=csl(sb["h1_t"], c), in0=csl(sb["xn_t"], c),
                                    scalar1=vcol("ca1", c), scalar2=vcol("sa", c),
                                    op0=ALU.mult, op1=ALU.add)
                v.drain()
                for c in range(NC_DM):
                    li = v.scalar_tensor_tensor(out=csl(sb["r0_t"], c), in0=csl(sb["h1_t"], c),
                                                scalar=vcol("gap", c), in1=csl(sb["x_t"], c),
                                                op0=ALU.mult, op1=ALU.add)
                inc(li, "sV")
                mark(f"r0c_{tau}", "sV")

            @sec("sync")
            def _(sy, tau=tau):
                wait(sy, f"r0c_{tau}")
                dma(sy, r0_buf[tau:tau + 1, :].rearrange("o (p c t) -> o p c t", p=128, c=NC_DM),
                    sb["r0_t"][:].rearrange("p (c t) -> p c t", c=NC_DM), q="dR")
                mark(f"r0sp_{tau}", "dR")

            def _scf2(sc, tau=tau):
                wait(sc, f"r0c_{tau}")

            def _tef2(te, tau=tau):
                wait(te, f"post1_{tau}")
                wait(te, f"r0c_{tau}")

            stats_round("s2", tau, "h1_t", sc_first=_scf2, te_first=_tef2)
            alias(f"stats2_{tau}", f"stats_s2_{tau}")

            @sec("vector")
            def _(v, tau=tau):
                wait(v, f"stats2_{tau}")
                stats_post(v, tau, "ln2i")

            @sec("scalar")
            def _(sc, tau=tau):
                wait(sc, f"vpre_ln2i_{tau}")
                sc.activation(out=sb["stt2"][:], in_=sb["stt1"][:], func=AF.Ln)
                sc.drain()
                sc.activation(out=sb["r_a"][:], in_=sb["stt2"][:], func=AF.Exp,
                              scale=-0.5).then_inc(sems["sA"], 1)
                bump("sA", 1)
                mark(f"rstd_ln2i_{tau}", "sA")

            @sec("vector")
            def _(v, tau=tau):
                wait(v, f"rstd_ln2i_{tau}")
                newton(v, tau, "ln2i")
            alias(f"post2_{tau}", f"post_ln2i_{tau}")

            @sec("tensor")
            def _(te, tau=tau):
                wait(te, f"post2_{tau}")
                wait(te, f"rep1_{tau}")
                bcast(te, tau, "ln2i")

            @sec("vector")
            def _(v, tau=tau):
                wait(v, f"bc_ln2i_{tau}")
                rep_evac(v, tau, "ln2i")
            alias(f"rep2_{tau}", f"rep_ln2i_{tau}")

            @sec("vector")
            def _(v, tau=tau):
                v.drain()
                for c in range(NC_DM):
                    v.tensor_tensor(out=csl(sb["h1_t"], c), in0=csl(sb["h1_t"], c),
                                    in1=sb["murep"][:], op=ALU.subtract)
                v.drain()
                for c in range(NC_DM):
                    v.tensor_tensor(out=csl(sb["h1_t"], c), in0=csl(sb["h1_t"], c),
                                    in1=sb["rrep"][:], op=ALU.mult)
                v.drain()
                for c in range(NC_DM):
                    li = v.tensor_scalar(out=csl(sb["h1_t"], c), in0=csl(sb["h1_t"], c),
                                         scalar1=vcol("lngam", c), scalar2=vcol("lnbet", c),
                                         op0=ALU.mult, op1=ALU.add)
                inc(li, "sV")
                mark(f"xn2_{tau}", "sV")

            @sec("vector")
            def _(v, tau=tau):
                if tau > 0:
                    wait(v, f"conv_{tau - 1}")
                    src = AP(sb["xs_t"][:].tensor, TT,
                             [[NC_DH * (TT + 3), 128], [TT + 3, NC_DH], [1, 3]])
                    dst = AP(sb["xs_t"][:].tensor, 0,
                             [[NC_DH * (TT + 3), 128], [TT + 3, NC_DH], [1, 3]])
                    inc(v.tensor_copy(dst, src), "sV")
                else:
                    inc(v.memset(AP(sb["xs_t"][:].tensor, 0,
                                    [[NC_DH * (TT + 3), 128], [TT + 3, NC_DH], [1, 3]]),
                                 0.0), "sV")
                mark(f"ctx_{tau}", "sV")


        def emit_winevac(tau, o):
            @sec("scalar")
            def _(sc, tau=tau, o=o):
                wait(sc, f"xzmm_{tau}_{o}")
                pst = ps["xz0"] if o % 2 == 0 else ps["xz1"]
                if o == 0:
                    wait(sc, f"ctx_{tau}")
                li = sc.copy(sb["xs_t"][:, o * (TT + 3) + 3:(o + 1) * (TT + 3)],
                             pst[:])
                inc(li, "sA")
                mark(f"xzev_{tau}_{o}", "sA")
                if o == 7:
                    mark(f"xsv_{tau}", "sA")

        def stage_win(tau):
            for o in range(16):
                @sec("tensor")
                def _(te, tau=tau, o=o):
                    if o == 0:
                        wait(te, f"xn2_{tau}")
                    if o < 2 and tau > 0:
                        wait(te, f"sp_{tau - 1}_{o + 6}")
                    if o >= 2 or tau > 0:
                        oo, tt2 = o - 2, tau
                        if oo < 0:
                            oo += 16
                            tt2 -= 1
                        wait(te, f"xzev_{tt2}_{oo}")
                    pst = ps["xz0"] if o % 2 == 0 else ps["xz1"]
                    for k in range(NC_DM):
                        li = te.matmul(pst[:], sb["w_in_sb"][:, k * 2048 + o * 128:
                                                             k * 2048 + o * 128 + 128],
                                       csl(sb["h1_t"], k), start=(k == 0),
                                       stop=(k == NC_DM - 1))
                    inc(li, "sP")
                    mark(f"xzmm_{tau}_{o}", "sP")

                if o >= 8:
                    @sec("scalar")
                    def _(sc, tau=tau, o=o):
                        wait(sc, f"xzmm_{tau}_{o}")
                        pst = ps["xz0"] if o % 2 == 0 else ps["xz1"]
                        if tau > 0 and o == 8:
                            wait(sc, f"g_{tau - 1}")
                        li = sc.activation(out=csl(sb["sz_t"], o - 8), in_=pst[:],
                                           func=AF.Silu)
                        inc(li, "sA")
                        mark(f"xzev_{tau}_{o}", "sA")
                        if o == 15:
                            mark(f"xsev_{tau}", "sA")
                elif tau == 0:
                    emit_winevac(0, o)


        def stage_tail(tau):
            for p in range(4):
                @sec("vector")
                def _(v, tau=tau, p=p):
                    if p == 0:
                        wait(v, f"xsv_{tau}")
                        if tau > 0:
                            wait(v, f"exp_{tau - 1}_7_3")
                    v.drain()
                    for j in range(4):
                        if j > 0:
                            v.drain()
                        for c in (2 * p, 2 * p + 1):
                            base = c * (TT + 3)
                            if j == 0:
                                li = v.tensor_scalar(
                                    out=csl(sb["scrB"], c),
                                    in0=sb["xs_t"][:, base:base + TT],
                                    scalar1=sb["convw_sb"][:, 4 * c:4 * c + 1],
                                    scalar2=vcol("convb", c),
                                    op0=ALU.mult, op1=ALU.add)
                            else:
                                li = v.scalar_tensor_tensor(
                                    out=csl(sb["scrB"], c),
                                    in0=sb["xs_t"][:, base + j:base + j + TT],
                                    scalar=sb["convw_sb"][:, 4 * c + j:4 * c + j + 1],
                                    in1=csl(sb["scrB"], c),
                                    op0=ALU.mult, op1=ALU.add)
                    inc(li, "sV")
                    mark(f"conv_{tau}_{p}", "sV")
                    if p == 3:
                        mark(f"conv_{tau}", "sV")

                @sec("scalar")
                def _(sc, tau=tau, p=p):
                    wait(sc, f"conv_{tau}_{p}")
                    if p == 0 and tau > 0:
                        wait(sc, f"gsp_{tau - 1}")
                    inc(sc.activation(out=sb["scrA"][:, 2 * p * TT:(2 * p + 2) * TT],
                                      in_=sb["scrB"][:, 2 * p * TT:(2 * p + 2) * TT],
                                      func=AF.Silu), "sA")
                    mark(f"u_{tau}_{p}", "sA")
                    if p == 3:
                        mark(f"u_{tau}", "sA")

                @sec("tensor")
                def _(te, tau=tau, p=p):
                    wait(te, f"u_{tau}_{p}")
                    if p == 0 and tau > 0:
                        wait(te, f"projev_{tau - 1}")
                    for k in (2 * p, 2 * p + 1):
                        te.matmul(ps["pdt"][:], sb["wxp_sb"][:, k * 160:k * 160 + 128],
                                  csl(sb["scrA"], k), start=(k == 0),
                                  stop=(k == NC_DH - 1))
                        li = te.matmul(ps["pbc"][:],
                                       sb["wxp_sb"][:, k * 160 + 128:k * 160 + 160],
                                       csl(sb["scrA"], k), start=(k == 0),
                                       stop=(k == NC_DH - 1))
                    if p == 3:
                        inc(li, "sP")
                        mark(f"projmm_{tau}", "sP")

            @sec("vector")
            def _(v, tau=tau):
                wait(v, f"projmm_{tau}")
                if tau > 0:
                    wait(v, f"prst_{tau - 1}")
                v.tensor_copy(sb["pdt_t"][:], ps["pdt"][:])
                inc(v.tensor_copy(sb["pbc_t"][:], ps["pbc"][:]), "sV")
                mark(f"projev_{tau}", "sV")

            @sec("gpsimd")
            def _(g, tau=tau):
                wait(g, f"projev_{tau}")
                dma(g, arp_i[tau:tau + 1, :].rearrange("o (p t) -> o p t", p=160)[:, 0:128, :],
                    sb["pdt_t"][:], q="dG")
                dma(g, arp_i[tau:tau + 1, :].rearrange("o (p t) -> o p t", p=160)[:, 128:160, :],
                    sb["pbc_t"][:], q="dG")
                mark(f"prst_{tau}", "dG")
                wait(g, f"prst_{tau}")
                g.collective_compute(
                    "AllReduce", ALU.add,
                    replica_groups=[[0, 1], [2, 3], [4, 5], [6, 7]],
                    ins=[arp_i[tau:tau + 1, :].opt()],
                    outs=[arp_o[tau:tau + 1, :].opt()],
                ).then_inc(sems["sC"], 1)
                bump("sC", 1)
                mark(f"ccp_{tau}", "sC")
                if tau > 0:
                    wait(g, f"wdtmm_{tau - 1}")
                    wait(g, f"bcfl_{tau - 1}_3")
                wait(g, f"ccp_{tau}")
                dma(g, sb["dt_sb"][:],
                    arp_o[tau:tau + 1, :].rearrange("o (p t) -> o p t", p=160)[:, 0:128, :],
                    q="dG")
                dma(g, sb["bc_sb"][:],
                    arp_o[tau:tau + 1, :].rearrange("o (p t) -> o p t", p=160)[:, 128:160, :],
                    q="dG")
                mark(f"dtbc_{tau}", "dG")


        def punit(tau, h, k, n):
            idx = h * 8 + k - n
            t2 = tau
            if idx < 0:
                idx += 32
                t2 -= 1
            return t2, idx // 8, idx % 8

        def hctree(v, tau, h, k):
            hcp = sb["HC0"]
            Hp = sb["H_t"] if k % 2 == 0 else sb["H1"]
            bcr = sb["bcrep0"] if h % 2 == 0 else sb["bcrep1"]
            v.drain()
            v.tensor_tensor(out=hcp[:], in0=Hp[:],
                            in1=bcr[:, 4 * TT:8 * TT], op=ALU.mult)
            v.drain()
            hv = AP(hcp[:].tensor, 0, [[4 * TT, 128], [1, TT], [TT, 4]])
            if h == 0:
                li = v.tensor_reduce(out=csl(sb["y_t"], k), in_=hv, axis=AXX,
                                     op=ALU.add)
            else:
                v.tensor_reduce(out=sb["murep"][:], in_=hv, axis=AXX,
                                op=ALU.add)
                v.drain()
                li = v.tensor_tensor(out=csl(sb["y_t"], k), in0=csl(sb["y_t"], k),
                                     in1=sb["murep"][:], op=ALU.add)
            inc(li, "sV")
            mark(f"tree_{tau}_{k}_{h}", "sV")

        def stage_scanblk(tau):
            for o in range(NC_DH):
                @sec("tensor")
                def _(te, tau=tau, o=o):
                    if o == 0:
                        wait(te, f"dtbc_{tau}")
                        if tau > 0:
                            wait(te, f"exp_{tau - 1}_7_3")
                    if o >= 2:
                        wait(te, f"sp_{tau}_{o - 2}")
                    pst = ps["dl0"] if o % 2 == 0 else ps["dl1"]
                    te.matmul(pst[:], sb["wdt_sb"][:, o * 128:(o + 1) * 128],
                              sb["dt_sb"][:], start=True, stop=True).then_inc(sems["sP"], 1)
                    bump("sP", 1)
                    mark(f"wdt_{tau}_{o}", "sP")
                    if o == NC_DH - 1:
                        mark(f"wdtmm_{tau}", "sP")

                @sec("scalar")
                def _(sc, tau=tau, o=o):
                    wait(sc, f"wdt_{tau}_{o}")
                    pst = ps["dl0"] if o % 2 == 0 else ps["dl1"]
                    sc.activation(out=csl(sb["xsq"], o % 2), in_=pst[:], func=AF.Exp,
                                  bias=vcol("bdt", o))
                    sc.drain()
                    sc.activation(out=csl(sb["delta_t"], o), in_=csl(sb["xsq"], o % 2),
                                  func=AF.Ln, bias=1.0).then_inc(sems["sA"], 1)
                    bump("sA", 1)
                    mark(f"sp_{tau}_{o}", "sA")
                    if o == NC_DH - 1:
                        mark(f"delta_{tau}", "sA")

            @sec("vector")
            def _(v, tau=tau):
                wait(v, f"delta_{tau}")
                if tau > 0:
                    wait(v, f"bw_{tau - 1}_7_3")
                v.drain()
                inc(v.tensor_tensor(out=sb["du_t"][:], in0=sb["delta_t"][:],
                                    in1=sb["scrA"][:], op=ALU.mult), "sV")
                mark(f"du_{tau}", "sV")

            for h in range(4):
                @sec("sync")
                def _(sy, tau=tau, h=h):
                    wait(sy, f"dtbc_{tau}")
                    bcr = sb["bcrep0"] if h % 2 == 0 else sb["bcrep1"]
                    qn = "dBC" if h % 2 == 0 else "dBC1"
                    if tau * 4 + h >= 2:
                        pt, ph = (tau, h - 2) if h >= 2 else (tau - 1, h + 2)
                        wait(sy, f"bw_{pt}_7_{ph}")
                        wait(sy, f"tree_{pt}_7_{ph}")
                    dma(sy, bcr[0:1, 0:4 * TT].rearrange("o (n t) -> o n t", n=4),
                        sb["bc_sb"][4 * h:4 * h + 4, :], q=qn)
                    dma(sy, bcr[0:1, 4 * TT:8 * TT].rearrange("o (n t) -> o n t", n=4),
                        sb["bc_sb"][16 + 4 * h:20 + 4 * h, :], q=qn)
                    nh = 1
                    step = 0
                    while nh < 128:
                        mark(f"bcq_{tau}_{h}_{step}", qn)
                        wait(sy, f"bcq_{tau}_{h}_{step}")
                        dma(sy, bcr[nh:2 * nh, :], bcr[0:nh, :], q=qn)
                        nh *= 2
                        step += 1
                    mark(f"bcfl_{tau}_{h}", qn)

                for k in range(NC_DH):
                    @sec("scalar")
                    def _(sc, tau=tau, h=h, k=k):
                        wait(sc, f"delta_{tau}")
                        p2t, p2h, p2k = punit(tau, h, k, 2)
                        if p2t >= 0:
                            wait(sc, f"scans_{p2t}_{p2k}_{p2h}")
                        jb = 4 * (k % 2)
                        for j in range(4):
                            n = 4 * h + j
                            li = sc.activation(
                                out=csl(sb["scrB"], jb + j), in_=csl(sb["delta_t"], k),
                                func=AF.Exp,
                                scale=sb["aneg"][:, k * NST + n:k * NST + n + 1])
                        inc(li, "sA")
                        mark(f"exp_{tau}_{k}_{h}", "sA")
                        p1t, p1h, p1k = punit(tau, h, k, 1)
                        if p1t >= 0:
                            wait(sc, f"scans_{p1t}_{p1k}_{p1h}")
                            Hp = sb["H_t"] if p1k % 2 == 0 else sb["H1"]
                            inc(sc.copy(
                                sb["lastst"][:, p1k * NST + 4 * p1h:
                                             p1k * NST + 4 * p1h + 4],
                                AP(Hp[:].tensor, TT - 1, [[4 * TT, 128], [TT, 4]])),
                                "sA")

                    if h == 1 and tau + 1 < NT:
                        emit_winevac(tau + 1, k)

                    @sec("gpsimd")
                    def _(g, tau=tau, h=h, k=k):
                        wait(g, f"du_{tau}")
                        wait(g, f"bcfl_{tau}_{h}")
                        p2t, p2h, p2k = punit(tau, h, k, 2)
                        if p2t == tau:
                            wait(g, f"scans_{p2t}_{p2k}_{p2h}")
                        elif (h, k) == (0, 1):
                            wait(g, f"scans_{tau}_0_0")
                        rep = AP(sb["du_t"][:].tensor, k * TT,
                                 [[NC_DH * TT, 128], [0, 4], [1, TT]])
                        bcr = sb["bcrep0"] if h % 2 == 0 else sb["bcrep1"]
                        scrCp = sb["scrC"] if k % 2 == 0 else sb["scrC1"]
                        g.tensor_tensor(out=scrCp[:].rearrange(
                                            "p (n t) -> p n t", n=4),
                                        in0=rep,
                                        in1=bcr[:, 0:4 * TT].rearrange(
                                            "p (n t) -> p n t", n=4),
                                        op=ALU.mult).then_inc(sems["sG"], 1)
                        bump("sG", 1)
                        mark(f"bw_{tau}_{k}_{h}", "sG")

                    @sec("vector")
                    def _(v, tau=tau, h=h, k=k):
                        wait(v, f"exp_{tau}_{k}_{h}")
                        wait(v, f"bw_{tau}_{k}_{h}")
                        jb = 4 * (k % 2)
                        Hp = sb["H_t"] if k % 2 == 0 else sb["H1"]
                        scrCp = sb["scrC"] if k % 2 == 0 else sb["scrC1"]
                        for j in range(4):
                            n = 4 * h + j
                            li = v.tensor_tensor_scan(
                                out=csl(Hp, j), data0=csl(sb["scrB"], jb + j),
                                data1=csl(scrCp, j),
                                initial=sb["lastst"][:, k * NST + n:k * NST + n + 1],
                                op0=ALU.mult, op1=ALU.add)
                        inc(li, "sV")
                        mark(f"scans_{tau}_{k}_{h}", "sV")
                        hctree(v, tau, h, k)

            @sec("vector")
            def _(v, tau=tau):
                v.drain()
                for k in range(NC_DH):
                    v.scalar_tensor_tensor(out=csl(sb["y_t"], k), in0=csl(sb["scrA"], k),
                                           scalar=vcol("dpar", k), in1=csl(sb["y_t"], k),
                                           op0=ALU.mult, op1=ALU.add)
                v.drain()
                for k in range(NC_DH):
                    li = v.tensor_tensor(out=csl(sb["scrA"], k), in0=csl(sb["y_t"], k),
                                         in1=csl(sb["sz_t"], k), op=ALU.mult)
                inc(li, "sV")
                mark(f"g_{tau}", "sV")

            @sec("sync")
            def _(sy, tau=tau):
                wait(sy, f"g_{tau}")
                dma(sy, g_buf[tau:tau + 1, :].rearrange("o (p c t) -> o p c t",
                                                        p=128, c=NC_DH),
                    sb["scrA"][:].rearrange("p (c t) -> p c t", c=NC_DH), q="dOut")
                mark(f"gsp_{tau}", "dOut")

            if DBG and tau == 0:
                @sec("sync")
                def _(sy, tau=tau):
                    wait(sy, f"g_{tau}")
                    dma(sy, dbg, sb["delta_t"][:])
                    mark("dbgw", "dS")


        for tau in range(NT):
            stage_front(tau)
            if tau >= 1:
                stage_scanblk(tau - 1)
            stage_win(tau)
            stage_tail(tau)
        stage_scanblk(NT - 1)
        actx.close()
        alloc("wout_sb", [128, NC_DH * DM])
        alloc("g_ld", [128, NC_DH * TT])
        alloc("g_ld1", [128, NC_DH * TT])
        alloc("pout_t", [128, NC_DM * TT])
        alloc("pout_u", [128, NC_DM * TT])
        alloc("qi_t", [128, NC_DM * TT], I32)
        alloc("qf_t", [128, NC_DM * TT])
        alloc("qb_t", [128, NC_DM * TT], BF16)
        alloc("wf8", [128, NC_DM * DM], I8)
        alloc("wfb", [128, NC_DM * DM], BF16)
        alloc("c2scr", [128, NC_DM * TT])
        alloc("ps_ld0", [128, NC_DM * TT])
        alloc("ps_ld1", [128, NC_DM * TT])
        alloc("r0_ld0", [128, NC_DM * TT])
        alloc("r0_ld1", [128, NC_DM * TT])

        mark_led("barrierB")

        def wait_barrier(eng):
            wait_led(eng, "barrierB")

        @sec("sync")
        def _(sy):
            wait_barrier(sy)
            dma(sy, sb["wout_sb"][:].rearrange("p (c o) -> p c o", c=NC_DH),
                wout_t[:, :].rearrange("(c p) o -> p c o", p=128), q="dW")
            dma(sy, sb["wf8"][:].rearrange("p (c o) -> p c o", c=NC_DM),
                wffn_t[:, :].rearrange("(c p) o -> p c o", p=128), q="dW")
            mark("bweights", "dW")

        @sec("vector")
        def _(v):
            wait_barrier(v)
            wait(v, "bweights")
            inc(v.tensor_copy(sb["wfb"][:], sb["wf8"][:]), "sV")
            mark("wfb_done", "sV")

        @sec("tensor")
        def _(te):
            wait_barrier(te)
            te.sem_inc(sems["sP"], 1)
            bump("sP", 1)
            mark("te_bar", "sP")

        @sec("scalar")
        def _(sc):
            wait_barrier(sc)
            sc.sem_inc(sems["sA"], 1)
            bump("sA", 1)
            mark("sc_bar", "sA")

        @sec("gpsimd")
        def _(g):
            wait_barrier(g)
            g.sem_inc(sems["sG"], 1)
            bump("sG", 1)
            mark("g_bar", "sG")

        def stage_B(tau):
            gl = "g_ld" if tau % 2 == 0 else "g_ld1"

            @sec("sync")
            def _(sy, tau=tau, gl=gl):
                if tau == 0:
                    wait(sy, f"gsp_{NT - 1}")
                elif tau == 1:
                    wait(sy, "bweights")
                else:
                    wait(sy, f"poutmm_{tau - 2}")
                qn = "dX" if tau % 2 == 0 else "dW"
                dma(sy, sb[gl][:].rearrange("p (c t) -> p c t", c=NC_DH),
                    g_buf[tau:tau + 1, :].rearrange("o (p c t) -> o p c t",
                                                    p=128, c=NC_DH), q=qn)
                mark(f"gld_{tau}", qn)

            for o in range(NC_DM):
                @sec("tensor")
                def _(te, tau=tau, o=o, gl=gl):
                    if o == 0:
                        wait(te, f"gld_{tau}")
                        wait(te, "bweights")
                    if o >= 2 or tau > 0:
                        oo, tt2 = o - 2, tau
                        if oo < 0:
                            oo += NC_DM
                            tt2 -= 1
                        wait(te, f"poutev_{tt2}_{oo}")
                    pst = ps["xz0"] if o % 2 == 0 else ps["xz1"]
                    for k in range(NC_DH):
                        li = te.matmul(pst[:], sb["wout_sb"][:, k * DM + o * 128:
                                                             k * DM + o * 128 + 128],
                                       csl(sb[gl], k), start=(k == 0),
                                       stop=(k == NC_DH - 1))
                    inc(li, "sP")
                    mark(f"poutm_{tau}_{o}", "sP")
                    if o == NC_DM - 1:
                        mark(f"poutmm_{tau}", "sP")

                @sec("vector")
                def _(v, tau=tau, o=o):
                    wait(v, f"poutm_{tau}_{o}")
                    if tau > 0 and o == 0:
                        wait(v, f"post_{tau - 1}")
                    pst = ps["xz0"] if o % 2 == 0 else ps["xz1"]
                    inc(v.tensor_copy(csl(sb["pout_t"], o), pst[:]), "sV")
                    mark(f"poutev_{tau}_{o}", "sV")
                    if o == NC_DM - 1:
                        mark(f"poutevall_{tau}", "sV")

            @sec("gpsimd")
            def _(g, tau=tau):
                wait(g, f"poutevall_{tau}")
                dma(g, aro_i[tau:tau + 1, :].rearrange("o (p c t) -> o p c t",
                                                       p=128, c=NC_DM),
                    sb["pout_t"][:].rearrange("p (c t) -> p c t", c=NC_DM), q="dG")
                mark(f"post_{tau}", "dG")
                wait(g, f"post_{tau}")
                g.collective_compute(
                    "AllReduce", ALU.add,
                    replica_groups=[[0, 1], [2, 3], [4, 5], [6, 7]],
                    ins=[aro_i[tau:tau + 1, :].opt()],
                    outs=[aro_o[tau:tau + 1, :].opt()],
                ).then_inc(sems["sC"], 1)
                bump("sC", 1)
                mark(f"cco_{tau}", "sC")


        def stage_C1(tau):
            pl = "ps_ld0" if tau % 2 == 0 else "ps_ld1"
            rl = "r0_ld0" if tau % 2 == 0 else "r0_ld1"

            @sec("sync")
            def _(sy, tau=tau, pl=pl):
                wait(sy, f"cco_{tau}")
                if tau > 1:
                    wait(sy, f"x2c_{tau - 2}")
                    wait(sy, f"mx_{tau - 2}")
                    wait(sy, f"stats3_{tau - 2}")
                    wait(sy, f"s3sq_{tau - 2}_{NC_DM - 1}")
                    wait(sy, f"x2sp_{tau - 2}")
                qn = "dMsc" if tau % 2 == 0 else "dM1"
                dma(sy, sb[pl][:].rearrange("p (c t) -> p c t", c=NC_DM),
                    aro_o[tau:tau + 1, :].rearrange("o (p c t) -> o p c t",
                                                    p=128, c=NC_DM), q=qn)
                mark(f"psld_{tau}", qn)

            @sec("sync")
            def _(sy, tau=tau, rl=rl):
                if tau > 1:
                    wait(sy, f"mx_{tau - 2}")
                qn = "dR" if tau % 2 == 0 else "dM0"
                dma(sy, sb[rl][:].rearrange("p (c t) -> p c t", c=NC_DM),
                    r0_buf[tau:tau + 1, :].rearrange("o (p c t) -> o p c t",
                                                     p=128, c=NC_DM), q=qn)
                mark(f"r0ld_{tau}", qn)

            @sec("vector")
            def _(v, tau=tau, pl=pl, rl=rl):
                wait(v, f"psld_{tau}")
                wait(v, f"r0ld_{tau}")
                v.drain()
                for c in range(NC_DM):
                    li = v.scalar_tensor_tensor(out=csl(sb[pl], c),
                                                in0=csl(sb[pl], c),
                                                scalar=vcol("gap", c), in1=csl(sb[rl], c),
                                                op0=ALU.mult, op1=ALU.add)
                inc(li, "sV")
                mark(f"x2c_{tau}", "sV")

            @sec("sync")
            def _(sy, tau=tau, pl=pl):
                wait(sy, f"x2c_{tau}")
                qn = "dOut" if tau % 2 == 0 else "dS"
                dma(sy, x2_buf[tau:tau + 1, :].rearrange("o (p c t) -> o p c t",
                                                         p=128, c=NC_DM),
                    sb[pl][:].rearrange("p (c t) -> p c t", c=NC_DM), q=qn)
                mark(f"x2sp_{tau}", qn)

            def _scf3(sc, tau=tau):
                if tau == 0:
                    wait_barrier(sc)
                wait(sc, f"x2c_{tau}")

            def _tef3(te, tau=tau):
                wait(te, f"x2c_{tau}")
                if tau > 0:
                    wait(te, f"p3_{tau - 1}")

            stats_round("s3", tau, pl, sc_first=_scf3, te_first=_tef3)
            alias(f"stats3_{tau}", f"stats_s3_{tau}")

            @sec("vector")
            def _(v, tau=tau):
                wait(v, f"stats3_{tau}")
                if tau > 0:
                    wait(v, f"stsp_{tau - 1}")
                stats_post(v, tau, "ln3")

            @sec("scalar")
            def _(sc, tau=tau):
                wait(sc, f"vpre_ln3_{tau}")
                sc.activation(out=sb["stt2"][:], in_=sb["stt1"][:], func=AF.Ln)
                sc.drain()
                sc.activation(out=sb["r_a"][:], in_=sb["stt2"][:], func=AF.Exp,
                              scale=-0.5).then_inc(sems["sA"], 1)
                bump("sA", 1)
                mark(f"rstd_ln3_{tau}", "sA")

            @sec("vector")
            def _(v, tau=tau):
                wait(v, f"rstd_ln3_{tau}")
                newton(v, tau, "ln3")
            alias(f"p3_{tau}", f"post_ln3_{tau}")

            @sec("sync")
            def _(sy, tau=tau):
                wait(sy, f"p3_{tau}")
                dma(sy, st_buf[tau:tau + 1, 0:TT], sb["mu_a"][:], q="dSt")
                dma(sy, st_buf[tau:tau + 1, TT:2 * TT], sb["r_a"][:], q="dSt")
                mark(f"stsp_{tau}", "dSt")

            @sec("tensor")
            def _(te, tau=tau):
                wait(te, f"p3_{tau}")
                if tau > 0:
                    wait(te, f"mx_{tau - 1}")
                bcast(te, tau, "ln3")

            @sec("vector")
            def _(v, tau=tau, pl=pl, rl=rl):
                wait(v, f"bc_ln3_{tau}")
                rep_evac(v, tau, "ln3")
                wait(v, f"x2sp_{tau}")
                v.drain()
                for c in range(NC_DM):
                    v.tensor_tensor(out=csl(sb[rl], c), in0=csl(sb[pl], c),
                                    in1=sb["murep"][:], op=ALU.subtract)
                v.drain()
                for c in range(NC_DM):
                    v.tensor_tensor(out=csl(sb[rl], c), in0=csl(sb[rl], c),
                                    in1=sb["rrep"][:], op=ALU.mult)
                v.drain()
                for c in range(NC_DM):
                    v.tensor_scalar(out=csl(sb[rl], c), in0=csl(sb[rl], c),
                                    scalar1=vcol("cb1", c), scalar2=vcol("sbv", c),
                                    op0=ALU.mult, op1=ALU.add)
                v.drain()
                v.reduce_max(out=sb["rrep"][:, 0:1], in_=sb[rl][:], axis=AXX,
                             apply_absolute_value=True)
                v.drain()
                inc(v.tensor_tensor(out=sb["mxacc"][:], in0=sb["mxacc"][:],
                                    in1=sb["rrep"][:, 0:1], op=ALU.max), "sV")
                mark(f"mx_{tau}", "sV")


        for tau in range(NT + 1):
            if tau < NT:
                stage_B(tau)
            if tau >= 1:
                stage_C1(tau - 1)

        @sec("sync")
        def _(sy):
            wait(sy, f"mx_{NT - 1}")
            dma(sy, sb["mxrow"][0:1, 0:128].rearrange("o (a b) -> o a b", a=128),
                sb["mxacc"][:], q="dSt")
            mark("mxfl", "dSt")

        @sec("vector")
        def _(v):
            wait(v, "mxfl")
            v.drain()
            inc(v.reduce_max(out=sb["mxrow"][0:1, 128:129], in_=sb["mxrow"][0:1, 0:128],
                             axis=AXX, apply_absolute_value=False), "sV")
            mark("mxloc", "sV")

        @sec("gpsimd")
        def _(g):
            wait(g, "mxloc")
            dma(g, mx_i, sb["mxrow"][0:1, 128:129], q="dG")
            mark("mxst", "dG")
            wait(g, "mxst")
            g.collective_compute(
                "AllReduce", ALU.max,
                replica_groups=[list(range(8))],
                ins=[mx_i.opt()], outs=[mx_o.opt()],
            ).then_inc(sems["sC"], 1)
            bump("sC", 1)
            mark("ccmx", "sC")
            wait(g, "ccmx")
            dma(g, sb["mxrow"][0:1, 129:130], mx_o, q="dG")
            mark("mxg", "dG")

        @sec("vector")
        def _(v):
            wait(v, "mxg")
            v.drain()
            v.tensor_scalar(out=sb["mxrow"][0:1, 129:130], in0=sb["mxrow"][0:1, 129:130],
                            scalar1=1.0 / 127.0, scalar2=EPS, op0=ALU.mult, op1=ALU.max)
            v.drain()
            inc(v.reciprocal(sb["mxrow"][0:1, 128:129], sb["mxrow"][0:1, 129:130]), "sV")
            mark("xscv", "sV")

        @sec("tensor")
        def _(te):
            wait(te, "xscv")
            te.matmul(ps["bc"][:, 0:1], sb["onesr"][:], sb["mxrow"][0:1, 128:129],
                      start=True, stop=True)
            te.matmul(ps["bc"][:, 1:2], sb["onesr"][:], sb["mxrow"][0:1, 129:130],
                      start=True, stop=True).then_inc(sems["sP"], 1)
            bump("sP", 1)
            mark("xscb", "sP")

        @sec("vector")
        def _(v):
            wait(v, "xscb")
            V = sb["vecs"]
            sl = lambda nm: V[:, VOFF[nm]:VOFF[nm] + NC_DM]
            v.drain()
            v.tensor_copy(sb["mxacc"][:], ps["bc"][:, 0:1])
            v.tensor_copy(sb["ones1"][:], ps["bc"][:, 1:2])
            v.drain()
            v.tensor_scalar(out=sl("cb1"), in0=sl("cb1"), scalar1=sb["mxacc"][:],
                            scalar2=None, op0=ALU.mult)
            v.tensor_scalar(out=sl("sbv"), in0=sl("sbv"), scalar1=sb["mxacc"][:],
                            scalar2=None, op0=ALU.mult)
            inc(v.tensor_scalar(out=sl("gbw"), in0=sl("gbw"), scalar1=sb["ones1"][:],
                                scalar2=None, op0=ALU.mult), "sV")
            mark("c2const", "sV")

        for tau in range(NT):
            pp = "pout_t" if tau % 2 == 0 else "pout_u"

            @sec("sync")
            def _(sy, tau=tau, pp=pp):
                if tau == 0:
                    wait(sy, "c2const")
                    wait(sy, "mxfl")
                else:
                    wait(sy, f"bc_c2_{tau - 1}_{tau - 1}")
                if tau > 1:
                    wait(sy, f"x3_{tau - 2}")
                dma(sy, sb[pp][:].rearrange("p (c t) -> p c t", c=NC_DM),
                    x2_buf[tau:tau + 1, :].rearrange("o (p c t) -> o p c t",
                                                     p=128, c=NC_DM), q="dX")
                dma(sy, sb["mu_a"][:], st_buf[tau:tau + 1, 0:TT], q="dX")
                dma(sy, sb["r_a"][:], st_buf[tau:tau + 1, TT:2 * TT], q="dX")
                mark(f"x2ld_{tau}", "dX")

            @sec("tensor")
            def _(te, tau=tau):
                wait(te, f"x2ld_{tau}")
                if tau == 0:
                    wait(te, "c2const")
                else:
                    wait(te, f"rep_c2r_{tau - 1}_{tau - 1}")
                bcast(te, tau, f"c2_{tau}")

            @sec("vector")
            def _(v, tau=tau, pp=pp):
                wait(v, f"bc_c2_{tau}_{tau}")
                rep_evac(v, tau, f"c2r_{tau}")
                wait(v, "c2const")
                if tau > 0:
                    wait(v, f"fmm_{tau - 1}_{NC_DM - 1}")
                    wait(v, f"x3sp_{tau - 1}")
                v.drain()
                for c in range(NC_DM):
                    v.tensor_tensor(out=csl(sb["g_ld"], c), in0=csl(sb[pp], c),
                                    in1=sb["murep"][:], op=ALU.subtract)
                v.drain()
                for c in range(NC_DM):
                    v.tensor_tensor(out=csl(sb["g_ld"], c), in0=csl(sb["g_ld"], c),
                                    in1=sb["rrep"][:], op=ALU.mult)
                v.drain()
                for c in range(NC_DM):
                    v.tensor_scalar(out=csl(sb["g_ld"], c), in0=csl(sb["g_ld"], c),
                                    scalar1=vcol("cb1", c), scalar2=vcol("sbv", c),
                                    op0=ALU.mult, op1=ALU.add)
                v.drain()
                v.tensor_copy(sb["qi_t"][:], sb["g_ld"][:])
                v.drain()
                v.tensor_copy(sb["qf_t"][:], sb["qi_t"][:])
                v.drain()
                v.tensor_tensor(out=sb["c2scr"][:], in0=sb["g_ld"][:], in1=sb["qf_t"][:],
                                op=ALU.subtract)
                v.drain()
                v.scalar_tensor_tensor(out=sb["qf_t"][:], in0=sb["c2scr"][:], scalar=0.5,
                                       in1=sb["qf_t"][:], op0=ALU.is_ge, op1=ALU.add)
                v.drain()
                v.scalar_tensor_tensor(out=sb["qf_t"][:], in0=sb["c2scr"][:], scalar=-0.5,
                                       in1=sb["qf_t"][:], op0=ALU.is_ge, op1=ALU.add)
                v.drain()
                inc(v.tensor_scalar(out=sb["qb_t"][:], in0=sb["qf_t"][:],
                                    scalar1=-1.0, scalar2=127.0, op0=ALU.add,
                                    op1=ALU.min), "sV")
                mark(f"vq_{tau}", "sV")

            for o in range(NC_DM):
                @sec("tensor")
                def _(te, tau=tau, o=o):
                    if o == 0:
                        wait(te, f"vq_{tau}")
                        if tau == 0:
                            wait(te, "wfb_done")
                    if o >= 2 or tau > 0:
                        oo, tt2 = o - 2, tau
                        if oo < 0:
                            oo += NC_DM
                            tt2 -= 1
                        wait(te, f"fev_{tt2}_{oo}")
                    pst = ps["xz0"] if o % 2 == 0 else ps["xz1"]
                    for k in range(NC_DM):
                        li = te.matmul(pst[:], sb["wfb"][:, k * DM + o * 128:
                                                         k * DM + o * 128 + 128],
                                       csl(sb["qb_t"], k), start=(k == 0),
                                       stop=(k == NC_DM - 1))
                    inc(li, "sP")
                    mark(f"fmm_{tau}_{o}", "sP")

                @sec("vector")
                def _(v, tau=tau, o=o, pp=pp):
                    wait(v, f"fmm_{tau}_{o}")
                    if tau > 0 and o == 0:
                        wait(v, f"x3sp_{tau - 1}")
                    pst = ps["xz0"] if o % 2 == 0 else ps["xz1"]
                    v.drain()
                    v.tensor_scalar(out=csl(sb["qf_t"], o), in0=pst[:],
                                    scalar1=vcol("gbw", o), scalar2=vcol("gbb", o),
                                    op0=ALU.mult, op1=ALU.add)
                    v.drain()
                    inc(v.tensor_tensor(out=csl(sb["qf_t"], o), in0=csl(sb["qf_t"], o),
                                        in1=csl(sb[pp], o), op=ALU.add), "sV")
                    mark(f"fev_{tau}_{o}", "sV")
                    if o == NC_DM - 1:
                        mark(f"x3_{tau}", "sV")

            @sec("sync")
            def _(sy, tau=tau):
                wait(sy, f"x3_{tau}")
                dma(sy, xout[:, tsl(tau)].rearrange("(c p) t -> p c t", p=128),
                    sb["qf_t"][:].rearrange("p (c t) -> p c t", c=NC_DM), q="dOut")
                mark(f"x3sp_{tau}", "dOut")

        mark_led("final")

        @sec("sync")
        def _(sy):
            wait_led(sy, "final", only={"dW", "dX", "dR", "dBC", "dBC1", "dOut",
                                        "dMsc", "dSt", "dM0", "dM1", "dS"})

        @sec("gpsimd")
        def _(g):
            wait_led(g, "final", only={"dG"})

        if pass2:
            for engname in ["sync", "tensor", "vector", "scalar", "gpsimd"]:
                fns = thunks[engname]

                @getattr(block, engname)
                def _(eng, fns=fns):
                    for fn in fns:
                        fn(eng)

    return nc, ev


def kernel(L0_attn_x, L0_attn_ln_gamma, L0_attn_ln_beta, L0_attn_W_in,
           L0_attn_conv_w, L0_attn_conv_b, L0_attn_W_xproj, L0_attn_W_dt,
           L0_attn_b_dt, L0_attn_A_log, L0_attn_D_param, L0_attn_W_out,
           L0_ffn_weight_int8, L0_ffn_weight_scale, L0_ffn_bias, L0_c, L0_wmod):
    _, ev1 = build_program()
    nc, _ = build_program(ev1)

    def vec_pack(v):
        return np.ascontiguousarray(np.asarray(v, np.float32).reshape(NC_DM, 128).T)

    x = np.asarray(L0_attn_x, np.float32)
    W_in = np.asarray(L0_attn_W_in, np.float32)
    conv_w = np.asarray(L0_attn_conv_w, np.float32)
    conv_b = np.asarray(L0_attn_conv_b, np.float32)
    W_xp = np.asarray(L0_attn_W_xproj, np.float32)
    W_dt = np.asarray(L0_attn_W_dt, np.float32)
    b_dt = np.asarray(L0_attn_b_dt, np.float32)
    A_log = np.asarray(L0_attn_A_log, np.float32)
    D_par = np.asarray(L0_attn_D_param, np.float32)
    W_out = np.asarray(L0_attn_W_out, np.float32)
    w8 = np.asarray(L0_ffn_weight_int8, np.int8)
    wscale = np.asarray(L0_ffn_weight_scale, np.float32)
    fb = np.asarray(L0_ffn_bias, np.float32)
    cc = np.asarray(L0_c, np.float32)
    wmod = np.ascontiguousarray(np.asarray(L0_wmod, np.float32))

    in_maps = []
    for core in range(8):
        b = core // 2
        hh = core % 2
        dsl = slice(hh * DH, (hh + 1) * DH)
        w_in_sl = np.concatenate([W_in[hh * DH:(hh + 1) * DH],
                                  W_in[2048 + hh * DH:2048 + (hh + 1) * DH]], 0)
        cw = conv_w[dsl, 0, :]
        m = dict(
            xin=np.ascontiguousarray(x[b].T),
            w_in_t=np.ascontiguousarray(w_in_sl.T),
            wxp_t=np.ascontiguousarray(W_xp[:, dsl].T),
            wdt_t=np.ascontiguousarray(W_dt[dsl].T),
            wout_t=np.ascontiguousarray(W_out[:, dsl].T),
            wmod=wmod,
            wffn_t=np.ascontiguousarray(w8.T),
            convw=np.ascontiguousarray(
                cw.reshape(NC_DH, 128, 4).transpose(1, 0, 2).reshape(128, NC_DH * 4)),
            convb=vec_pack(conv_b[dsl]),
            bdt=vec_pack(b_dt[dsl]),
            alog=np.ascontiguousarray(
                A_log[dsl].reshape(NC_DH, 128, NST).transpose(1, 0, 2)
                .reshape(128, NC_DH * NST)),
            dpar=vec_pack(D_par[dsl]),
            lngam=vec_pack(np.asarray(L0_attn_ln_gamma, np.float32)),
            lnbet=vec_pack(np.asarray(L0_attn_ln_beta, np.float32)),
            wsc=vec_pack(wscale[:, 0]),
            fbias=vec_pack(fb),
            cvec=vec_pack(cc[b]),
        )
        in_maps.append(m)

    res = run_bass_kernel_spmd(nc, in_maps, core_ids=list(range(8)))
    out = np.empty((4, S, DM), np.float32)
    for b in range(4):
        out[b] = res.results[2 * b]["xout"].T
    return out


# version 19
# speedup vs baseline: 1.2146x; 1.0004x over previous
import numpy as np
import concourse.bass as bass
import concourse.mybir as mybir
from concourse.bass import AP
from concourse.bass_utils import run_bass_kernel_spmd

F32 = mybir.dt.float32
BF16 = mybir.dt.bfloat16
I8 = mybir.dt.int8
I32 = mybir.dt.int32
AF = mybir.ActivationFunctionType
ALU = mybir.AluOpType
AXX = mybir.AxisListType.X

DM = 1024
DH = 1024
NC_DM = DM // 128
NC_DH = DH // 128
NST = 16
S = 8192
TT = 256
NT = S // TT
EPS = 1e-5

DBG = False


class _DummyInstr:
    def then_inc(self, *a, **k):
        return self


class _DummyEngine:
    def __getattr__(self, name):
        def f(*a, **k):
            return _DummyInstr()
        return f


def build_program(ev_table=None):
    pass2 = ev_table is not None
    nc = bass.Bass("TRN2", target_bir_lowering=False)

    xin = nc.dram_tensor("xin", [DM, S], F32, kind="ExternalInput").ap()
    w_in_t = nc.dram_tensor("w_in_t", [DM, 2 * DH], F32, kind="ExternalInput").ap()
    wxp_t = nc.dram_tensor("wxp_t", [DH, 160], F32, kind="ExternalInput").ap()
    wdt_t = nc.dram_tensor("wdt_t", [128, DH], F32, kind="ExternalInput").ap()
    wout_t = nc.dram_tensor("wout_t", [DH, DM], F32, kind="ExternalInput").ap()
    wmod = nc.dram_tensor("wmod", [DM, 6 * DM], F32, kind="ExternalInput").ap()
    wffn_t = nc.dram_tensor("wffn_t", [DM, DM], I8, kind="ExternalInput").ap()
    convw = nc.dram_tensor("convw", [128, NC_DH * 4], F32, kind="ExternalInput").ap()
    convb = nc.dram_tensor("convb", [128, NC_DH], F32, kind="ExternalInput").ap()
    bdt = nc.dram_tensor("bdt", [128, NC_DH], F32, kind="ExternalInput").ap()
    alog = nc.dram_tensor("alog", [128, NC_DH * NST], F32, kind="ExternalInput").ap()
    dpar = nc.dram_tensor("dpar", [128, NC_DH], F32, kind="ExternalInput").ap()
    lngam = nc.dram_tensor("lngam", [128, NC_DM], F32, kind="ExternalInput").ap()
    lnbet = nc.dram_tensor("lnbet", [128, NC_DM], F32, kind="ExternalInput").ap()
    wsc = nc.dram_tensor("wsc", [128, NC_DM], F32, kind="ExternalInput").ap()
    fbias = nc.dram_tensor("fbias", [128, NC_DM], F32, kind="ExternalInput").ap()
    cvec = nc.dram_tensor("cvec", [128, NC_DM], F32, kind="ExternalInput").ap()

    xout = nc.dram_tensor("xout", [DM, S], F32, kind="ExternalOutput").ap()
    dbg = nc.dram_tensor("dbg", [128, 8 * TT], F32, kind="ExternalOutput").ap() if DBG else None

    g_buf = nc.dram_tensor("g_buf", [NT, DH * TT], F32).ap()
    r0_buf = nc.dram_tensor("r0_buf", [NT, DM * TT], F32).ap()
    x2_buf = nc.dram_tensor("x2_buf", [NT, DM * TT], F32).ap()
    arp_i = nc.dram_tensor("arp_i", [NT, 160 * TT], F32).ap()
    arp_o = nc.dram_tensor("arp_o", [NT, 160 * TT], F32).ap()
    aro_i = nc.dram_tensor("aro_i", [NT, DM * TT], F32).ap()
    aro_o = nc.dram_tensor("aro_o", [NT, DM * TT], F32).ap()
    mx_i = nc.dram_tensor("mx_i", [1, 1], F32).ap()
    mx_o = nc.dram_tensor("mx_o", [1, 1], F32).ap()
    st_buf = nc.dram_tensor("st_buf", [NT, 2 * TT], F32).ap()

    from contextlib import ExitStack

    led = {}
    ev = {}

    def bump(sem_name, amt):
        if pass2:
            return 0
        led[sem_name] = led.get(sem_name, 0) + amt
        return led[sem_name]

    def mark(name, sem_name):
        if pass2:
            return
        ev[name] = (sem_name, led.get(sem_name, 0))

    def alias(dst, src):
        if pass2:
            return
        ev[dst] = ev[src]

    def mark_led(name):
        if pass2:
            return
        ev[name] = dict(led)

    def wait(eng, name):
        tbl = ev_table or ev
        if not pass2 and name not in tbl:
            return
        s, v = tbl[name]
        if v > 0:
            eng.wait_ge(sems[s], v)

    def wait_led(eng, name, only=None):
        for snm, val in (ev_table or ev)[name].items():
            if val > 0 and (only is None or snm in only):
                eng.wait_ge(sems[snm], val)

    sems = {}


    with ExitStack() as ctx:
        for sname in ["dS", "dG", "dM0", "dM1", "dW", "dX", "dR", "dBC", "dBC1",
                      "dOut", "dMsc", "dSt", "sP", "sV", "sA", "sG", "sC"]:
            sems[sname] = ctx.enter_context(nc.semaphore(sname))

        def inc(instr, sname):
            instr.then_inc(sems[sname], 1)
            return bump(sname, 1)

        def dma(eng, out, in_, q="dS"):
            eng.dma_start(out=out, in_=in_).then_inc(sems[q], 16)
            return bump(q, 16)

        sb = {}
        actx = ctx.enter_context(ExitStack())

        def alloc(name, shape, dtype=F32, scope=None):
            tgt = scope if scope is not None else ctx
            sb[name] = tgt.enter_context(nc.sbuf_tensor(name, shape, dtype))
            return sb[name]

        vecs = alloc("vecs", [128, 16 * NC_DM])
        aneg = alloc("aneg", [128, NC_DH * NST])
        ones1 = alloc("ones1", [128, 1])
        onesr = alloc("onesr", [1, 128])
        lastst = alloc("lastst", [128, NC_DH * NST])

        VOFF = {}
        for i, nm in enumerate(["convb", "bdt", "dpar", "lngam", "lnbet", "wsc",
                                "fbias", "sa", "ca1", "gap", "sbv", "cb1", "gbp",
                                "gbw", "gbb", "cvec"]):
            VOFF[nm] = i * NC_DM

        def vcol(nm, c):
            return sb["vecs"][:, VOFF[nm] + c:VOFF[nm] + c + 1]

        mu_a = alloc("mu_a", [1, TT])
        r_a = alloc("r_a", [1, TT])
        stt1 = alloc("stt1", [1, TT])
        stt2 = alloc("stt2", [1, TT])
        murep = alloc("murep", [128, TT])
        rrep = alloc("rrep", [128, TT])
        mxacc = alloc("mxacc", [128, 1])
        mxrow = alloc("mxrow", [1, 130])
        xsq = alloc("xsq", [128, 2 * TT])

        alloc("w_in_sb", [128, NC_DM * 2 * DH], scope=actx)
        alloc("wxp_sb", [128, NC_DH * 160], scope=actx)
        alloc("wdt_sb", [128, DH], scope=actx)
        alloc("convw_sb", [128, NC_DH * 4], scope=actx)
        alloc("x_t", [128, NC_DM * TT], scope=actx)
        alloc("h1_t", [128, NC_DM * TT], scope=actx)
        alloc("r0_t", [128, NC_DM * TT], scope=actx)
        sb["xn_t"] = sb["r0_t"]
        alloc("xs_t", [128, NC_DH * (TT + 3)], scope=actx)
        alloc("scrA", [128, NC_DH * TT], scope=actx)
        alloc("scrB", [128, 8 * TT], scope=actx)
        alloc("scrC", [128, 4 * TT], scope=actx)
        alloc("delta_t", [128, NC_DH * TT], scope=actx)
        alloc("du_t", [128, NC_DH * TT], scope=actx)
        alloc("HC0", [128, 4 * TT], scope=actx)
        alloc("HC1", [128, 4 * TT], scope=actx)
        alloc("sz_t", [128, NC_DH * TT], scope=actx)
        alloc("H_t", [128, 4 * TT], scope=actx)
        alloc("H1", [128, 4 * TT], scope=actx)
        alloc("scrC1", [128, 4 * TT], scope=actx)
        alloc("y_t", [128, NC_DH * TT], scope=actx)
        alloc("pdt_t", [128, TT], scope=actx)
        alloc("pbc_t", [32, TT], scope=actx)
        alloc("dt_sb", [128, TT], scope=actx)
        alloc("bc_sb", [32, TT], scope=actx)
        s0ctx = actx.enter_context(ExitStack())
        alloc("cs_t", [128, NC_DM], scope=s0ctx)
        alloc("modrow", [1, 1024], scope=s0ctx)
        alloc("wm0", [128, 512], scope=s0ctx)
        alloc("wm1", [128, 512], scope=s0ctx)

        ps = {}
        for nm, shape in [("smu", [1, TT]), ("sm2", [1, TT]), ("bc", [128, 2 * TT]),
                          ("xz0", [128, TT]), ("xz1", [128, TT]),
                          ("pdt", [128, TT]), ("pbc", [32, TT])]:
            ps[nm] = ctx.enter_context(nc.psum_tensor("ps_" + nm, shape, F32))
        ps["mu"] = ps["smu"][0:1, :]
        ps["m2"] = ps["sm2"][0:1, :]
        ps["mod"] = ps["bc"][0:1, 0:512]
        ps["dl0"] = ps["xz0"]
        ps["dl1"] = ps["xz1"]

        block = ctx.enter_context(nc.Block())
        thunks = {k: [] for k in ["sync", "tensor", "vector", "scalar", "gpsimd"]}
        _dummy = _DummyEngine()

        def sec(engname):
            if not pass2:
                def dec(fn):
                    fn(_dummy)
                return dec

            def dec(fn):
                thunks[engname].append(fn)
            return dec

        @sec("sync")
        def _(sy):
            dma(sy, sb["w_in_sb"][:].rearrange("p (c o) -> p c o", c=NC_DM),
                w_in_t[:, :].rearrange("(c p) o -> p c o", p=128), q="dW")
            dma(sy, sb["wxp_sb"][:].rearrange("p (c o) -> p c o", c=NC_DH),
                wxp_t[:, :].rearrange("(c p) o -> p c o", p=128), q="dW")
            dma(sy, sb["wdt_sb"][:], wdt_t, q="dW")
            dma(sy, sb["convw_sb"][:], convw, q="dW")
            dma(sy, sb["aneg"][:], alog, q="dW")
            for nm, src in [("convb", convb), ("bdt", bdt), ("dpar", dpar),
                            ("lngam", lngam), ("lnbet", lnbet), ("wsc", wsc),
                            ("fbias", fbias), ("cvec", cvec)]:
                dma(sy, sb["vecs"][:, VOFF[nm]:VOFF[nm] + NC_DM], src, q="dW")
            mark("weights", "dW")

        @sec("vector")
        def _(v):
            v.memset(sb["ones1"][:], 1.0)
            v.memset(sb["onesr"][:], 1.0)
            v.memset(sb["lastst"][:], 0.0)
            wait(v, "weights")
            inc(v.memset(sb["mxacc"][:], 0.0), "sV")
            mark("v_setup", "sV")

        @sec("scalar")
        def _(sc):
            wait(sc, "weights")
            sc.activation(out=sb["cs_t"][:], in_=sb["vecs"][:, VOFF["cvec"]:VOFF["cvec"] + NC_DM],
                          func=AF.Silu)
            inc(sc.activation(out=sb["aneg"][:], in_=sb["aneg"][:], func=AF.Exp), "sA")
            mark("a_setup", "sA")

        @sec("vector")
        def _(v):
            wait(v, "a_setup")
            inc(v.tensor_scalar(out=sb["aneg"][:], in0=sb["aneg"][:], scalar1=-1.0,
                                scalar2=None, op0=ALU.mult), "sV")
            mark("aneg_done", "sV")

        NJ = 6 * DM // 512
        QNAMES = ["sa", "ca1", "gap", "sbv", "cb1", "gbp"]
        for j in range(NJ):
            for k in range(NC_DM):
                idx = j * NC_DM + k

                @sec("sync")
                def _(sy, j=j, k=k, idx=idx):
                    if idx == 0:
                        wait(sy, "a_setup")
                    if idx >= 2:
                        wait(sy, f"modmm_{idx - 2}")
                    wm = [sb["wm0"], sb["wm1"]][idx % 2]
                    dma(sy, wm[:], wmod[k * 128:(k + 1) * 128, j * 512:(j + 1) * 512],
                        q="dM%d" % (idx % 2))
                    mark(f"modld_{idx}", "dM%d" % (idx % 2))

                @sec("tensor")
                def _(te, j=j, k=k, idx=idx):
                    wm = [sb["wm0"], sb["wm1"]][idx % 2]
                    wait(te, f"modld_{idx}")
                    if k == 0 and j > 0:
                        wait(te, f"modev_{j - 1}")
                    te.matmul(ps["mod"], sb["cs_t"][:, k:k + 1], wm[:],
                              start=(k == 0), stop=(k == NC_DM - 1)).then_inc(sems["sP"], 1)
                    bump("sP", 1)
                    mark(f"modmm_{idx}", "sP")
                    if k == NC_DM - 1:
                        mark(f"modcol_{j}", "sP")

            @sec("vector")
            def _(v, j=j):
                wait(v, f"modcol_{j}")
                if j >= 2:
                    wait(v, f"modsc_{(j - 2) // 2}")
                inc(v.tensor_copy(sb["modrow"][0:1, (j % 2) * 512:(j % 2) * 512 + 512],
                                  ps["mod"]), "sV")
                mark(f"modev_{j}", "sV")

            if j % 2 == 1:
                q = j // 2

                @sec("sync")
                def _(sy, q=q, j=j):
                    wait(sy, f"modev_{j}")
                    for jj in range(NC_DM):
                        sy.dma_start(out=sb["vecs"][:, VOFF[QNAMES[q]] + jj:VOFF[QNAMES[q]] + jj + 1],
                                     in_=sb["modrow"][0:1, jj * 128:(jj + 1) * 128],
                                     allow_slow_non_contiguous=True).then_inc(sems["dMsc"], 16)
                        bump("dMsc", 16)
                    mark(f"modsc_{q}", "dMsc")
        mark("mod_scattered", "dMsc")

        @sec("vector")
        def _(v):
            wait(v, "mod_scattered")
            V = sb["vecs"]
            sl = lambda nm: V[:, VOFF[nm]:VOFF[nm] + NC_DM]
            v.tensor_scalar(out=sl("ca1"), in0=sl("ca1"), scalar1=1.0, scalar2=None, op0=ALU.add)
            v.tensor_scalar(out=sl("cb1"), in0=sl("cb1"), scalar1=1.0, scalar2=None, op0=ALU.add)
            for nm in ["gap", "gbp"]:
                g = sl(nm)
                v.drain()
                v.tensor_scalar(out=sb["murep"][:, 0:NC_DM], in0=g, scalar1=0.0,
                                scalar2=None, op0=ALU.is_ge)
                v.drain()
                v.tensor_scalar(out=sb["murep"][:, 0:NC_DM], in0=sb["murep"][:, 0:NC_DM],
                                scalar1=2.0, scalar2=-1.0, op0=ALU.mult, op1=ALU.add)
                v.tensor_scalar(out=sb["rrep"][:, 0:NC_DM], in0=g, scalar1=-1.0,
                                scalar2=None, op0=ALU.mult)
                v.drain()
                v.tensor_tensor(out=sb["rrep"][:, 0:NC_DM], in0=g, in1=sb["rrep"][:, 0:NC_DM],
                                op=ALU.max)
                v.drain()
                v.tensor_scalar(out=sb["rrep"][:, 0:NC_DM], in0=sb["rrep"][:, 0:NC_DM],
                                scalar1=0.25, scalar2=None, op0=ALU.max)
                v.drain()
                v.tensor_tensor(out=g, in0=sb["murep"][:, 0:NC_DM],
                                in1=sb["rrep"][:, 0:NC_DM], op=ALU.mult)
            v.drain()
            v.tensor_tensor(out=sl("gbw"), in0=sl("gbp"), in1=sl("wsc"), op=ALU.mult)
            inc(v.tensor_tensor(out=sl("gbb"), in0=sl("gbp"), in1=sl("fbias"),
                                op=ALU.mult), "sV")
            mark("mod_done", "sV")

        s0ctx.close()
        alloc("bcrep0", [128, 2 * 4 * TT], scope=actx)
        alloc("bcrep1", [128, 2 * 4 * TT], scope=actx)
        mark_led("bar0")

        def wait_bar0(eng):
            wait_led(eng, "bar0")

        def tsl(tau):
            return slice(tau * TT, (tau + 1) * TT)

        def csl(tile, c, width=TT):
            return tile[:, c * width:(c + 1) * width]

        def stats_round(tag, tau, src_name, sc_first=None, te_first=None):
            for c in range(NC_DM):
                @sec("scalar")
                def _(sc, c=c):
                    if c == 0 and sc_first is not None:
                        sc_first(sc)
                    if c >= 2:
                        wait(sc, f"{tag}m2_{tau}_{c - 2}")
                    sc.activation(out=csl(sb["xsq"], c % 2), in_=csl(sb[src_name], c),
                                  func=AF.Square).then_inc(sems["sA"], 1)
                    bump("sA", 1)
                    mark(f"{tag}sq_{tau}_{c}", "sA")

                @sec("tensor")
                def _(te, c=c):
                    if c == 0 and te_first is not None:
                        te_first(te)
                    te.matmul(ps["mu"], sb["ones1"][:], csl(sb[src_name], c),
                              start=(c == 0), stop=(c == NC_DM - 1)).then_inc(sems["sP"], 1)
                    bump("sP", 1)
                    wait(te, f"{tag}sq_{tau}_{c}")
                    te.matmul(ps["m2"], sb["ones1"][:], csl(sb["xsq"], c % 2),
                              start=(c == 0), stop=(c == NC_DM - 1)).then_inc(sems["sP"], 1)
                    bump("sP", 1)
                    mark(f"{tag}m2_{tau}_{c}", "sP")
            mark(f"stats_{tag}_{tau}", "sP")

        def stats_post(v, tau, tag):
            v.drain()
            v.tensor_scalar(out=sb["mu_a"][:], in0=ps["mu"], scalar1=1.0 / DM,
                            scalar2=None, op0=ALU.mult)
            v.tensor_scalar(out=sb["stt1"][:], in0=ps["m2"], scalar1=1.0 / DM,
                            scalar2=None, op0=ALU.mult)
            v.drain()
            v.tensor_tensor(out=sb["stt2"][:], in0=sb["mu_a"][:], in1=sb["mu_a"][:],
                            op=ALU.mult)
            v.drain()
            v.tensor_tensor(out=sb["stt1"][:], in0=sb["stt1"][:], in1=sb["stt2"][:],
                            op=ALU.subtract)
            v.drain()
            inc(v.tensor_scalar(out=sb["stt1"][:], in0=sb["stt1"][:], scalar1=EPS,
                                scalar2=None, op0=ALU.add), "sV")
            mark(f"vpre_{tag}_{tau}", "sV")


        def newton(v, tau, tag):
            alias(f"post_{tag}_{tau}", f"rstd_{tag}_{tau}")


        def bcast(te, tau, tag):
            te.matmul(ps["bc"][:, 0:TT], sb["onesr"][:], sb["mu_a"][:],
                      start=True, stop=True).then_inc(sems["sP"], 1)
            bump("sP", 1)
            te.matmul(ps["bc"][:, TT:2 * TT], sb["onesr"][:], sb["r_a"][:],
                      start=True, stop=True).then_inc(sems["sP"], 1)
            bump("sP", 1)
            mark(f"bc_{tag}_{tau}", "sP")


        def rep_evac(v, tau, tag):
            v.drain()
            v.tensor_copy(sb["murep"][:], ps["bc"][:, 0:TT])
            inc(v.tensor_copy(sb["rrep"][:], ps["bc"][:, TT:2 * TT]), "sV")
            mark(f"rep_{tag}_{tau}", "sV")


        def stage_front(tau):
            @sec("sync")
            def _(sy, tau=tau):
                if tau == 0:
                    wait_bar0(sy)
                else:
                    wait(sy, f"r0c_{tau - 1}")
                dma(sy, sb["x_t"][:].rearrange("p (c t) -> p c t", c=NC_DM),
                    xin[:, tsl(tau)].rearrange("(c p) t -> p c t", p=128), q="dX")
                mark(f"xld_{tau}", "dX")

            def _scf1(sc, tau=tau):
                wait(sc, f"xld_{tau}")
                if tau == 0:
                    wait_bar0(sc)

            def _tef1(te, tau=tau):
                if tau == 0:
                    wait_bar0(te)
                else:
                    wait(te, f"post2_{tau - 1}")
                wait(te, f"xld_{tau}")

            stats_round("s1", tau, "x_t", sc_first=_scf1, te_first=_tef1)
            alias(f"stats1_{tau}", f"stats_s1_{tau}")

            @sec("vector")
            def _(v, tau=tau):
                wait(v, f"stats1_{tau}")
                stats_post(v, tau, "ln1")

            @sec("scalar")
            def _(sc, tau=tau):
                wait(sc, f"vpre_ln1_{tau}")
                sc.activation(out=sb["stt2"][:], in_=sb["stt1"][:], func=AF.Ln)
                sc.drain()
                sc.activation(out=sb["r_a"][:], in_=sb["stt2"][:], func=AF.Exp,
                              scale=-0.5).then_inc(sems["sA"], 1)
                bump("sA", 1)
                mark(f"rstd_ln1_{tau}", "sA")

            @sec("vector")
            def _(v, tau=tau):
                wait(v, f"rstd_ln1_{tau}")
                newton(v, tau, "ln1")
            alias(f"post1_{tau}", f"post_ln1_{tau}")

            @sec("tensor")
            def _(te, tau=tau):
                wait(te, f"post1_{tau}")
                if tau > 0:
                    wait(te, f"rep2_{tau - 1}")
                bcast(te, tau, "ln1")

            @sec("vector")
            def _(v, tau=tau):
                wait(v, f"bc_ln1_{tau}")
                rep_evac(v, tau, "ln1")
            alias(f"rep1_{tau}", f"rep_ln1_{tau}")

            @sec("vector")
            def _(v, tau=tau):
                V = sb["vecs"]
                if tau > 0:
                    wait(v, f"r0sp_{tau - 1}")
                v.drain()
                for c in range(NC_DM):
                    v.tensor_tensor(out=csl(sb["xn_t"], c), in0=csl(sb["x_t"], c),
                                    in1=sb["murep"][:], op=ALU.subtract)
                v.drain()
                for c in range(NC_DM):
                    v.tensor_tensor(out=csl(sb["xn_t"], c), in0=csl(sb["xn_t"], c),
                                    in1=sb["rrep"][:], op=ALU.mult)
                if tau > 0:
                    wait(v, f"xzmm_{tau - 1}_15")
                v.drain()
                for c in range(NC_DM):
                    v.tensor_scalar(out=csl(sb["h1_t"], c), in0=csl(sb["xn_t"], c),
                                    scalar1=vcol("ca1", c), scalar2=vcol("sa", c),
                                    op0=ALU.mult, op1=ALU.add)
                v.drain()
                for c in range(NC_DM):
                    li = v.scalar_tensor_tensor(out=csl(sb["r0_t"], c), in0=csl(sb["h1_t"], c),
                                                scalar=vcol("gap", c), in1=csl(sb["x_t"], c),
                                                op0=ALU.mult, op1=ALU.add)
                inc(li, "sV")
                mark(f"r0c_{tau}", "sV")

            @sec("sync")
            def _(sy, tau=tau):
                wait(sy, f"r0c_{tau}")
                dma(sy, r0_buf[tau:tau + 1, :].rearrange("o (p c t) -> o p c t", p=128, c=NC_DM),
                    sb["r0_t"][:].rearrange("p (c t) -> p c t", c=NC_DM), q="dR")
                mark(f"r0sp_{tau}", "dR")

            def _scf2(sc, tau=tau):
                wait(sc, f"r0c_{tau}")

            def _tef2(te, tau=tau):
                wait(te, f"post1_{tau}")
                wait(te, f"r0c_{tau}")

            stats_round("s2", tau, "h1_t", sc_first=_scf2, te_first=_tef2)
            alias(f"stats2_{tau}", f"stats_s2_{tau}")

            @sec("vector")
            def _(v, tau=tau):
                wait(v, f"stats2_{tau}")
                stats_post(v, tau, "ln2i")

            @sec("scalar")
            def _(sc, tau=tau):
                wait(sc, f"vpre_ln2i_{tau}")
                sc.activation(out=sb["stt2"][:], in_=sb["stt1"][:], func=AF.Ln)
                sc.drain()
                sc.activation(out=sb["r_a"][:], in_=sb["stt2"][:], func=AF.Exp,
                              scale=-0.5).then_inc(sems["sA"], 1)
                bump("sA", 1)
                mark(f"rstd_ln2i_{tau}", "sA")

            @sec("vector")
            def _(v, tau=tau):
                wait(v, f"rstd_ln2i_{tau}")
                newton(v, tau, "ln2i")
            alias(f"post2_{tau}", f"post_ln2i_{tau}")

            @sec("tensor")
            def _(te, tau=tau):
                wait(te, f"post2_{tau}")
                wait(te, f"rep1_{tau}")
                bcast(te, tau, "ln2i")

            @sec("vector")
            def _(v, tau=tau):
                wait(v, f"bc_ln2i_{tau}")
                rep_evac(v, tau, "ln2i")
            alias(f"rep2_{tau}", f"rep_ln2i_{tau}")

            @sec("vector")
            def _(v, tau=tau):
                v.drain()
                for c in range(NC_DM):
                    v.tensor_tensor(out=csl(sb["h1_t"], c), in0=csl(sb["h1_t"], c),
                                    in1=sb["murep"][:], op=ALU.subtract)
                v.drain()
                for c in range(NC_DM):
                    v.tensor_tensor(out=csl(sb["h1_t"], c), in0=csl(sb["h1_t"], c),
                                    in1=sb["rrep"][:], op=ALU.mult)
                v.drain()
                for c in range(NC_DM):
                    li = v.tensor_scalar(out=csl(sb["h1_t"], c), in0=csl(sb["h1_t"], c),
                                         scalar1=vcol("lngam", c), scalar2=vcol("lnbet", c),
                                         op0=ALU.mult, op1=ALU.add)
                inc(li, "sV")
                mark(f"xn2_{tau}", "sV")

            @sec("vector")
            def _(v, tau=tau):
                if tau > 0:
                    wait(v, f"conv_{tau - 1}")
                    src = AP(sb["xs_t"][:].tensor, TT,
                             [[NC_DH * (TT + 3), 128], [TT + 3, NC_DH], [1, 3]])
                    dst = AP(sb["xs_t"][:].tensor, 0,
                             [[NC_DH * (TT + 3), 128], [TT + 3, NC_DH], [1, 3]])
                    inc(v.tensor_copy(dst, src), "sV")
                else:
                    inc(v.memset(AP(sb["xs_t"][:].tensor, 0,
                                    [[NC_DH * (TT + 3), 128], [TT + 3, NC_DH], [1, 3]]),
                                 0.0), "sV")
                mark(f"ctx_{tau}", "sV")


        def emit_winevac(tau, o):
            @sec("scalar")
            def _(sc, tau=tau, o=o):
                wait(sc, f"xzmm_{tau}_{o}")
                pst = ps["xz0"] if o % 2 == 0 else ps["xz1"]
                if o == 0:
                    wait(sc, f"ctx_{tau}")
                li = sc.copy(sb["xs_t"][:, o * (TT + 3) + 3:(o + 1) * (TT + 3)],
                             pst[:])
                inc(li, "sA")
                mark(f"xzev_{tau}_{o}", "sA")
                if o == 7:
                    mark(f"xsv_{tau}", "sA")

        def stage_win(tau):
            for o in range(16):
                @sec("tensor")
                def _(te, tau=tau, o=o):
                    if o == 0:
                        wait(te, f"xn2_{tau}")
                    if o < 2 and tau > 0:
                        wait(te, f"sp_{tau - 1}_{o + 6}")
                    if o >= 2 or tau > 0:
                        oo, tt2 = o - 2, tau
                        if oo < 0:
                            oo += 16
                            tt2 -= 1
                        wait(te, f"xzev_{tt2}_{oo}")
                    pst = ps["xz0"] if o % 2 == 0 else ps["xz1"]
                    for k in range(NC_DM):
                        li = te.matmul(pst[:], sb["w_in_sb"][:, k * 2048 + o * 128:
                                                             k * 2048 + o * 128 + 128],
                                       csl(sb["h1_t"], k), start=(k == 0),
                                       stop=(k == NC_DM - 1))
                    inc(li, "sP")
                    mark(f"xzmm_{tau}_{o}", "sP")

                if o >= 8:
                    @sec("scalar")
                    def _(sc, tau=tau, o=o):
                        wait(sc, f"xzmm_{tau}_{o}")
                        pst = ps["xz0"] if o % 2 == 0 else ps["xz1"]
                        if tau > 0 and o == 8:
                            wait(sc, f"g_{tau - 1}")
                        li = sc.activation(out=csl(sb["sz_t"], o - 8), in_=pst[:],
                                           func=AF.Silu)
                        inc(li, "sA")
                        mark(f"xzev_{tau}_{o}", "sA")
                        if o == 15:
                            mark(f"xsev_{tau}", "sA")
                elif tau == 0:
                    emit_winevac(0, o)


        def stage_tail(tau):
            for p in range(4):
                @sec("vector")
                def _(v, tau=tau, p=p):
                    if p == 0:
                        wait(v, f"xsv_{tau}")
                        if tau > 0:
                            wait(v, f"exp_{tau - 1}_7_3")
                    v.drain()
                    for j in range(4):
                        if j > 0:
                            v.drain()
                        for c in (2 * p, 2 * p + 1):
                            base = c * (TT + 3)
                            if j == 0:
                                li = v.tensor_scalar(
                                    out=csl(sb["scrB"], c),
                                    in0=sb["xs_t"][:, base:base + TT],
                                    scalar1=sb["convw_sb"][:, 4 * c:4 * c + 1],
                                    scalar2=vcol("convb", c),
                                    op0=ALU.mult, op1=ALU.add)
                            else:
                                li = v.scalar_tensor_tensor(
                                    out=csl(sb["scrB"], c),
                                    in0=sb["xs_t"][:, base + j:base + j + TT],
                                    scalar=sb["convw_sb"][:, 4 * c + j:4 * c + j + 1],
                                    in1=csl(sb["scrB"], c),
                                    op0=ALU.mult, op1=ALU.add)
                    inc(li, "sV")
                    mark(f"conv_{tau}_{p}", "sV")
                    if p == 3:
                        mark(f"conv_{tau}", "sV")

                @sec("scalar")
                def _(sc, tau=tau, p=p):
                    wait(sc, f"conv_{tau}_{p}")
                    if p == 0 and tau > 0:
                        wait(sc, f"gsp_{tau - 1}")
                    inc(sc.activation(out=sb["scrA"][:, 2 * p * TT:(2 * p + 2) * TT],
                                      in_=sb["scrB"][:, 2 * p * TT:(2 * p + 2) * TT],
                                      func=AF.Silu), "sA")
                    mark(f"u_{tau}_{p}", "sA")
                    if p == 3:
                        mark(f"u_{tau}", "sA")

                @sec("tensor")
                def _(te, tau=tau, p=p):
                    wait(te, f"u_{tau}_{p}")
                    if p == 0 and tau > 0:
                        wait(te, f"projev_{tau - 1}")
                    for k in (2 * p, 2 * p + 1):
                        te.matmul(ps["pdt"][:], sb["wxp_sb"][:, k * 160:k * 160 + 128],
                                  csl(sb["scrA"], k), start=(k == 0),
                                  stop=(k == NC_DH - 1))
                        li = te.matmul(ps["pbc"][:],
                                       sb["wxp_sb"][:, k * 160 + 128:k * 160 + 160],
                                       csl(sb["scrA"], k), start=(k == 0),
                                       stop=(k == NC_DH - 1))
                    if p == 3:
                        inc(li, "sP")
                        mark(f"projmm_{tau}", "sP")

            @sec("vector")
            def _(v, tau=tau):
                wait(v, f"projmm_{tau}")
                if tau > 0:
                    wait(v, f"prst_{tau - 1}")
                v.tensor_copy(sb["pdt_t"][:], ps["pdt"][:])
                inc(v.tensor_copy(sb["pbc_t"][:], ps["pbc"][:]), "sV")
                mark(f"projev_{tau}", "sV")

            @sec("gpsimd")
            def _(g, tau=tau):
                wait(g, f"projev_{tau}")
                dma(g, arp_i[tau:tau + 1, :].rearrange("o (p t) -> o p t", p=160)[:, 0:128, :],
                    sb["pdt_t"][:], q="dG")
                dma(g, arp_i[tau:tau + 1, :].rearrange("o (p t) -> o p t", p=160)[:, 128:160, :],
                    sb["pbc_t"][:], q="dG")
                mark(f"prst_{tau}", "dG")
                wait(g, f"prst_{tau}")
                g.collective_compute(
                    "AllReduce", ALU.add,
                    replica_groups=[[0, 1], [2, 3], [4, 5], [6, 7]],
                    ins=[arp_i[tau:tau + 1, :].opt()],
                    outs=[arp_o[tau:tau + 1, :].opt()],
                ).then_inc(sems["sC"], 1)
                bump("sC", 1)
                mark(f"ccp_{tau}", "sC")
                if tau > 0:
                    wait(g, f"wdtmm_{tau - 1}")
                    wait(g, f"bcfl_{tau - 1}_3")
                wait(g, f"ccp_{tau}")
                dma(g, sb["dt_sb"][:],
                    arp_o[tau:tau + 1, :].rearrange("o (p t) -> o p t", p=160)[:, 0:128, :],
                    q="dG")
                dma(g, sb["bc_sb"][:],
                    arp_o[tau:tau + 1, :].rearrange("o (p t) -> o p t", p=160)[:, 128:160, :],
                    q="dG")
                mark(f"dtbc_{tau}", "dG")


        def punit(tau, h, k, n):
            idx = h * 8 + k - n
            t2 = tau
            if idx < 0:
                idx += 32
                t2 -= 1
            return t2, idx // 8, idx % 8

        def hctree(v, tau, h, k):
            hcp = sb["HC0"]
            Hp = sb["H_t"] if k % 2 == 0 else sb["H1"]
            bcr = sb["bcrep0"] if h % 2 == 0 else sb["bcrep1"]
            v.drain()
            v.tensor_tensor(out=hcp[:], in0=Hp[:],
                            in1=bcr[:, 4 * TT:8 * TT], op=ALU.mult)
            v.drain()
            hv = AP(hcp[:].tensor, 0, [[4 * TT, 128], [1, TT], [TT, 4]])
            if h == 0:
                li = v.tensor_reduce(out=csl(sb["y_t"], k), in_=hv, axis=AXX,
                                     op=ALU.add)
            else:
                v.tensor_reduce(out=sb["murep"][:], in_=hv, axis=AXX,
                                op=ALU.add)
                v.drain()
                li = v.tensor_tensor(out=csl(sb["y_t"], k), in0=csl(sb["y_t"], k),
                                     in1=sb["murep"][:], op=ALU.add)
            inc(li, "sV")
            mark(f"tree_{tau}_{k}_{h}", "sV")

        def stage_scanblk(tau):
            for o in range(NC_DH):
                @sec("tensor")
                def _(te, tau=tau, o=o):
                    if o == 0:
                        wait(te, f"dtbc_{tau}")
                        if tau > 0:
                            wait(te, f"exp_{tau - 1}_7_3")
                    if o >= 2:
                        wait(te, f"sp_{tau}_{o - 2}")
                    pst = ps["dl0"] if o % 2 == 0 else ps["dl1"]
                    te.matmul(pst[:], sb["wdt_sb"][:, o * 128:(o + 1) * 128],
                              sb["dt_sb"][:], start=True, stop=True).then_inc(sems["sP"], 1)
                    bump("sP", 1)
                    mark(f"wdt_{tau}_{o}", "sP")
                    if o == NC_DH - 1:
                        mark(f"wdtmm_{tau}", "sP")

                @sec("scalar")
                def _(sc, tau=tau, o=o):
                    wait(sc, f"wdt_{tau}_{o}")
                    pst = ps["dl0"] if o % 2 == 0 else ps["dl1"]
                    sc.activation(out=csl(sb["xsq"], o % 2), in_=pst[:], func=AF.Exp,
                                  bias=vcol("bdt", o))
                    sc.drain()
                    sc.activation(out=csl(sb["delta_t"], o), in_=csl(sb["xsq"], o % 2),
                                  func=AF.Ln, bias=1.0).then_inc(sems["sA"], 1)
                    bump("sA", 1)
                    mark(f"sp_{tau}_{o}", "sA")
                    if o == NC_DH - 1:
                        mark(f"delta_{tau}", "sA")

            @sec("vector")
            def _(v, tau=tau):
                wait(v, f"delta_{tau}")
                if tau > 0:
                    wait(v, f"bw_{tau - 1}_7_3")
                v.drain()
                inc(v.tensor_tensor(out=sb["du_t"][:], in0=sb["delta_t"][:],
                                    in1=sb["scrA"][:], op=ALU.mult), "sV")
                mark(f"du_{tau}", "sV")

            for h in range(4):
                @sec("sync")
                def _(sy, tau=tau, h=h):
                    wait(sy, f"dtbc_{tau}")
                    bcr = sb["bcrep0"] if h % 2 == 0 else sb["bcrep1"]
                    qn = "dBC" if h % 2 == 0 else "dBC1"
                    if tau * 4 + h >= 2:
                        pt, ph = (tau, h - 2) if h >= 2 else (tau - 1, h + 2)
                        wait(sy, f"bw_{pt}_7_{ph}")
                        wait(sy, f"tree_{pt}_7_{ph}")
                    dma(sy, bcr[0:1, 0:4 * TT].rearrange("o (n t) -> o n t", n=4),
                        sb["bc_sb"][4 * h:4 * h + 4, :], q=qn)
                    dma(sy, bcr[0:1, 4 * TT:8 * TT].rearrange("o (n t) -> o n t", n=4),
                        sb["bc_sb"][16 + 4 * h:20 + 4 * h, :], q=qn)
                    nh = 1
                    step = 0
                    while nh < 128:
                        mark(f"bcq_{tau}_{h}_{step}", qn)
                        wait(sy, f"bcq_{tau}_{h}_{step}")
                        dma(sy, bcr[nh:2 * nh, :], bcr[0:nh, :], q=qn)
                        nh *= 2
                        step += 1
                    mark(f"bcfl_{tau}_{h}", qn)

                for k in range(NC_DH):
                    @sec("scalar")
                    def _(sc, tau=tau, h=h, k=k):
                        wait(sc, f"delta_{tau}")
                        p2t, p2h, p2k = punit(tau, h, k, 2)
                        if p2t >= 0:
                            wait(sc, f"scans_{p2t}_{p2k}_{p2h}")
                        jb = 4 * (k % 2)
                        for j in range(4):
                            n = 4 * h + j
                            li = sc.activation(
                                out=csl(sb["scrB"], jb + j), in_=csl(sb["delta_t"], k),
                                func=AF.Exp,
                                scale=sb["aneg"][:, k * NST + n:k * NST + n + 1])
                        inc(li, "sA")
                        mark(f"exp_{tau}_{k}_{h}", "sA")
                        p1t, p1h, p1k = punit(tau, h, k, 1)
                        if p1t >= 0:
                            wait(sc, f"scans_{p1t}_{p1k}_{p1h}")
                            Hp = sb["H_t"] if p1k % 2 == 0 else sb["H1"]
                            inc(sc.copy(
                                sb["lastst"][:, p1k * NST + 4 * p1h:
                                             p1k * NST + 4 * p1h + 4],
                                AP(Hp[:].tensor, TT - 1, [[4 * TT, 128], [TT, 4]])),
                                "sA")

                    if h == 1 and tau + 1 < NT:
                        emit_winevac(tau + 1, k)

                    @sec("gpsimd")
                    def _(g, tau=tau, h=h, k=k):
                        wait(g, f"du_{tau}")
                        wait(g, f"bcfl_{tau}_{h}")
                        p2t, p2h, p2k = punit(tau, h, k, 2)
                        if p2t == tau:
                            wait(g, f"scans_{p2t}_{p2k}_{p2h}")
                        elif (h, k) == (0, 1):
                            wait(g, f"scans_{tau}_0_0")
                        rep = AP(sb["du_t"][:].tensor, k * TT,
                                 [[NC_DH * TT, 128], [0, 4], [1, TT]])
                        bcr = sb["bcrep0"] if h % 2 == 0 else sb["bcrep1"]
                        scrCp = sb["scrC"] if k % 2 == 0 else sb["scrC1"]
                        g.tensor_tensor(out=scrCp[:].rearrange(
                                            "p (n t) -> p n t", n=4),
                                        in0=rep,
                                        in1=bcr[:, 0:4 * TT].rearrange(
                                            "p (n t) -> p n t", n=4),
                                        op=ALU.mult).then_inc(sems["sG"], 1)
                        bump("sG", 1)
                        mark(f"bw_{tau}_{k}_{h}", "sG")

                    @sec("vector")
                    def _(v, tau=tau, h=h, k=k):
                        wait(v, f"exp_{tau}_{k}_{h}")
                        wait(v, f"bw_{tau}_{k}_{h}")
                        jb = 4 * (k % 2)
                        Hp = sb["H_t"] if k % 2 == 0 else sb["H1"]
                        scrCp = sb["scrC"] if k % 2 == 0 else sb["scrC1"]
                        for j in range(4):
                            n = 4 * h + j
                            li = v.tensor_tensor_scan(
                                out=csl(Hp, j), data0=csl(sb["scrB"], jb + j),
                                data1=csl(scrCp, j),
                                initial=sb["lastst"][:, k * NST + n:k * NST + n + 1],
                                op0=ALU.mult, op1=ALU.add)
                        inc(li, "sV")
                        mark(f"scans_{tau}_{k}_{h}", "sV")
                        hctree(v, tau, h, k)

            @sec("vector")
            def _(v, tau=tau):
                v.drain()
                for k in range(NC_DH):
                    v.scalar_tensor_tensor(out=csl(sb["y_t"], k), in0=csl(sb["scrA"], k),
                                           scalar=vcol("dpar", k), in1=csl(sb["y_t"], k),
                                           op0=ALU.mult, op1=ALU.add)
                v.drain()
                for k in range(NC_DH):
                    li = v.tensor_tensor(out=csl(sb["scrA"], k), in0=csl(sb["y_t"], k),
                                         in1=csl(sb["sz_t"], k), op=ALU.mult)
                inc(li, "sV")
                mark(f"g_{tau}", "sV")

            @sec("sync")
            def _(sy, tau=tau):
                wait(sy, f"g_{tau}")
                dma(sy, g_buf[tau:tau + 1, :].rearrange("o (p c t) -> o p c t",
                                                        p=128, c=NC_DH),
                    sb["scrA"][:].rearrange("p (c t) -> p c t", c=NC_DH), q="dOut")
                mark(f"gsp_{tau}", "dOut")

            if DBG and tau == 0:
                @sec("sync")
                def _(sy, tau=tau):
                    wait(sy, f"g_{tau}")
                    dma(sy, dbg, sb["delta_t"][:])
                    mark("dbgw", "dS")


        for tau in range(NT):
            stage_front(tau)
            if tau >= 1:
                stage_scanblk(tau - 1)
            stage_win(tau)
            stage_tail(tau)
        stage_scanblk(NT - 1)
        actx.close()
        alloc("wout_sb", [128, NC_DH * DM])
        alloc("g_ld", [128, NC_DH * TT])
        alloc("pout_t", [128, NC_DM * TT])
        alloc("pout_u", [128, NC_DM * TT])
        alloc("qi_t", [128, NC_DM * TT], I32)
        alloc("qf_t", [128, NC_DM * TT])
        alloc("qb_t", [128, NC_DM * TT], BF16)
        alloc("wf8", [128, NC_DM * DM], I8)
        alloc("wfb", [128, NC_DM * DM], BF16)
        alloc("c2scr", [128, NC_DM * TT])
        alloc("ps_ld0", [128, NC_DM * TT])
        alloc("ps_ld1", [128, NC_DM * TT])
        alloc("r0_ld0", [128, NC_DM * TT])
        alloc("r0_ld1", [128, NC_DM * TT])

        mark_led("barrierB")

        def wait_barrier(eng):
            wait_led(eng, "barrierB")

        @sec("sync")
        def _(sy):
            wait_barrier(sy)
            dma(sy, sb["wout_sb"][:].rearrange("p (c o) -> p c o", c=NC_DH),
                wout_t[:, :].rearrange("(c p) o -> p c o", p=128), q="dW")
            dma(sy, sb["wf8"][:].rearrange("p (c o) -> p c o", c=NC_DM),
                wffn_t[:, :].rearrange("(c p) o -> p c o", p=128), q="dW")
            mark("bweights", "dW")

        @sec("vector")
        def _(v):
            wait_barrier(v)
            wait(v, "bweights")
            inc(v.tensor_copy(sb["wfb"][:], sb["wf8"][:]), "sV")
            mark("wfb_done", "sV")

        @sec("tensor")
        def _(te):
            wait_barrier(te)
            te.sem_inc(sems["sP"], 1)
            bump("sP", 1)
            mark("te_bar", "sP")

        @sec("scalar")
        def _(sc):
            wait_barrier(sc)
            sc.sem_inc(sems["sA"], 1)
            bump("sA", 1)
            mark("sc_bar", "sA")

        @sec("gpsimd")
        def _(g):
            wait_barrier(g)
            g.sem_inc(sems["sG"], 1)
            bump("sG", 1)
            mark("g_bar", "sG")

        def stage_B(tau):
            @sec("sync")
            def _(sy, tau=tau):
                if tau == 0:
                    wait(sy, f"gsp_{NT - 1}")
                else:
                    wait(sy, f"poutmm_{tau - 1}")
                dma(sy, sb["g_ld"][:].rearrange("p (c t) -> p c t", c=NC_DH),
                    g_buf[tau:tau + 1, :].rearrange("o (p c t) -> o p c t",
                                                    p=128, c=NC_DH), q="dX")
                mark(f"gld_{tau}", "dX")

            for o in range(NC_DM):
                @sec("tensor")
                def _(te, tau=tau, o=o):
                    if o == 0:
                        wait(te, f"gld_{tau}")
                        wait(te, "bweights")
                    if o >= 2 or tau > 0:
                        oo, tt2 = o - 2, tau
                        if oo < 0:
                            oo += NC_DM
                            tt2 -= 1
                        wait(te, f"poutev_{tt2}_{oo}")
                    pst = ps["xz0"] if o % 2 == 0 else ps["xz1"]
                    for k in range(NC_DH):
                        li = te.matmul(pst[:], sb["wout_sb"][:, k * DM + o * 128:
                                                             k * DM + o * 128 + 128],
                                       csl(sb["g_ld"], k), start=(k == 0),
                                       stop=(k == NC_DH - 1))
                    inc(li, "sP")
                    mark(f"poutm_{tau}_{o}", "sP")
                    if o == NC_DM - 1:
                        mark(f"poutmm_{tau}", "sP")

                @sec("vector")
                def _(v, tau=tau, o=o):
                    wait(v, f"poutm_{tau}_{o}")
                    if tau > 1 and o == 0:
                        wait(v, f"post_{tau - 2}")
                    pst = ps["xz0"] if o % 2 == 0 else ps["xz1"]
                    po = "pout_t" if tau % 2 == 0 else "pout_u"
                    inc(v.tensor_copy(csl(sb[po], o), pst[:]), "sV")
                    mark(f"poutev_{tau}_{o}", "sV")
                    if o == NC_DM - 1:
                        mark(f"poutevall_{tau}", "sV")

            @sec("gpsimd")
            def _(g, tau=tau):
                wait(g, f"poutevall_{tau}")
                po = "pout_t" if tau % 2 == 0 else "pout_u"
                dma(g, aro_i[tau:tau + 1, :].rearrange("o (p c t) -> o p c t",
                                                       p=128, c=NC_DM),
                    sb[po][:].rearrange("p (c t) -> p c t", c=NC_DM), q="dG")
                mark(f"post_{tau}", "dG")
                wait(g, f"post_{tau}")
                g.collective_compute(
                    "AllReduce", ALU.add,
                    replica_groups=[[0, 1], [2, 3], [4, 5], [6, 7]],
                    ins=[aro_i[tau:tau + 1, :].opt()],
                    outs=[aro_o[tau:tau + 1, :].opt()],
                ).then_inc(sems["sC"], 1)
                bump("sC", 1)
                mark(f"cco_{tau}", "sC")


        def stage_C1(tau):
            pl = "ps_ld0" if tau % 2 == 0 else "ps_ld1"
            rl = "r0_ld0" if tau % 2 == 0 else "r0_ld1"

            @sec("sync")
            def _(sy, tau=tau, pl=pl):
                wait(sy, f"cco_{tau}")
                if tau > 1:
                    wait(sy, f"x2c_{tau - 2}")
                    wait(sy, f"mx_{tau - 2}")
                    wait(sy, f"stats3_{tau - 2}")
                    wait(sy, f"s3sq_{tau - 2}_{NC_DM - 1}")
                    wait(sy, f"x2sp_{tau - 2}")
                qn = "dMsc" if tau % 2 == 0 else "dM1"
                dma(sy, sb[pl][:].rearrange("p (c t) -> p c t", c=NC_DM),
                    aro_o[tau:tau + 1, :].rearrange("o (p c t) -> o p c t",
                                                    p=128, c=NC_DM), q=qn)
                mark(f"psld_{tau}", qn)

            @sec("sync")
            def _(sy, tau=tau, rl=rl):
                if tau > 1:
                    wait(sy, f"mx_{tau - 2}")
                qn = "dR" if tau % 2 == 0 else "dM0"
                dma(sy, sb[rl][:].rearrange("p (c t) -> p c t", c=NC_DM),
                    r0_buf[tau:tau + 1, :].rearrange("o (p c t) -> o p c t",
                                                     p=128, c=NC_DM), q=qn)
                mark(f"r0ld_{tau}", qn)

            @sec("vector")
            def _(v, tau=tau, pl=pl, rl=rl):
                wait(v, f"psld_{tau}")
                wait(v, f"r0ld_{tau}")
                v.drain()
                for c in range(NC_DM):
                    li = v.scalar_tensor_tensor(out=csl(sb[pl], c),
                                                in0=csl(sb[pl], c),
                                                scalar=vcol("gap", c), in1=csl(sb[rl], c),
                                                op0=ALU.mult, op1=ALU.add)
                inc(li, "sV")
                mark(f"x2c_{tau}", "sV")

            @sec("sync")
            def _(sy, tau=tau, pl=pl):
                wait(sy, f"x2c_{tau}")
                qn = "dOut" if tau % 2 == 0 else "dS"
                dma(sy, x2_buf[tau:tau + 1, :].rearrange("o (p c t) -> o p c t",
                                                         p=128, c=NC_DM),
                    sb[pl][:].rearrange("p (c t) -> p c t", c=NC_DM), q=qn)
                mark(f"x2sp_{tau}", qn)

            def _scf3(sc, tau=tau):
                if tau == 0:
                    wait_barrier(sc)
                wait(sc, f"x2c_{tau}")

            def _tef3(te, tau=tau):
                wait(te, f"x2c_{tau}")
                if tau > 0:
                    wait(te, f"p3_{tau - 1}")

            stats_round("s3", tau, pl, sc_first=_scf3, te_first=_tef3)
            alias(f"stats3_{tau}", f"stats_s3_{tau}")

            @sec("vector")
            def _(v, tau=tau):
                wait(v, f"stats3_{tau}")
                if tau > 0:
                    wait(v, f"stsp_{tau - 1}")
                stats_post(v, tau, "ln3")

            @sec("scalar")
            def _(sc, tau=tau):
                wait(sc, f"vpre_ln3_{tau}")
                sc.activation(out=sb["stt2"][:], in_=sb["stt1"][:], func=AF.Ln)
                sc.drain()
                sc.activation(out=sb["r_a"][:], in_=sb["stt2"][:], func=AF.Exp,
                              scale=-0.5).then_inc(sems["sA"], 1)
                bump("sA", 1)
                mark(f"rstd_ln3_{tau}", "sA")

            @sec("vector")
            def _(v, tau=tau):
                wait(v, f"rstd_ln3_{tau}")
                newton(v, tau, "ln3")
            alias(f"p3_{tau}", f"post_ln3_{tau}")

            @sec("sync")
            def _(sy, tau=tau):
                wait(sy, f"p3_{tau}")
                dma(sy, st_buf[tau:tau + 1, 0:TT], sb["mu_a"][:], q="dSt")
                dma(sy, st_buf[tau:tau + 1, TT:2 * TT], sb["r_a"][:], q="dSt")
                mark(f"stsp_{tau}", "dSt")

            @sec("tensor")
            def _(te, tau=tau):
                wait(te, f"p3_{tau}")
                if tau > 0:
                    wait(te, f"mx_{tau - 1}")
                bcast(te, tau, "ln3")

            @sec("vector")
            def _(v, tau=tau, pl=pl, rl=rl):
                wait(v, f"bc_ln3_{tau}")
                rep_evac(v, tau, "ln3")
                wait(v, f"x2sp_{tau}")
                v.drain()
                for c in range(NC_DM):
                    v.tensor_tensor(out=csl(sb[rl], c), in0=csl(sb[pl], c),
                                    in1=sb["murep"][:], op=ALU.subtract)
                v.drain()
                for c in range(NC_DM):
                    v.tensor_tensor(out=csl(sb[rl], c), in0=csl(sb[rl], c),
                                    in1=sb["rrep"][:], op=ALU.mult)
                v.drain()
                for c in range(NC_DM):
                    v.tensor_scalar(out=csl(sb[rl], c), in0=csl(sb[rl], c),
                                    scalar1=vcol("cb1", c), scalar2=vcol("sbv", c),
                                    op0=ALU.mult, op1=ALU.add)
                v.drain()
                v.reduce_max(out=sb["rrep"][:, 0:1], in_=sb[rl][:], axis=AXX,
                             apply_absolute_value=True)
                v.drain()
                inc(v.tensor_tensor(out=sb["mxacc"][:], in0=sb["mxacc"][:],
                                    in1=sb["rrep"][:, 0:1], op=ALU.max), "sV")
                mark(f"mx_{tau}", "sV")


        for tau in range(NT + 1):
            if tau < NT:
                stage_B(tau)
            if tau >= 1:
                stage_C1(tau - 1)

        @sec("sync")
        def _(sy):
            wait(sy, f"mx_{NT - 1}")
            dma(sy, sb["mxrow"][0:1, 0:128].rearrange("o (a b) -> o a b", a=128),
                sb["mxacc"][:], q="dSt")
            mark("mxfl", "dSt")

        @sec("vector")
        def _(v):
            wait(v, "mxfl")
            v.drain()
            inc(v.reduce_max(out=sb["mxrow"][0:1, 128:129], in_=sb["mxrow"][0:1, 0:128],
                             axis=AXX, apply_absolute_value=False), "sV")
            mark("mxloc", "sV")

        @sec("gpsimd")
        def _(g):
            wait(g, "mxloc")
            dma(g, mx_i, sb["mxrow"][0:1, 128:129], q="dG")
            mark("mxst", "dG")
            wait(g, "mxst")
            g.collective_compute(
                "AllReduce", ALU.max,
                replica_groups=[list(range(8))],
                ins=[mx_i.opt()], outs=[mx_o.opt()],
            ).then_inc(sems["sC"], 1)
            bump("sC", 1)
            mark("ccmx", "sC")
            wait(g, "ccmx")
            dma(g, sb["mxrow"][0:1, 129:130], mx_o, q="dG")
            mark("mxg", "dG")

        @sec("vector")
        def _(v):
            wait(v, "mxg")
            v.drain()
            v.tensor_scalar(out=sb["mxrow"][0:1, 129:130], in0=sb["mxrow"][0:1, 129:130],
                            scalar1=1.0 / 127.0, scalar2=EPS, op0=ALU.mult, op1=ALU.max)
            v.drain()
            inc(v.reciprocal(sb["mxrow"][0:1, 128:129], sb["mxrow"][0:1, 129:130]), "sV")
            mark("xscv", "sV")

        @sec("tensor")
        def _(te):
            wait(te, "xscv")
            te.matmul(ps["bc"][:, 0:1], sb["onesr"][:], sb["mxrow"][0:1, 128:129],
                      start=True, stop=True)
            te.matmul(ps["bc"][:, 1:2], sb["onesr"][:], sb["mxrow"][0:1, 129:130],
                      start=True, stop=True).then_inc(sems["sP"], 1)
            bump("sP", 1)
            mark("xscb", "sP")

        @sec("vector")
        def _(v):
            wait(v, "xscb")
            V = sb["vecs"]
            sl = lambda nm: V[:, VOFF[nm]:VOFF[nm] + NC_DM]
            v.drain()
            v.tensor_copy(sb["mxacc"][:], ps["bc"][:, 0:1])
            v.tensor_copy(sb["ones1"][:], ps["bc"][:, 1:2])
            v.drain()
            v.tensor_scalar(out=sl("cb1"), in0=sl("cb1"), scalar1=sb["mxacc"][:],
                            scalar2=None, op0=ALU.mult)
            v.tensor_scalar(out=sl("sbv"), in0=sl("sbv"), scalar1=sb["mxacc"][:],
                            scalar2=None, op0=ALU.mult)
            inc(v.tensor_scalar(out=sl("gbw"), in0=sl("gbw"), scalar1=sb["ones1"][:],
                                scalar2=None, op0=ALU.mult), "sV")
            mark("c2const", "sV")

        for tau in range(NT):
            pp = "pout_t" if tau % 2 == 0 else "pout_u"

            @sec("sync")
            def _(sy, tau=tau, pp=pp):
                if tau == 0:
                    wait(sy, "c2const")
                    wait(sy, "mxfl")
                else:
                    wait(sy, f"bc_c2_{tau - 1}_{tau - 1}")
                if tau > 1:
                    wait(sy, f"x3_{tau - 2}")
                dma(sy, sb[pp][:].rearrange("p (c t) -> p c t", c=NC_DM),
                    x2_buf[tau:tau + 1, :].rearrange("o (p c t) -> o p c t",
                                                     p=128, c=NC_DM), q="dX")
                dma(sy, sb["mu_a"][:], st_buf[tau:tau + 1, 0:TT], q="dX")
                dma(sy, sb["r_a"][:], st_buf[tau:tau + 1, TT:2 * TT], q="dX")
                mark(f"x2ld_{tau}", "dX")

            @sec("tensor")
            def _(te, tau=tau):
                wait(te, f"x2ld_{tau}")
                if tau == 0:
                    wait(te, "c2const")
                else:
                    wait(te, f"rep_c2r_{tau - 1}_{tau - 1}")
                bcast(te, tau, f"c2_{tau}")

            @sec("vector")
            def _(v, tau=tau, pp=pp):
                wait(v, f"bc_c2_{tau}_{tau}")
                rep_evac(v, tau, f"c2r_{tau}")
                wait(v, "c2const")
                if tau > 0:
                    wait(v, f"fmm_{tau - 1}_{NC_DM - 1}")
                    wait(v, f"x3sp_{tau - 1}")
                v.drain()
                for c in range(NC_DM):
                    v.tensor_tensor(out=csl(sb["g_ld"], c), in0=csl(sb[pp], c),
                                    in1=sb["murep"][:], op=ALU.subtract)
                v.drain()
                for c in range(NC_DM):
                    v.tensor_tensor(out=csl(sb["g_ld"], c), in0=csl(sb["g_ld"], c),
                                    in1=sb["rrep"][:], op=ALU.mult)
                v.drain()
                for c in range(NC_DM):
                    v.tensor_scalar(out=csl(sb["g_ld"], c), in0=csl(sb["g_ld"], c),
                                    scalar1=vcol("cb1", c), scalar2=vcol("sbv", c),
                                    op0=ALU.mult, op1=ALU.add)
                v.drain()
                v.tensor_copy(sb["qi_t"][:], sb["g_ld"][:])
                v.drain()
                v.tensor_copy(sb["qf_t"][:], sb["qi_t"][:])
                v.drain()
                v.tensor_tensor(out=sb["c2scr"][:], in0=sb["g_ld"][:], in1=sb["qf_t"][:],
                                op=ALU.subtract)
                v.drain()
                v.scalar_tensor_tensor(out=sb["qf_t"][:], in0=sb["c2scr"][:], scalar=0.5,
                                       in1=sb["qf_t"][:], op0=ALU.is_ge, op1=ALU.add)
                v.drain()
                v.scalar_tensor_tensor(out=sb["qf_t"][:], in0=sb["c2scr"][:], scalar=-0.5,
                                       in1=sb["qf_t"][:], op0=ALU.is_ge, op1=ALU.add)
                v.drain()
                inc(v.tensor_scalar(out=sb["qb_t"][:], in0=sb["qf_t"][:],
                                    scalar1=-1.0, scalar2=127.0, op0=ALU.add,
                                    op1=ALU.min), "sV")
                mark(f"vq_{tau}", "sV")

            for o in range(NC_DM):
                @sec("tensor")
                def _(te, tau=tau, o=o):
                    if o == 0:
                        wait(te, f"vq_{tau}")
                        if tau == 0:
                            wait(te, "wfb_done")
                    if o >= 2 or tau > 0:
                        oo, tt2 = o - 2, tau
                        if oo < 0:
                            oo += NC_DM
                            tt2 -= 1
                        wait(te, f"fev_{tt2}_{oo}")
                    pst = ps["xz0"] if o % 2 == 0 else ps["xz1"]
                    for k in range(NC_DM):
                        li = te.matmul(pst[:], sb["wfb"][:, k * DM + o * 128:
                                                         k * DM + o * 128 + 128],
                                       csl(sb["qb_t"], k), start=(k == 0),
                                       stop=(k == NC_DM - 1))
                    inc(li, "sP")
                    mark(f"fmm_{tau}_{o}", "sP")

                @sec("vector")
                def _(v, tau=tau, o=o, pp=pp):
                    wait(v, f"fmm_{tau}_{o}")
                    if tau > 0 and o == 0:
                        wait(v, f"x3sp_{tau - 1}")
                    pst = ps["xz0"] if o % 2 == 0 else ps["xz1"]
                    v.drain()
                    v.tensor_scalar(out=csl(sb["qf_t"], o), in0=pst[:],
                                    scalar1=vcol("gbw", o), scalar2=vcol("gbb", o),
                                    op0=ALU.mult, op1=ALU.add)
                    v.drain()
                    inc(v.tensor_tensor(out=csl(sb["qf_t"], o), in0=csl(sb["qf_t"], o),
                                        in1=csl(sb[pp], o), op=ALU.add), "sV")
                    mark(f"fev_{tau}_{o}", "sV")
                    if o == NC_DM - 1:
                        mark(f"x3_{tau}", "sV")

            @sec("sync")
            def _(sy, tau=tau):
                wait(sy, f"x3_{tau}")
                dma(sy, xout[:, tsl(tau)].rearrange("(c p) t -> p c t", p=128),
                    sb["qf_t"][:].rearrange("p (c t) -> p c t", c=NC_DM), q="dOut")
                mark(f"x3sp_{tau}", "dOut")

        mark_led("final")

        @sec("sync")
        def _(sy):
            wait_led(sy, "final", only={"dW", "dX", "dR", "dBC", "dBC1", "dOut",
                                        "dMsc", "dSt", "dM0", "dM1", "dS"})

        @sec("gpsimd")
        def _(g):
            wait_led(g, "final", only={"dG"})

        if pass2:
            for engname in ["sync", "tensor", "vector", "scalar", "gpsimd"]:
                fns = thunks[engname]

                @getattr(block, engname)
                def _(eng, fns=fns):
                    for fn in fns:
                        fn(eng)

    return nc, ev


def kernel(L0_attn_x, L0_attn_ln_gamma, L0_attn_ln_beta, L0_attn_W_in,
           L0_attn_conv_w, L0_attn_conv_b, L0_attn_W_xproj, L0_attn_W_dt,
           L0_attn_b_dt, L0_attn_A_log, L0_attn_D_param, L0_attn_W_out,
           L0_ffn_weight_int8, L0_ffn_weight_scale, L0_ffn_bias, L0_c, L0_wmod):
    _, ev1 = build_program()
    nc, _ = build_program(ev1)

    def vec_pack(v):
        return np.ascontiguousarray(np.asarray(v, np.float32).reshape(NC_DM, 128).T)

    x = np.asarray(L0_attn_x, np.float32)
    W_in = np.asarray(L0_attn_W_in, np.float32)
    conv_w = np.asarray(L0_attn_conv_w, np.float32)
    conv_b = np.asarray(L0_attn_conv_b, np.float32)
    W_xp = np.asarray(L0_attn_W_xproj, np.float32)
    W_dt = np.asarray(L0_attn_W_dt, np.float32)
    b_dt = np.asarray(L0_attn_b_dt, np.float32)
    A_log = np.asarray(L0_attn_A_log, np.float32)
    D_par = np.asarray(L0_attn_D_param, np.float32)
    W_out = np.asarray(L0_attn_W_out, np.float32)
    w8 = np.asarray(L0_ffn_weight_int8, np.int8)
    wscale = np.asarray(L0_ffn_weight_scale, np.float32)
    fb = np.asarray(L0_ffn_bias, np.float32)
    cc = np.asarray(L0_c, np.float32)
    wmod = np.ascontiguousarray(np.asarray(L0_wmod, np.float32))

    in_maps = []
    for core in range(8):
        b = core // 2
        hh = core % 2
        dsl = slice(hh * DH, (hh + 1) * DH)
        w_in_sl = np.concatenate([W_in[hh * DH:(hh + 1) * DH],
                                  W_in[2048 + hh * DH:2048 + (hh + 1) * DH]], 0)
        cw = conv_w[dsl, 0, :]
        m = dict(
            xin=np.ascontiguousarray(x[b].T),
            w_in_t=np.ascontiguousarray(w_in_sl.T),
            wxp_t=np.ascontiguousarray(W_xp[:, dsl].T),
            wdt_t=np.ascontiguousarray(W_dt[dsl].T),
            wout_t=np.ascontiguousarray(W_out[:, dsl].T),
            wmod=wmod,
            wffn_t=np.ascontiguousarray(w8.T),
            convw=np.ascontiguousarray(
                cw.reshape(NC_DH, 128, 4).transpose(1, 0, 2).reshape(128, NC_DH * 4)),
            convb=vec_pack(conv_b[dsl]),
            bdt=vec_pack(b_dt[dsl]),
            alog=np.ascontiguousarray(
                A_log[dsl].reshape(NC_DH, 128, NST).transpose(1, 0, 2)
                .reshape(128, NC_DH * NST)),
            dpar=vec_pack(D_par[dsl]),
            lngam=vec_pack(np.asarray(L0_attn_ln_gamma, np.float32)),
            lnbet=vec_pack(np.asarray(L0_attn_ln_beta, np.float32)),
            wsc=vec_pack(wscale[:, 0]),
            fbias=vec_pack(fb),
            cvec=vec_pack(cc[b]),
        )
        in_maps.append(m)

    res = run_bass_kernel_spmd(nc, in_maps, core_ids=list(range(8)))
    out = np.empty((4, S, DM), np.float32)
    for b in range(4):
        out[b] = res.results[2 * b]["xout"].T
    return out
